# version 12
# baseline (speedup 1.0000x reference)
import contextlib
import numpy as np
import concourse.bass as bass
import concourse.mybir as mybir
from concourse.bass_utils import run_bass_kernel_spmd

F32 = mybir.dt.float32
BF16 = mybir.dt.bfloat16
AF = mybir.ActivationFunctionType
ALU = mybir.AluOpType
AX = mybir.AxisListType

D = 1024
NCORES = 8
C = 8
P = 128
FF = 4096
NH = 16
DH = 64
LN_EPS = 1e-5
RMS_EPS = 1.1920929e-07
SC_A = 0.125
SC_B = 0.03125

ENGS = ("sync", "scalar", "vector", "tensor", "gpsimd")



class Tok:
    __slots__ = ("writer", "readers", "name")

    def __init__(self, name=""):
        self.writer = None
        self.readers = []
        self.name = name


def toks(n, name=""):
    return [Tok(f"{name}[{i}]") for i in range(n)]


class Builder:
    def __init__(self):
        self.ops = []

    def emit(self, eng, fn, r=(), w=(), dma=False, tag=""):
        opid = len(self.ops)
        deps = set()
        for t in r:
            if t.writer is not None:
                deps.add(t.writer)
        for t in w:
            if t.writer is not None:
                deps.add(t.writer)
            deps.update(t.readers)
        for t in r:
            t.readers.append(opid)
        for t in w:
            t.writer = opid
            t.readers = []
        self.ops.append(dict(eng=eng, fn=fn, deps=deps, dma=dma, tag=tag))
        return opid

    def alias(self, name, *olds):
        t = Tok(name)
        rd = []
        for o in olds:
            rd.extend(o.readers)
            if o.writer is not None:
                rd.append(o.writer)
        t.readers = rd
        return t

    def finalize(self, nc):
        ops = self.ops
        n = len(ops)
        if getattr(Builder, "SERIAL", False):
            for i in range(1, n):
                ops[i]["deps"].add(i - 1)
        need_inc = [False] * n
        for op in ops:
            for d in op["deps"]:
                if ops[d]["eng"] != op["eng"] and not ops[d]["dma"]:
                    need_inc[d] = True
        reach = [0] * n
        for i, op in enumerate(ops):
            m = 0
            for d in op["deps"]:
                m |= reach[d] | (1 << d)
            reach[i] = m
        chains = {}
        cnt = {}
        val = [None] * n
        key = [None] * n
        for i, op in enumerate(ops):
            if op["dma"]:
                e = op["eng"]
                cl = chains.setdefault(e, [])
                for c in cl:
                    if (reach[i] >> c["last"]) & 1:
                        c["last"] = i
                        c["count"] += 1
                        val[i], key[i] = 16 * c["count"], c["key"]
                        cnt[c["key"]] = val[i]
                        break
                else:
                    k = ("d", e, len(cl))
                    cl.append(dict(last=i, count=1, key=k))
                    val[i], key[i] = 16, k
                    cnt[k] = 16
            elif need_inc[i]:
                k = ("c", op["eng"])
                cnt[k] = cnt.get(k, 0) + 1
                val[i], key[i] = cnt[k], k
        snap = [None] * n
        vc = {e: {} for e in ENGS}
        waits = [None] * n
        elide = getattr(Builder, "ELIDE", True)
        for i, op in enumerate(ops):
            e = op["eng"]
            myvc = vc[e]
            wl = {}
            for d in op["deps"]:
                dop = ops[d]
                if dop["eng"] == e and not dop["dma"]:
                    continue
                k = key[d]
                if val[d] is None:
                    raise RuntimeError(f"dep op {d} has no sem")
                if not elide or myvc.get(k, 0) < val[d]:
                    wl[k] = max(wl.get(k, 0), val[d])
            for k, v in wl.items():
                myvc[k] = max(myvc.get(k, 0), v)
            for d in op["deps"]:
                if val[d] is not None and snap[d] is not None \
                        and myvc.get(key[d], 0) >= val[d]:
                    for kk, vv in snap[d].items():
                        if myvc.get(kk, 0) < vv:
                            myvc[kk] = vv
            waits[i] = sorted(wl.items())
            if val[i] is not None:
                s = dict(myvc)
                s[key[i]] = val[i]
                snap[i] = s
                myvc[key[i]] = val[i]
        final_waits = [(k, v) for k, v in sorted(cnt.items()) if k[0] == "d"]
        nsems = len(cnt)
        per_eng = {e: [] for e in ENGS}
        for i, op in enumerate(ops):
            per_eng[op["eng"]].append(i)
        with contextlib.ExitStack() as stack:
            sems = {}
            for k in sorted(cnt):
                sems[k] = stack.enter_context(
                    nc.semaphore("_".join(str(p) for p in k)))
            with nc.Block() as block:
                def mk(se):
                    def section(engine):
                        for i in per_eng[se]:
                            op = ops[i]
                            for k, v in waits[i]:
                                engine.wait_ge(sems[k], v)
                            inst = op["fn"](engine)
                            if val[i] is not None:
                                if inst is None:
                                    raise RuntimeError(
                                        f"op {i} ({op['tag']}) needs inc, no inst")
                                if op["dma"] or se == "tensor" \
                                        or getattr(Builder, "DIRECT_INC",
                                                   False):
                                    inst.then_inc(sems[key[i]],
                                                  16 if op["dma"] else 1)
                                else:
                                    engine.maybe_drain_then_inc(
                                        (sems[key[i]], 1), fusable=False)
                        if se == "sync":
                            for k, v in final_waits:
                                engine.wait_ge(sems[k], v)
                    return section
                block.sync(mk("sync"))
                block.scalar(mk("scalar"))
                block.vector(mk("vector"))
                block.tensor(mk("tensor"))
                block.gpsimd(mk("gpsimd"))
        return dict(n_ops=n,
                    n_waits=sum(len(w) for w in waits if w),
                    n_incs=sum(1 for v in val if v is not None),
                    n_sems=nsems,
                    dma_chains={e: len(cl) for e, cl in chains.items()})


Builder.ELIDE = False


class Ctx:
    pass



def build_program(T=4096):
    NT = T // P
    NG = T // 512
    nc = bass.Bass("TRN2", target_bir_lowering=False,
                   detect_race_conditions=False)
    B = Builder()
    cx = Ctx()
    cx.nc, cx.B, cx.T, cx.NT, cx.NG = nc, B, T, NT, NG
    em = B.emit

    def dram(name, shape, dtype=F32, kind="Internal"):
        return nc.dram_tensor(name, list(shape), dtype, kind=kind).ap()

    I = {}
    for name, shape in [
            ("x", (T, D)),
            ("aWq", (D, D)), ("aWkv", (D, 2 * D)), ("aWout", (D, D)),
            ("aWg", (D, NH)), ("a_normw", (D,)),
            ("fWr", (D, D)), ("fWk", (D, D)), ("fWv", (D, D)),
            ("fmur", (D,)), ("fmuk", (D,)),
            ("bWq", (D, D)), ("bWk", (D, D)), ("bWv", (D, D)),
            ("bWg", (D, D)), ("bWp", (D, D)), ("bWc", (D, 1)), ("bWo", (D, D)),
            ("b_gb", (D,)), ("b_pb", (D,)), ("b_cb", (1,)), ("b_ob", (D,)),
            ("b_ln1g", (D,)), ("b_ln1b", (D,)),
            ("b_ln2g", (D,)), ("b_ln2b", (D,)),
            ("bW1", (D, FF)), ("bB1", (FF,)), ("bW2", (FF, D)), ("bB2", (D,)),
            ("rc", (T, 512)), ("rs", (T, 512))]:
        I[name] = dram(name, shape, F32, kind="ExternalInput")
    cx.I = I
    cx.y = dram("y", (T, D), F32, kind="ExternalOutput")

    S = {}
    dbg = getattr(build_program, "debug_outputs", ())
    for name, shape, dt in [
            ("x1", (T, D), F32), ("x2", (T, D), F32), ("x3", (T, D), F32),
            ("aKT", (D, T), BF16), ("aQT", (D, T), BF16),
            ("aV", (T, NH * 65), BF16), ("gT", (NH, T), F32),
            ("t_ffn", (T, D), F32),
            ("wrT", (D, D), BF16), ("wkT", (D, D), BF16), ("wvT", (D, D), BF16),
            ("tb", (T, D), F32),
            ("qa", (T, D), F32), ("ka", (T, D), F32),
            ("gj", (T, D), F32), ("pp", (T, D), F32),
            ("bQT", (D, T), BF16), ("bKT", (D, T), BF16), ("bV", (T, D), BF16),
            ("x2i", (T, D), F32),
            ("brow", (4096,), F32), ("dbgst", (4 * P, 64), F32),
            ("dbgbc", (4 * P, D), F32)]:
        S[name] = dram(name, shape, dt,
                       kind="ExternalOutput" if name in dbg else "Internal")
    cx.S = S
    cx.dt = {
        "x1": toks(NT, "dx1"), "x2": toks(NT, "dx2"), "x3": toks(NT, "dx3"),
        "t_ffn": toks(NT, "dtf"), "tb": toks(NT, "dtb"),
        "qa": toks(NT, "dqa"), "ka": toks(NT, "dka"),
        "gj": toks(NT, "dgj"), "pp": toks(NT, "dpp"),
        "x2i": toks(NT, "dx2i"), "bV": toks(NT, "dbV"),
        "bQT": toks(NT, "dbQT"), "bKT": toks(NT, "dbKT"),
        "aKT": toks(NG, "daKT"), "aQT": toks(NG, "daQT"),
        "aV": toks(NG, "daV"), "gT": toks(NG, "dgT"),
        "wrT": [Tok("dwrT")], "wkT": [Tok("dwkT")], "wvT": [Tok("dwvT")],
        "brow": [Tok("dbrow")],
        "y": toks(NT, "dy"),
    }
    cx.artoks = []

    mgr = []

    def sbuf(name, shape, dtype):
        h = nc.sbuf_tensor(name, list(shape), dtype)
        t = h.__enter__()
        mgr.append(h)
        return t
    cx._mgr = mgr

    if getattr(build_program, "STATS_FIRST", False):
        stA = sbuf("stA", (P, 64), F32)
        stB = sbuf("stB", (P, 64), F32)
        stC = sbuf("stC", (P, 64), F32)
        stD = sbuf("stD", (P, 64), F32)
        stE = sbuf("stE", (P, 64), F32)
        stF = sbuf("stF", (P, 64), F32)
        clockc = sbuf("clockc", (P, 64), F32)
        ccol = [sbuf(f"ccol{i}", (P, 8), F32) for i in range(4)]
        carry = sbuf("carry", (1, 2 * D), F32)
    AR = sbuf("AR", (P, 60 * 1024), BF16)
    TL = [sbuf(f"TL{i}", (P, D), F32) for i in range(8)]
    BC = [sbuf(f"BC{i}", (P, D), F32) for i in range(4)]
    OUTS = sbuf("OUTS", (P, 4096), F32)
    RP = sbuf("RP", (P, 2048), F32)
    identf = sbuf("identf", (P, P), F32)
    ident = sbuf("ident", (P, P), BF16)
    tri_f = sbuf("tri_f", (P, P), F32)
    onesb = sbuf("onesb", (1, P), BF16)
    onesf = sbuf("onesf", (1, P), F32)
    onescol = sbuf("onescol", (P, 1), F32)
    if not getattr(build_program, "STATS_FIRST", False):
        stA = sbuf("stA", (P, 64), F32)
        stB = sbuf("stB", (P, 64), F32)
        stC = sbuf("stC", (P, 64), F32)
        stD = sbuf("stD", (P, 64), F32)
        stE = sbuf("stE", (P, 64), F32)
        stF = sbuf("stF", (P, 64), F32)
    if not getattr(build_program, "STATS_FIRST", False):
        clockc = sbuf("clockc", (P, 64), F32)
        ccol = [sbuf(f"ccol{i}", (P, 8), F32) for i in range(4)]
        carry = sbuf("carry", (1, 2 * D), F32)

    cx.AR, cx.TL, cx.BC, cx.OUTS, cx.RP = AR, TL, BC, OUTS, RP
    cx.identf, cx.ident, cx.tri_f = identf, ident, tri_f
    cx.onesb, cx.onesf, cx.onescol = onesb, onesf, onescol
    cx.stA, cx.stB, cx.stC, cx.stD, cx.stE = stA, stB, stC, stD, stE
    cx.stF = stF
    cx.clockc, cx.carry = clockc, carry
    cx.ccol = ccol

    h = nc.psum_tensor("psA", [P, 2048], F32)
    psA = h.__enter__(); mgr.append(h)
    psB = []
    for i in range(2):
        h = nc.psum_tensor(f"psB{i}", [P, 512], F32)
        psB.append(h.__enter__()); mgr.append(h)
    ptA = []
    for i in range(2):
        h = nc.psum_tensor(f"ptA{i}", [P, 1024], BF16)
        ptA.append(h.__enter__()); mgr.append(h)
    cx.psA, cx.psB, cx.ptA = psA, psB, ptA

    tk = cx.tk = {}
    tk["psA"] = toks(4, "psA")
    tk["psB"] = toks(2, "psB")
    tk["ptA"] = toks(2, "ptA")
    tk["TL"] = toks(8, "TL")
    tk["BC"] = toks(4, "BC")
    tk["OUTS"] = Tok("OUTS")
    tk["RP"] = toks(2, "RP")
    tk["const"] = Tok("const")
    tk["stats"] = Tok("stats")
    tk["clock"] = Tok("clock")
    tk["carry"] = Tok("carry")
    tk["ccol"] = toks(4, "ccol")

    ctk = tk["const"]

    def gms(ap, v, tag):
        em("gpsimd", lambda e, ap=ap, v=v: e.memset(ap, v),
           r=(), w=[ctk], tag=tag)

    gms(ident[:], 0.0, "id0")
    em("gpsimd", lambda e: e.affine_select(
        out=ident[:], in_=ident[:], compare_op=ALU.not_equal, fill=1.0,
        base=0, pattern=[[-1, P]], channel_multiplier=1), w=[ctk], tag="id1")
    gms(identf[:], 0.0, "idf0")
    em("gpsimd", lambda e: e.affine_select(
        out=identf[:], in_=identf[:], compare_op=ALU.not_equal, fill=1.0,
        base=0, pattern=[[-1, P]], channel_multiplier=1), w=[ctk], tag="idf1")
    gms(tri_f[:], 1.0, "tri0")
    em("gpsimd", lambda e: e.affine_select(
        out=tri_f[:], in_=tri_f[:], compare_op=ALU.is_ge, fill=0.0,
        base=0, pattern=[[1, P]], channel_multiplier=-1), w=[ctk], tag="tri1")
    gms(onesb[:], 1.0, "ob")
    gms(onesf[:], 1.0, "of")
    gms(onescol[:], 1.0, "oc")
    em("scalar", lambda e: e.activation(stA[:, 0:1], onescol[:], AF.Ln,
                                        bias=0.0, scale=1.0),
       r=[ctk], w=[Tok("wu1")], tag="warm_ln")
    em("scalar", lambda e: e.activation(stA[:, 0:1], onescol[:], AF.Exp,
                                        bias=0.0, scale=1.0),
       r=[ctk], w=[Tok("wu2")], tag="warm_exp")

    cx.A_DMASK = 58 * 1024
    dm = lambda off=cx.A_DMASK: bass.AP(AR, off, [[60 * 1024, P], [1, 2048]])
    cx.dmask_ap = dm()
    gms(cx.dmask_ap, 1.0, "dm0")
    if not getattr(build_program, "SKIP_DMASK", False):
        em("gpsimd", lambda e: e.affine_select(
            out=bass.AP(AR, cx.A_DMASK, [[60 * 1024, P], [512, 4], [1, 512]]),
            in_=bass.AP(AR, cx.A_DMASK, [[60 * 1024, P], [512, 4], [1, 512]]),
            compare_op=ALU.is_ge, fill=0.0,
            base=0, pattern=[[-P, 4], [1, 512]], channel_multiplier=-1),
            w=[ctk], tag="dm1")

    return nc, cx


def ar_ap(cx, off, ap_dims):
    return bass.AP(cx.AR, off, [[60 * 1024, P]] + ap_dims)


def arena_phase(cx, names):
    old = cx.artoks
    newt = {nm: cx.B.alias(nm, *old) for nm in names}
    cx.artoks = list(newt.values())
    return newt



def ln_stats(cx, src, n0, nt, tag, fold_rms=False, src_toks=None):
    B = cx.B
    em = B.emit
    tlt = cx.tk["TL"]
    st = toks(nt, f"{tag}st")
    for i in range(nt):
        b = i % 2
        xt, xtt = cx.TL[b], tlt[b]
        em("sync", lambda e, i=i, xt=xt: e.dma_start(
            xt[:], src[(n0 + i) * P:(n0 + i + 1) * P, :]),
            r=([src_toks[n0 + i]] if src_toks else []), w=[xtt],
            dma=True, tag=f"{tag}ld{i}")
        em("vector", lambda e, i=i, xt=xt: e.reduce_sum(
            cx.stA[:, i:i + 1], xt[:], axis=AX.X),
            r=[xtt], w=[st[i]], tag=f"{tag}sm{i}")
        if getattr(ln_stats, "NO_ACCUM", False):
            em("scalar", lambda e, i=i, xt=xt: e.activation(
                cx.TL[2][:], xt[:], AF.Square, bias=0.0, scale=1.0),
                r=[xtt], w=[tlt[2]], tag=f"{tag}sq{i}")
            em("vector", lambda e, i=i: e.reduce_sum(
                cx.stB[:, i:i + 1], cx.TL[2][:], axis=AX.X),
                r=[tlt[2]], w=[st[i]], tag=f"{tag}sqr{i}")
        else:
            em("scalar", lambda e, i=i, xt=xt: e.activation(
                cx.TL[2][:], xt[:], AF.Square, bias=0.0, scale=1.0,
                accum_out=cx.stB[:, i:i + 1]),
                r=[xtt], w=[tlt[2], st[i]], tag=f"{tag}sq{i}")
    stok = Tok(f"{tag}stats")
    m, v, u, r2 = cx.TL[3], cx.TL[4], cx.TL[5], cx.TL[6]
    mt, vt, ut, r2t = tlt[3], tlt[4], tlt[5], tlt[6]
    sl = slice(0, nt)
    nmsl = slice(32, 32 + nt)
    em("vector", lambda e: e.tensor_scalar_mul(m[:, sl], cx.stA[:, sl], 1.0 / D),
       r=st, w=[mt], tag=f"{tag}m")
    em("scalar", lambda e: e.activation(cx.stD[:, nmsl], m[:, sl], AF.Copy,
                                        bias=0.0, scale=-1.0),
       r=[mt], w=[stok], tag=f"{tag}nm")
    em("vector", lambda e: e.tensor_scalar_mul(v[:, sl], cx.stB[:, sl], 1.0 / D),
       r=st, w=[vt], tag=f"{tag}v0")
    em("scalar", lambda e: e.activation(cx.stC[:, nmsl], m[:, sl], AF.Square,
                                        bias=0.0, scale=1.0),
       r=[mt], w=[stok], tag=f"{tag}ms")
    em("gpsimd", lambda e: e.tensor_sub(v[:, sl], v[:, sl], cx.stC[:, nmsl]),
       r=[vt, stok], w=[vt], tag=f"{tag}v")
    if fold_rms in (True, "both"):
        eps2 = RMS_EPS if fold_rms is True else LN_EPS
        dstC = cx.stC if fold_rms is True else cx.stE
        dstD = cx.stD if fold_rms is True else cx.stF
        em("vector", lambda e: e.tensor_scalar(
            r2[:, sl], v[:, sl], 1.0 + eps2, LN_EPS * eps2,
            op0=ALU.mult, op1=ALU.add),
            r=[vt], w=[r2t], tag=f"{tag}r2")
        em("scalar", lambda e: e.activation(dstC[:, sl], r2[:, sl], AF.Ln,
                                            bias=0.0, scale=1.0),
           r=[r2t], w=[stok], tag=f"{tag}l1")
        em("scalar", lambda e: e.activation(dstC[:, sl], dstC[:, sl], AF.Exp,
                                            bias=0.0, scale=-0.5),
           r=[stok], w=[stok], tag=f"{tag}ex")
        em("gpsimd", lambda e: e.tensor_mul(dstD[:, sl], cx.stD[:, nmsl],
                                            dstC[:, sl]),
           r=[stok], w=[stok], tag=f"{tag}fb")
    if fold_rms is not True:
        em("vector", lambda e: e.tensor_scalar_add(u[:, sl], v[:, sl], LN_EPS),
           r=[vt], w=[ut], tag=f"{tag}ue")
        em("scalar", lambda e: e.activation(cx.stC[:, sl], u[:, sl], AF.Ln,
                                            bias=0.0, scale=1.0),
           r=[ut], w=[stok], tag=f"{tag}l1p")
        em("scalar", lambda e: e.activation(cx.stC[:, sl], cx.stC[:, sl], AF.Exp,
                                            bias=0.0, scale=-0.5),
           r=[stok], w=[stok], tag=f"{tag}exp")
        em("gpsimd", lambda e: e.tensor_mul(cx.stD[:, sl], cx.stD[:, nmsl],
                                            cx.stC[:, sl]),
           r=[stok], w=[stok], tag=f"{tag}b1")
    return stok


def bcast_row(cx, dst, dst_tok, row_ap, row_toks, n, tag):
    em = cx.B.emit
    pbt = cx.tk["psB"]
    for j in range(0, n, 512):
        w = min(512, n - j)
        pi = (j // 512) % 2
        em("tensor", lambda e, j=j, w=w, pi=pi: e.matmul(
            cx.psB[pi][:, 0:w], cx.onesf[:], row_ap[0:1, j:j + w],
            start=True, stop=True),
            r=list(row_toks) + [cx.tk["const"]], w=[pbt[pi]], tag=f"{tag}m{j}")
        em("vector", lambda e, j=j, w=w, pi=pi: e.tensor_copy(
            dst[:, j:j + w], cx.psB[pi][:, 0:w]),
            r=[pbt[pi]], w=[dst_tok], tag=f"{tag}e{j}")


def load_row_tl(cx, tl_i, src_1d, n, tag):
    t = cx.tk["TL"][tl_i]
    cx.B.emit("sync", lambda e: e.dma_start(
        cx.TL[tl_i][0:1, 0:n], src_1d.rearrange("(o a) -> o a", o=1)),
        w=[t], dma=True, tag=tag)
    return t



A_FM = 0
A_WKV = 16 * 1024
A_WQ = 32 * 1024
A_WG = 40960
A_EV = 41984
A_VE = 44032
A_GE = 46336

PITCH = 60 * 1024


def pA(cx, js):
    return cx.psA[:, js * 512:(js + 1) * 512]


def arp(cx, off, dims, pbase=0, pcnt=P):
    return bass.AP(cx.AR, off + pbase * PITCH, [[PITCH, pcnt]] + dims)


def phase_s1(cx):
    nc, B, T, NT, NG = cx.nc, cx.B, cx.T, cx.NT, cx.NG
    em = B.emit
    I, S = cx.I, cx.S
    TL, BC = cx.TL, cx.BC
    tlt, bct = cx.tk["TL"], cx.tk["BC"]
    psA, psB = cx.psA, cx.psB
    pat, pbt = cx.tk["psA"], cx.tk["psB"]
    ctk = cx.tk["const"]

    at = arena_phase(cx, ["fm0", "fm1", "wkv", "wq", "wg",
                          "ev0", "ev1", "ev2", "ev3", "ve0", "ve1", "ge"])
    fm_t = [at["fm0"], at["fm1"]]
    wkv_t, wq_t, wg_t = at["wkv"], at["wq"], at["wg"]
    ev_t = [at[f"ev{i}"] for i in range(4)]
    ve_t = [at[f"ve{i}"] for i in range(2)]
    ge_t = at["ge"]
    cx.s1 = dict(fm_t=fm_t, wkv_t=wkv_t, wq_t=wq_t, wg_t=wg_t,
                 ev_t=ev_t, ve_t=ve_t, ge_t=ge_t)

    stok = ln_stats(cx, I["x"], 0, NT, "s1n", fold_rms=True)
    defer = [stok] if getattr(phase_s1, "DEFER_W", False) else []

    em("gpsimd", lambda e: e.dma_start(
        arp(cx, A_WKV, [[2048, C], [1, 2048]]),
        I["aWkv"].rearrange("(c p) n -> p c n", p=P)),
        r=defer, w=[wkv_t], dma=True, tag="ld_wkv")
    em("gpsimd", lambda e: e.dma_start(
        arp(cx, A_WQ, [[1024, C], [1, 1024]]),
        I["aWq"].rearrange("(c p) n -> p c n", p=P)),
        r=defer, w=[wq_t], dma=True, tag="ld_wq")
    em("gpsimd", lambda e: e.dma_start(
        arp(cx, A_WG, [[16, C], [1, 16]]),
        I["aWg"].rearrange("(c p) n -> p c n", p=P)),
        r=defer, w=[wg_t], dma=True, tag="ld_wg")
    nw_t = load_row_tl(cx, 7, I["a_normw"], D, "ld_nw")
    bcast_row(cx, BC[0], bct[0], TL[7][0:1, :], [nw_t], D, "bc_nw")
    for i in range(NT):
        b = i % 2
        g, gb = i // 4, (i // 4) % 2
        xt, xtt = TL[b], tlt[b]
        xf, xft = TL[2 + b], tlt[2 + b]
        em("sync", lambda e, i=i, xt=xt: e.dma_start(
            xt[:], I["x"][i * P:(i + 1) * P, :]),
            w=[xtt], dma=True, tag=f"s1a_ld{i}")
        em("vector", lambda e, i=i, xt=xt, xf=xf: e.tensor_scalar(
            xf[:], xt[:], cx.stC[:, i:i + 1], cx.stD[:, i:i + 1],
            op0=ALU.mult, op1=ALU.add),
            r=[xtt, stok], w=[xft], tag=f"s1a_sc{i}")
        em("vector", lambda e, xf=xf: e.tensor_mul(xf[:], xf[:], BC[0][:]),
           r=[xft, bct[0]], w=[xft], tag=f"s1a_nw{i}")
        em("tensor", lambda e, xf=xf, b=b: [
            e.transpose(psA[:, (2 * b) * 512 + c * P:(2 * b) * 512 + (c + 1) * P],
                        xf[:, c * P:(c + 1) * P], cx.identf[:])
            for c in range(C)][-1],
            r=[xft, ctk], w=[pat[2 * b], pat[2 * b + 1]], tag=f"s1a_tr{i}")
        for hhalf in range(2):
            em("scalar", lambda e, i=i, gb=gb, b=b, hh=hhalf: e.copy(
                bass.AP(cx.AR, A_FM + gb * 8192 + (4 * hh) * 1024 + (i % 4) * P,
                        [[PITCH, P], [1024, 4], [1, P]]),
                psA[:, (2 * b + hh) * 512:(2 * b + hh + 1) * 512]
                .rearrange("p (a b2) -> p a b2", a=4)),
                r=[pat[2 * b + hhalf]], w=[fm_t[gb], pat[2 * b + hhalf]],
                tag=f"s1a_ev{i}_{hhalf}")
        if i % 4 == 3:
            _s1_proj_group(cx, g, gb)


    _s1_attention(cx)


def _s1_proj_group(cx, g, gb):
    B, em, S = cx.B, cx.B.emit, cx.S
    psA, psB = cx.psA, cx.psB
    pat, pbt = cx.tk["psA"], cx.tk["psB"]
    d = cx.s1
    fm_t, wkv_t, wq_t, wg_t = d["fm_t"], d["wkv_t"], d["wq_t"], d["wg_t"]
    ev_t, ve_t, ge_t = d["ev_t"], d["ve_t"], d["ge_t"]

    def fm_rhs(c, j0=0, n=512):
        return arp(cx, A_FM + gb * 8192 + c * 1024 + j0, [[1, n]])

    pg = g % 2
    em("tensor", lambda e, pg=pg: [
        e.matmul(psB[pg][0:16, :], arp(cx, A_WG + c * 16, [[1, 16]]),
                 fm_rhs(c), start=(c == 0), stop=(c == C - 1))
        for c in range(C)][-1],
        r=[wg_t, fm_t[gb]], w=[pbt[pg]], tag=f"s1g_mm{g}")
    em("scalar", lambda e, pg=pg: e.activation(
        arp(cx, A_GE, [[1, 512]], pcnt=16), psB[pg][0:16, :],
        AF.Sigmoid, bias=0.0, scale=1.0),
        r=[pbt[pg]], w=[ge_t, pbt[pg]], tag=f"s1g_sg{g}")
    em("gpsimd", lambda e, g=g: e.dma_start(
        S["gT"][:, g * 512:(g + 1) * 512], arp(cx, A_GE, [[1, 512]], pcnt=16)),
        r=[ge_t], w=[cx.dt["gT"][g]], dma=True, tag=f"s1g_st{g}")

    for which, wtok, wbase, wpitch, dst in (
            ("k", wkv_t, A_WKV, 2048, S["aKT"]),
            ("q", wq_t, A_WQ, 1024, S["aQT"])):
        for dk in range(C):
            js = dk % 4
            evi = dk % 4
            em("tensor", lambda e, dk=dk, js=js, wbase=wbase, wpitch=wpitch,
               wtok=wtok: [
                e.matmul(pA(cx, js), arp(cx, wbase + c * wpitch + dk * P, [[1, P]]),
                         fm_rhs(c), start=(c == 0), stop=(c == C - 1))
                for c in range(C)][-1],
                r=[wtok, fm_t[gb]], w=[pat[js]], tag=f"s1{which}T_mm{dk}_{g}")
            em("scalar", lambda e, js=js, evi=evi: e.copy(
                arp(cx, A_EV + evi * 512, [[1, 512]]), pA(cx, js)),
                r=[pat[js]], w=[ev_t[evi], pat[js]],
                tag=f"s1{which}T_ev{dk}_{g}")
            dtk = cx.dt["aKT" if which == "k" else "aQT"][g]
            em("gpsimd", lambda e, dk=dk, g=g, evi=evi, dst=dst: e.dma_start(
                dst[dk * P:(dk + 1) * P, g * 512:(g + 1) * 512],
                arp(cx, A_EV + evi * 512, [[1, 512]])),
                r=[ev_t[evi]], w=[dtk], dma=True, tag=f"s1{which}T_st{dk}_{g}")

    for ii in range(4):
        jsA, jsB = 2 * (ii % 2), 2 * (ii % 2) + 1
        vv = ii % 2
        em("tensor", lambda e, ii=ii, jsA=jsA: [
            e.matmul(pA(cx, jsA), fm_rhs(c, ii * P, P),
                     arp(cx, A_WKV + c * 2048 + 1024, [[1, 512]]),
                     start=(c == 0), stop=(c == C - 1))
            for c in range(C)][-1],
            r=[wkv_t, fm_t[gb]], w=[pat[jsA]], tag=f"s1v_mmA{ii}_{g}")
        em("tensor", lambda e, ii=ii, jsB=jsB: [
            e.matmul(pA(cx, jsB), fm_rhs(c, ii * P, P),
                     arp(cx, A_WKV + c * 2048 + 1536, [[1, 512]]),
                     start=(c == 0), stop=(c == C - 1))
            for c in range(C)][-1],
            r=[wkv_t, fm_t[gb]], w=[pat[jsB]], tag=f"s1v_mmB{ii}_{g}")
        em("gpsimd", lambda e, vv=vv: e.memset(
            arp(cx, A_VE + vv * 1088, [[1, 1040]]), 1.0),
            r=[], w=[ve_t[vv]], tag=f"s1v_on{ii}_{g}")
        em("vector", lambda e, vv=vv, jsA=jsA: e.tensor_copy(
            bass.AP(cx.AR, A_VE + vv * 1088, [[PITCH, P], [65, 8], [1, 64]]),
            pA(cx, jsA).rearrange("p (a b2) -> p a b2", a=8)),
            r=[pat[jsA]], w=[ve_t[vv], pat[jsA]], tag=f"s1v_evA{ii}_{g}")
        em("vector", lambda e, vv=vv, jsB=jsB: e.tensor_copy(
            bass.AP(cx.AR, A_VE + vv * 1088 + 8 * 65,
                    [[PITCH, P], [65, 8], [1, 64]]),
            pA(cx, jsB).rearrange("p (a b2) -> p a b2", a=8)),
            r=[pat[jsB]], w=[ve_t[vv], pat[jsB]], tag=f"s1v_evB{ii}_{g}")
        em("gpsimd", lambda e, ii=ii, g=g, vv=vv: e.dma_start(
            S["aV"][(g * 4 + ii) * P:(g * 4 + ii + 1) * P, :],
            arp(cx, A_VE + vv * 1088, [[1, 1040]])),
            r=[ve_t[vv]], w=[cx.dt["aV"][g]], dma=True, tag=f"s1v_st{ii}_{g}")


def _s1_attention(cx):
    B, em, S, I = cx.B, cx.B.emit, cx.S, cx.I
    T, NG = cx.T, cx.NG
    psA, psB = cx.psA, cx.psB
    pat, pbt = cx.tk["psA"], cx.tk["psB"]
    TL, tlt, RP = cx.TL, cx.tk["TL"], cx.RP
    rpt = cx.tk["RP"]
    ctk = cx.tk["const"]
    OUTS, otk = cx.OUTS, cx.tk["OUTS"]
    d = cx.s1

    A_QT = 0
    A_KT = 4 * 1024
    A_V = 12 * 1024
    A_PT = 17 * 1024
    A_WO = 21 * 1024 + 512
    A_YT = 29 * 1024 + 1024

    at = arena_phase(cx, ["qt", "kt0", "kt1", "v0", "v1",
                          "pt0a", "pt0b", "pt1a", "pt1b", "wo", "yt"])
    qt_t = at["qt"]
    kt_t = [at["kt0"], at["kt1"]]
    v_t = [at["v0"], at["v1"]]
    pt_t = [[at["pt0a"], at["pt0b"]], [at["pt1a"], at["pt1b"]]]
    wo_t = at["wo"]
    yt_t = at["yt"]
    cx.s1at = at

    em("gpsimd", lambda e: e.dma_start(
        arp(cx, A_WO, [[1024, C], [1, 1024]]),
        I["aWout"].rearrange("(c p) n -> p c n", p=P)),
        r=(), w=[wo_t], dma=True, tag="ld_wout")

    for qt in range(NG):
        em("sync", lambda e, qt=qt: e.dma_start(
            arp(cx, A_QT, [[512, C], [1, 512]]),
            S["aQT"][:, qt * 512:(qt + 1) * 512]
            .rearrange("(c p) n -> p c n", p=P)),
            r=[cx.dt["aQT"][qt]], w=[qt_t], dma=True, tag=f"at_q{qt}")
        for hg in range(2):
            for jg in range(qt + 1):
                bj = jg % 2
                em("sync", lambda e, jg=jg, bj=bj: e.dma_start(
                    arp(cx, A_KT + bj * 4096, [[512, C], [1, 512]]),
                    S["aKT"][:, jg * 512:(jg + 1) * 512]
                    .rearrange("(c p) n -> p c n", p=P)),
                    r=[cx.dt["aKT"][jg]], w=[kt_t[bj]], dma=True,
                    tag=f"at_k{qt}_{hg}_{jg}")
                em("gpsimd", lambda e, jg=jg, bj=bj, hg=hg: e.dma_start(
                    arp(cx, A_V + bj * 2176, [[520, 4], [1, 520]]),
                    S["aV"].rearrange("(a p) n -> p a n", p=P)
                    [:, jg * 4:(jg + 1) * 4, hg * 520:(hg + 1) * 520]),
                    r=[cx.dt["aV"][jg]], w=[v_t[bj]], dma=True,
                    tag=f"at_v{qt}_{hg}_{jg}")
                for h in range(8):
                    habs = hg * 8 + h
                    ch, off = habs // 2, (habs % 2) * 64
                    bp = h % 2
                    for hf in range(2):
                        em("tensor", lambda e, ch=ch, off=off, bj=bj, hf=hf: [
                            e.matmul(pA(cx, js),
                                     arp(cx, A_KT + bj * 4096 + ch * 512
                                         + js * P, [[1, P]], pbase=off,
                                         pcnt=64),
                                     arp(cx, A_QT + ch * 512, [[1, 512]],
                                         pbase=off, pcnt=64),
                                     start=True, stop=True)
                            for js in (2 * hf, 2 * hf + 1)][-1],
                            r=[kt_t[bj], qt_t],
                            w=[pat[2 * hf], pat[2 * hf + 1]],
                            tag=f"at_s{qt}_{hg}_{jg}_{h}_{hf}")
                    for hf in range(2):
                        em("scalar", lambda e, bp=bp, hf=hf: e.activation(
                            arp(cx, A_PT + bp * 2048 + hf * 1024, [[1, 1024]]),
                            psA[:, hf * 1024:(hf + 1) * 1024],
                            AF.Exp, bias=0.0, scale=SC_A),
                            r=[pat[2 * hf], pat[2 * hf + 1]],
                            w=[pt_t[bp][hf], pat[2 * hf], pat[2 * hf + 1]],
                            tag=f"at_e{qt}_{hg}_{jg}_{h}_{hf}")
                        if jg == qt:
                            em("vector", lambda e, bp=bp, hf=hf: e.tensor_mul(
                                arp(cx, A_PT + bp * 2048 + hf * 1024,
                                    [[1, 1024]]),
                                arp(cx, A_PT + bp * 2048 + hf * 1024,
                                    [[1, 1024]]),
                                bass.AP(cx.AR, cx.A_DMASK + hf * 1024,
                                        [[60 * 1024, P], [1, 1024]])),
                                r=[pt_t[bp][hf], ctk], w=[pt_t[bp][hf]],
                                tag=f"at_m{qt}_{hg}_{jg}_{h}_{hf}")
                    for hf in range(2):
                        em("tensor", lambda e, h=h, bj=bj, bp=bp, hf=hf: [
                            e.matmul(psB[bp][0:65, :],
                                     bass.AP(cx.AR, A_V + bj * 2176 + js * 520
                                             + h * 65, [[PITCH, P], [1, 65]]),
                                     arp(cx, A_PT + bp * 2048 + js * 512,
                                         [[1, 512]]),
                                     start=(js == 0), stop=(js == 3))
                            for js in (2 * hf, 2 * hf + 1)][-1],
                            r=[v_t[bj], pt_t[bp][hf]], w=[pbt[bp]],
                            tag=f"at_av{qt}_{hg}_{jg}_{h}_{hf}")
                    if jg == 0:
                        em("vector", lambda e, h=h, bp=bp: e.tensor_copy(
                            OUTS[0:65, h * 512:(h + 1) * 512],
                            psB[bp][0:65, :]),
                            r=[pbt[bp]], w=[otk, pbt[bp]],
                            tag=f"at_o0{qt}_{hg}_{h}")
                    else:
                        em("vector", lambda e, h=h, bp=bp: e.tensor_add(
                            OUTS[0:65, h * 512:(h + 1) * 512],
                            OUTS[0:65, h * 512:(h + 1) * 512],
                            psB[bp][0:65, :]),
                            r=[pbt[bp], otk], w=[otk, pbt[bp]],
                            tag=f"at_oa{qt}_{hg}_{jg}_{h}")
            for hb in range(2):
                em("sync", lambda e, qt=qt, hg=hg, hb=hb: e.dma_start(
                    RP[0:1, :].rearrange("o (h n) -> o h n", h=4),
                    S["gT"][hg * 8 + hb * 4:hg * 8 + hb * 4 + 4,
                            qt * 512:(qt + 1) * 512]
                    .rearrange("(o h) n -> o h n", o=1)),
                    r=[cx.dt["gT"][qt]], w=[rpt[0], rpt[1]], dma=True,
                    tag=f"ae_g{qt}_{hg}_{hb}")
                for hh in range(4):
                    h = hb * 4 + hh
                    habs = hg * 8 + h
                    ch, off = habs // 2, (habs % 2) * 64
                    bs = h % 2
                    em("scalar", lambda e, h=h, bs=bs: e.activation(
                        TL[5][0:1, bs * 512:(bs + 1) * 512],
                        OUTS[64:65, h * 512:(h + 1) * 512],
                        AF.Ln, bias=0.0, scale=1.0),
                        r=[otk], w=[tlt[5]], tag=f"ae_r{qt}_{hg}_{h}")
                    em("scalar", lambda e, h=h, bs=bs: e.activation(
                        TL[5][0:1, bs * 512:(bs + 1) * 512],
                        TL[5][0:1, bs * 512:(bs + 1) * 512],
                        AF.Exp, bias=0.0, scale=-1.0),
                        r=[tlt[5]], w=[tlt[5]], tag=f"ae_r2{qt}_{hg}_{h}")
                    em("vector", lambda e, hh=hh, bs=bs: e.tensor_mul(
                        TL[5][0:1, bs * 512:(bs + 1) * 512],
                        TL[5][0:1, bs * 512:(bs + 1) * 512],
                        RP[0:1, hh * 512:(hh + 1) * 512]),
                        r=[tlt[5], rpt[0], rpt[1]], w=[tlt[5]],
                        tag=f"ae_s{qt}_{hg}_{h}")
                    em("tensor", lambda e, bs=bs: e.matmul(
                        psB[bs][0:64, :], cx.onesf[0:1, 0:64],
                        TL[5][0:1, bs * 512:(bs + 1) * 512],
                        start=True, stop=True),
                        r=[tlt[5], ctk], w=[pbt[bs]], tag=f"ae_b{qt}_{hg}_{h}")
                    em("vector", lambda e, h=h, bs=bs, ch=ch, off=off:
                       e.tensor_mul(
                        arp(cx, A_YT + ch * 512, [[1, 512]], pbase=off, pcnt=64),
                        OUTS[0:64, h * 512:(h + 1) * 512],
                        psB[bs][0:64, :]),
                        r=[otk, pbt[bs]], w=[yt_t, pbt[bs]],
                        tag=f"ae_y{qt}_{hg}_{h}")
        for ts in range(4):
            j0, j1 = (ts % 2) * 2, (ts % 2) * 2 + 1
            em("tensor", lambda e, ts=ts, j0=j0, j1=j1: [
                e.matmul(pA(cx, (j0 if nb == 0 else j1)),
                         arp(cx, A_YT + ch2 * 512 + ts * P, [[1, P]]),
                         arp(cx, A_WO + ch2 * 1024 + nb * 512, [[1, 512]]),
                         start=(ch2 == 0), stop=(ch2 == C - 1))
                for nb in range(2) for ch2 in range(C)][-1],
                r=[yt_t, wo_t], w=[pat[j0], pat[j1]],
                tag=f"ao_mm{qt}_{ts}")
            bx = ts % 2
            em("sync", lambda e, qt=qt, ts=ts, bx=bx: e.dma_start(
                TL[bx][:], I["x"][(qt * 4 + ts) * P:(qt * 4 + ts + 1) * P, :]),
                w=[tlt[bx]], dma=True, tag=f"ao_ldx{qt}_{ts}")
            em("vector", lambda e, bx=bx, j0=j0, j1=j1: [
                e.tensor_add(TL[2 + bx][:, nb * 512:(nb + 1) * 512],
                             TL[bx][:, nb * 512:(nb + 1) * 512],
                             pA(cx, (j0 if nb == 0 else j1)))
                for nb in range(2)][-1],
                r=[tlt[bx], pat[j0], pat[j1]],
                w=[tlt[2 + bx], pat[j0], pat[j1]],
                tag=f"ao_add{qt}_{ts}")
            em("gpsimd", lambda e, qt=qt, ts=ts, bx=bx: e.dma_start(
                S["x1"][(qt * 4 + ts) * P:(qt * 4 + ts + 1) * P, :],
                TL[2 + bx][:]),
                r=[tlt[2 + bx]], w=[cx.dt["x1"][qt * 4 + ts]], dma=True,
                tag=f"ao_st{qt}_{ts}")



A2_W = 0
A2_XT = 24576
A2_KT = 28672
A2_WL = 30720


def transpose_ffn_weights(cx):
    B, em, S, I = cx.B, cx.B.emit, cx.S, cx.I
    ptA = cx.ptA
    ptt = cx.tk["ptA"]
    ctk = cx.tk["const"]
    at = arena_phase(cx, ["wl", "st0", "st1"])
    wl_t = at["wl"]
    st_t = [at["st0"], at["st1"]]
    for wi, (wname, dname) in enumerate(
            (("fWr", "wrT"), ("fWk", "wkT"), ("fWv", "wvT"))):
        em("gpsimd", lambda e, wname=wname: e.dma_start(
            arp(cx, A2_WL, [[1024, C], [1, 1024]]),
            I[wname].rearrange("(c p) n -> p c n", p=P)),
            w=[wl_t], dma=True, tag=f"wt_ld{wi}")
        for cc in range(C):
            bp = cc % 2
            em("tensor", lambda e, cc=cc, bp=bp: [
                e.transpose(ptA[bp][:, dc * P:(dc + 1) * P],
                            arp(cx, A2_WL + dc * 1024 + cc * P, [[1, P]]),
                            cx.ident[:])
                for dc in range(C)][-1],
                r=[wl_t, ctk], w=[ptt[bp]], tag=f"wt_tr{wi}_{cc}")
            em("scalar", lambda e, cc=cc, bp=bp: e.copy(
                arp(cx, A2_KT + bp * 1024, [[1, 1024]]), ptA[bp][:, :]),
                r=[ptt[bp]], w=[st_t[bp], ptt[bp]], tag=f"wt_ev{wi}_{cc}")
            em("gpsimd", lambda e, cc=cc, bp=bp, dname=dname: e.dma_start(
                S[dname][cc * P:(cc + 1) * P, :],
                arp(cx, A2_KT + bp * 1024, [[1, 1024]])),
                r=[st_t[bp]], w=[cx.dt[dname][0]], dma=True,
                tag=f"wt_st{wi}_{cc}")


def phase_ffn(cx, src_name, dst_name, which):
    B, em, S, I = cx.B, cx.B.emit, cx.S, cx.I
    T, NT = cx.T, cx.NT
    TL, BC = cx.TL, cx.BC
    tlt, bct = cx.tk["TL"], cx.tk["BC"]
    psA, psB = cx.psA, cx.psB
    pat, pbt = cx.tk["psA"], cx.tk["psB"]
    ctk = cx.tk["const"]
    src = S[src_name]
    dst_ap = cx.y if dst_name == "y" else S[dst_name]
    dst_toks = cx.dt[dst_name]

    pfx = f"f{which}"
    at = arena_phase(cx, ["w0", "w1", "w2", "xt0", "xt1", "kt0", "kt1"])
    w_t = [at[f"w{i}"] for i in range(3)]
    xt_t = [at["xt0"], at["xt1"]]
    kt_t = [at["kt0"], at["kt1"]]
    for wi, dname in enumerate(("wrT", "wkT", "wvT")):
        em("sync", lambda e, wi=wi, dname=dname: e.dma_start(
            arp(cx, A2_W + wi * 8192, [[1024, C], [1, 1024]]),
            S[dname].rearrange("(c p) n -> p c n", p=P)),
            r=[cx.dt[dname][0]], w=[w_t[wi]], dma=True, tag=f"{pfx}_ldw{wi}")
    mur_t = load_row_tl(cx, 7, I["fmur"], D, f"{pfx}_ldmur")
    bcast_row(cx, BC[1], bct[1], TL[7][0:1, :], [mur_t], D, f"{pfx}_bmur")
    muk_t = load_row_tl(cx, 7, I["fmuk"], D, f"{pfx}_ldmuk")
    bcast_row(cx, BC[2], bct[2], TL[7][0:1, :], [muk_t], D, f"{pfx}_bmuk")

    stok = ln_stats(cx, src, 0, NT, f"{pfx}n", src_toks=cx.dt[src_name])
    for i in range(NT):
        b = i % 2
        xt, xtt = TL[b], tlt[b]
        tt_, ttt = TL[2 + b], tlt[2 + b]
        em("sync", lambda e, i=i, xt=xt: e.dma_start(
            xt[:], src[i * P:(i + 1) * P, :]),
            r=[cx.dt[src_name][i]], w=[xtt], dma=True, tag=f"{pfx}t_ld{i}")
        em("vector", lambda e, i=i, xt=xt, tt_=tt_: e.tensor_scalar(
            tt_[:], xt[:], cx.stC[:, i:i + 1], cx.stD[:, i:i + 1],
            op0=ALU.mult, op1=ALU.add),
            r=[xtt, stok], w=[ttt], tag=f"{pfx}t_sc{i}")
        em("gpsimd", lambda e, i=i, tt_=tt_: e.dma_start(
            S["t_ffn"][i * P:(i + 1) * P, :], tt_[:]),
            r=[ttt], w=[cx.dt["t_ffn"][i]], dma=True, tag=f"{pfx}t_st{i}")

    pend = None
    for i in range(NT):
        b = i % 2
        ta, tat = TL[b], tlt[b]
        ts_, tst = TL[2 + b], tlt[2 + b]
        dd, ddt = TL[4], tlt[4]
        xr, xrt = TL[5], tlt[5]
        ka, kat = TL[6], tlt[6]
        rr, rrt = TL[7], tlt[7]
        rx, rxt = (BC[0], bct[0]) if i % 2 == 0 else (BC[3], bct[3])
        em("sync", lambda e, i=i, ta=ta: e.dma_start(
            ta[:], S["t_ffn"][i * P:(i + 1) * P, :]),
            r=[cx.dt["t_ffn"][i]], w=[tat], dma=True, tag=f"{pfx}m_lda{i}")
        em("sync", lambda e, i=i, rx=rx: e.dma_start(
            rx[:], src[i * P:(i + 1) * P, :]),
            r=[cx.dt[src_name][i]], w=[rxt], dma=True, tag=f"{pfx}m_ldx{i}")
        if i == 0:
            em("gpsimd", lambda e, ts_=ts_: e.memset(ts_[0:1, :], 0.0),
               w=[tst], tag=f"{pfx}m_z0")
            em("sync", lambda e, ts_=ts_: e.dma_start(
                ts_[1:P, :], S["t_ffn"][0:P - 1, :]),
                r=[cx.dt["t_ffn"][0]], w=[tst], dma=True, tag=f"{pfx}m_lds0")
        else:
            em("sync", lambda e, i=i, ts_=ts_: e.dma_start(
                ts_[:], S["t_ffn"][i * P - 1:(i + 1) * P - 1, :]),
                r=[cx.dt["t_ffn"][i - 1], cx.dt["t_ffn"][i]], w=[tst],
                dma=True, tag=f"{pfx}m_lds{i}")
        em("vector", lambda e, ta=ta, ts_=ts_, dd=dd: e.tensor_sub(
            dd[:], ta[:], ts_[:]),
            r=[tat, tst], w=[ddt], tag=f"{pfx}m_d{i}")
        for half, (mubc, mubct, xoff) in enumerate(
                ((BC[1], bct[1], 0), (BC[2], bct[2], 1024))):
            em("vector", lambda e, dd=dd, xr=xr, mubc=mubc: e.tensor_mul(
                xr[:], dd[:], mubc[:]),
                r=[ddt, mubct], w=[xrt], tag=f"{pfx}m_x{half}a{i}")
            em("vector", lambda e, xr=xr, ts_=ts_: e.tensor_add(
                xr[:], xr[:], ts_[:]),
                r=[xrt, tst], w=[xrt], tag=f"{pfx}m_x{half}b{i}")
            pa0 = 2 * (i % 2)
            em("tensor", lambda e, xr=xr, pa0=pa0: [
                e.transpose(psA[:, pa0 * 512 + c * P:pa0 * 512 + (c + 1) * P],
                            xr[:, c * P:(c + 1) * P], cx.identf[:])
                for c in range(C)][-1],
                r=[xrt, ctk], w=[pat[pa0], pat[pa0 + 1]],
                tag=f"{pfx}m_xt{half}_{i}")
            em("scalar", lambda e, pa0=pa0, xoff=xoff, b=b: e.copy(
                arp(cx, A2_XT + b * 2048 + xoff, [[1, 1024]]),
                psA[:, pa0 * 512:pa0 * 512 + 1024]),
                r=[pat[pa0], pat[pa0 + 1]],
                w=[xt_t[b], pat[pa0], pat[pa0 + 1]],
                tag=f"{pfx}m_xe{half}_{i}")
        if pend is not None:
            pend()
            pend = None
        em("tensor", lambda e, b=b: [
            e.matmul(psB[nb][:, :],
                     arp(cx, A2_XT + b * 2048 + c * P, [[1, P]]),
                     arp(cx, A2_W + 0 + c * 1024 + nb * 512, [[1, 512]]),
                     start=(c == 0), stop=(c == C - 1))
            for nb in range(2) for c in range(C)][-1],
            r=[xt_t[b], w_t[0]], w=[pbt[0], pbt[1]], tag=f"{pfx}m_rmm{i}")
        em("scalar", lambda e, rr=rr: [
            e.activation(rr[:, nb * 512:(nb + 1) * 512], psB[nb][:, :],
                         AF.Sigmoid, bias=0.0, scale=1.0)
            for nb in range(2)][-1],
            r=[pbt[0], pbt[1]], w=[rrt, pbt[0], pbt[1]],
            tag=f"{pfx}m_rsg{i}")
        pa0 = 2 * (i % 2)
        em("tensor", lambda e, b=b, pa0=pa0: [
            e.matmul(psA[:, (pa0 + nb) * 512:(pa0 + nb + 1) * 512],
                     arp(cx, A2_XT + b * 2048 + 1024 + c * P, [[1, P]]),
                     arp(cx, A2_W + 8192 + c * 1024 + nb * 512, [[1, 512]]),
                     start=(c == 0), stop=(c == C - 1))
            for nb in range(2) for c in range(C)][-1],
            r=[xt_t[b], w_t[1]], w=[pat[pa0], pat[pa0 + 1]],
            tag=f"{pfx}m_kmm{i}")
        em("scalar", lambda e, ka=ka, pa0=pa0: e.activation(
            ka[:], psA[:, pa0 * 512:pa0 * 512 + 1024], AF.Relu,
            bias=0.0, scale=1.0),
            r=[pat[pa0], pat[pa0 + 1]], w=[kat, pat[pa0], pat[pa0 + 1]],
            tag=f"{pfx}m_krl{i}")
        em("scalar", lambda e, ka=ka: e.activation(
            ka[:], ka[:], AF.Square, bias=0.0, scale=1.0),
            r=[kat], w=[kat], tag=f"{pfx}m_ksq{i}")
        em("tensor", lambda e, ka=ka, pa0=pa0: [
            e.transpose(psA[:, pa0 * 512 + c * P:pa0 * 512 + (c + 1) * P],
                        ka[:, c * P:(c + 1) * P], cx.identf[:])
            for c in range(C)][-1],
            r=[kat, ctk], w=[pat[pa0], pat[pa0 + 1]], tag=f"{pfx}m_kt{i}")
        em("scalar", lambda e, pa0=pa0, b=b: e.copy(
            arp(cx, A2_KT + b * 1024, [[1, 1024]]),
            psA[:, pa0 * 512:pa0 * 512 + 1024]),
            r=[pat[pa0], pat[pa0 + 1]], w=[kt_t[b], pat[pa0], pat[pa0 + 1]],
            tag=f"{pfx}m_ke{i}")
        em("tensor", lambda e, b=b: [
            e.matmul(psB[nb][:, :],
                     arp(cx, A2_KT + b * 1024 + c * P, [[1, P]]),
                     arp(cx, A2_W + 16384 + c * 1024 + nb * 512, [[1, 512]]),
                     start=(c == 0), stop=(c == C - 1))
            for nb in range(2) for c in range(C)][-1],
            r=[kt_t[b], w_t[2]], w=[pbt[0], pbt[1]], tag=f"{pfx}m_vmm{i}")

        def mk_back(i=i, ka=ka, kat=kat, rr=rr, rrt=rrt, rx=rx, rxt=rxt):
            def go():
                em("vector", lambda e: [
                    e.tensor_mul(ka[:, nb * 512:(nb + 1) * 512],
                                 rr[:, nb * 512:(nb + 1) * 512], psB[nb][:, :])
                    for nb in range(2)][-1],
                    r=[rrt, pbt[0], pbt[1]], w=[kat, pbt[0], pbt[1]],
                    tag=f"{pfx}m_mul{i}")
                em("vector", lambda e: e.tensor_add(
                    ka[:], ka[:], rx[:]),
                    r=[kat, rxt], w=[kat], tag=f"{pfx}m_res{i}")
                em("gpsimd", lambda e: e.dma_start(
                    dst_ap[i * P:(i + 1) * P, :], ka[:]),
                    r=[kat], w=[dst_toks[i]], dma=True, tag=f"{pfx}m_st{i}")
            return go
        pend = mk_back()
    pend()



A3_FM = 0
A3_W = 16384
A3_EV = 57344
A3_QE = 59392

A4_QT = 0
A4_KT = 4096
A4_V = 12288
A4_P = 20480
A4_PT = 21504
A4_WO = 22528
A4_OT = 30720

A5_FM = 0
A5_HG = 16384
A5_W1 = 32768
A5_W2 = 40960


def phase_s3_p01(cx):
    B, em, S, I = cx.B, cx.B.emit, cx.S, cx.I
    T, NT, NG = cx.T, cx.NT, cx.NG
    TL, BC, RP = cx.TL, cx.BC, cx.RP
    tlt, bct, rpt = cx.tk["TL"], cx.tk["BC"], cx.tk["RP"]
    psA, psB = cx.psA, cx.psB
    pat, pbt = cx.tk["psA"], cx.tk["psB"]
    ctk = cx.tk["const"]
    OUTS, otk = cx.OUTS, cx.tk["OUTS"]
    ccol, cct = cx.ccol, cx.tk["ccol"]
    clkt = cx.tk["clock"]

    at = arena_phase(cx, ["fm0", "fm1", "wq", "wk", "wv", "wg", "wp",
                          "ve0", "ve1", "qe0", "qe1"])
    fm_t = [at["fm0"], at["fm1"]]
    w_t = [at["wq"], at["wk"], at["wv"], at["wg"], at["wp"]]
    ve_t = [at["ve0"], at["ve1"]]
    cx.s3 = dict(fm_t=fm_t, w_t=w_t, ve_t=ve_t,
                 qe_t=[at["qe0"], at["qe1"]])

    for wi, wname in enumerate(("bWq", "bWk", "bWv", "bWg", "bWp")):
        em("gpsimd", lambda e, wi=wi, wname=wname: e.dma_start(
            arp(cx, A3_W + wi * 8192, [[1024, C], [1, 1024]]),
            I[wname].rearrange("(c p) n -> p c n", p=P)),
            w=[w_t[wi]], dma=True, tag=f"s3_ldw{wi}")
    g1_t = load_row_tl(cx, 7, I["b_ln1g"], D, "s3_ldg1")
    bcast_row(cx, BC[0], bct[0], TL[7][0:1, :], [g1_t], D, "s3_bg1")
    b1_t = load_row_tl(cx, 7, I["b_ln1b"], D, "s3_ldb1")
    bcast_row(cx, BC[1], bct[1], TL[7][0:1, :], [b1_t], D, "s3_bb1")
    gb_t = load_row_tl(cx, 7, I["b_gb"], D, "s3_ldgb")
    bcast_row(cx, BC[2], bct[2], TL[7][0:1, :], [gb_t], D, "s3_bgb")
    pb_t = load_row_tl(cx, 7, I["b_pb"], D, "s3_ldpb")
    bcast_row(cx, BC[3], bct[3], TL[7][0:1, :], [pb_t], D, "s3_bpb")
    wc_t = cx.tk["TL"][7]
    em("sync", lambda e: e.dma_start(
        TL[7][0:1, :], I["bWc"].rearrange("a o -> o a")),
        w=[wc_t], dma=True, tag="s3_ldwc")
    wcb_t = B.alias("wcb", cx.tk["OUTS"])
    for j in range(0, D, 512):
        pi = (j // 512) % 2
        em("tensor", lambda e, j=j, pi=pi: e.matmul(
            psB[pi][:, :], cx.onesf[:], TL[7][0:1, j:j + 512],
            start=True, stop=True),
            r=[wc_t, ctk], w=[pbt[pi]], tag=f"s3_bwc{j}")
        em("vector", lambda e, j=j, pi=pi: e.tensor_copy(
            OUTS[:, 1024 + j:1024 + j + 512], psB[pi][:, :]),
            r=[pbt[pi]], w=[wcb_t, pbt[pi]], tag=f"s3_bwce{j}")
    cb_t = cx.tk["TL"][7]
    em("sync", lambda e: e.dma_start(
        TL[7][0:1, 0:1], I["b_cb"].rearrange("(o a) -> o a", o=1)),
        w=[cb_t], dma=True, tag="s3_ldcb")
    em("tensor", lambda e: e.matmul(
        psB[0][:, 0:1], cx.onesf[:], TL[7][0:1, 0:1], start=True, stop=True),
        r=[cb_t, ctk], w=[pbt[0]], tag="s3_bcb")
    em("vector", lambda e: e.tensor_copy(ccol[2][:, 0:1], psB[0][:, 0:1]),
       r=[pbt[0]], w=[cct[2]], tag="s3_bcbe")
    cx.s3["wcb_t"] = wcb_t

    if getattr(phase_s3_p01, "DBG_BC", False):
        for bi in range(4):
            em("gpsimd", lambda e, bi=bi: e.dma_start(
                S["dbgbc"][bi * P:(bi + 1) * P, :], BC[bi][:]),
                r=[bct[bi]], w=(), dma=True, tag=f"dbgbc{bi}")
    stok = ln_stats(cx, S["x2"], 0, NT, "s3n", src_toks=cx.dt["x2"],
                    fold_rms="both")
    if "dbgst" in getattr(build_program, "debug_outputs", ()):
        for k, stn in enumerate(("stC", "stD", "stE", "stF")):
            st_ = getattr(cx, stn)
            em("gpsimd", lambda e, k=k, st_=st_: e.dma_start(
                S["dbgst"][k * P:(k + 1) * P, :], st_[:]),
                r=[stok], w=(), dma=True, tag=f"dbgst{k}")
    v_, u_ = TL[4], TL[5]
    vt_, ut_ = tlt[4], tlt[5]
    sl = slice(0, NT)
    for i in range(NT):
        b = i % 2
        g, gb = i // 4, (i // 4) % 2
        xt, xtt = TL[b], tlt[b]
        tt_, ttt = TL[2 + b], tlt[2 + b]
        xnf, xnft = TL[4 + b], tlt[4 + b]
        em("sync", lambda e, i=i, xt=xt: e.dma_start(
            xt[:], S["x2"][i * P:(i + 1) * P, :]),
            r=[cx.dt["x2"][i]], w=[xtt], dma=True, tag=f"s3a_ld{i}")
        em("vector", lambda e, i=i, xt=xt, tt_=tt_: e.tensor_scalar(
            tt_[:], xt[:], cx.stC[:, i:i + 1], cx.stD[:, i:i + 1],
            op0=ALU.mult, op1=ALU.add),
            r=[xtt, stok], w=[ttt], tag=f"s3a_t{i}")
        em("gpsimd", lambda e, i=i, tt_=tt_: e.dma_start(
            S["tb"][i * P:(i + 1) * P, :], tt_[:]),
            r=[ttt], w=[cx.dt["tb"][i]], dma=True, tag=f"s3a_tst{i}")
        em("vector", lambda e, i=i, xt=xt, xnf=xnf: e.tensor_scalar(
            xnf[:], xt[:], cx.stE[:, i:i + 1], cx.stF[:, i:i + 1],
            op0=ALU.mult, op1=ALU.add),
            r=[xtt, stok], w=[xnft], tag=f"s3a_x0{i}")
        em("vector", lambda e, xnf=xnf: e.tensor_mul(xnf[:], xnf[:], BC[0][:]),
           r=[xnft, bct[0]], w=[xnft], tag=f"s3a_xg{i}")
        em("vector", lambda e, xnf=xnf: e.tensor_add(xnf[:], xnf[:], BC[1][:]),
           r=[xnft, bct[1]], w=[xnft], tag=f"s3a_xb{i}")
        dbg_xn = getattr(phase_s3_p01, "DBG_XN", ())
        if i in dbg_xn:
            k = list(dbg_xn).index(i)
            em("gpsimd", lambda e, k=k, xnf=xnf: e.dma_start(
                S["dbgbc"][k * P:(k + 1) * P, :], xnf[:]),
                r=[xnft], w=(), dma=True, tag=f"dbgxn{i}")
        if not getattr(phase_s3_p01, "SKIP_CLOCK", False):
            em("vector", lambda e, xnf=xnf: e.tensor_mul(
                TL[6][:], xnf[:], OUTS[:, 1024:2048]),
                r=[xnft, wcb_t], w=[tlt[6]], tag=f"s3a_ck{i}")
            em("vector", lambda e, i=i: e.reduce_sum(
                cx.clockc[:, i:i + 1], TL[6][:], axis=AX.X),
                r=[tlt[6]], w=[clkt], tag=f"s3a_ck2{i}")
        else:
            em("gpsimd", lambda e, i=i: e.memset(cx.clockc[:, i:i + 1], 0.5),
               r=(), w=[clkt], tag=f"s3a_ckz{i}")
        em("tensor", lambda e, xnf=xnf: [
            e.transpose(psA[:, 1024 + c * P:1024 + (c + 1) * P],
                        xnf[:, c * P:(c + 1) * P], cx.identf[:])
            for c in range(C)][-1],
            r=[xnft, ctk], w=[pat[2], pat[3]], tag=f"s3a_tr{i}")
        for hh in range(2):
            em("scalar", lambda e, i=i, gb=gb, hh=hh: e.copy(
                bass.AP(cx.AR, A3_FM + gb * 8192 + (4 * hh) * 1024 + (i % 4) * P,
                        [[PITCH, P], [1024, 4], [1, P]]),
                psA[:, (2 + hh) * 512:(3 + hh) * 512]
                .rearrange("p (a b2) -> p a b2", a=4)),
                r=[pat[2 + hh]], w=[fm_t[gb], pat[2 + hh]],
                tag=f"s3a_ev{i}_{hh}")
        if i % 4 == 3:
            _s3_proj_group(cx, g, gb)


def _s3_proj_group(cx, g, gb):
    B, em, S, I = cx.B, cx.B.emit, cx.S, cx.I
    TL, BC, RP = cx.TL, cx.BC, cx.RP
    tlt, bct, rpt = cx.tk["TL"], cx.tk["BC"], cx.tk["RP"]
    psA, psB = cx.psA, cx.psB
    pat, pbt = cx.tk["psA"], cx.tk["psB"]
    ctk = cx.tk["const"]
    OUTS, otk = cx.OUTS, cx.tk["OUTS"]
    ccol, cct = cx.ccol, cx.tk["ccol"]
    d = cx.s3
    fm_t, w_t, ve_t = d["fm_t"], d["w_t"], d["ve_t"]

    def fm_lhs(c, j0, n=P):
        return arp(cx, A3_FM + gb * 8192 + c * 1024 + j0, [[1, n]])

    for ii in range(4):
        i = g * 4 + ii
        b2 = ii % 2
        em("sync", lambda e, i=i, b2=b2: e.dma_start(
            RP[:, b2 * 1024:b2 * 1024 + 512], I["rc"][i * P:(i + 1) * P, :]),
            w=[rpt[b2]], dma=True, tag=f"s3p_rc{i}")
        em("sync", lambda e, i=i, b2=b2: e.dma_start(
            RP[:, b2 * 1024 + 512:b2 * 1024 + 1024],
            I["rs"][i * P:(i + 1) * P, :]),
            w=[rpt[b2]], dma=True, tag=f"s3p_rs{i}")
        for which, wi, dst in (("q", 0, "qa"), ("k", 1, "ka")):
            em("tensor", lambda e, ii=ii, wi=wi: [
                e.matmul(psB[nb][:, :], fm_lhs(c, ii * P),
                         arp(cx, A3_W + wi * 8192 + c * 1024 + nb * 512,
                             [[1, 512]]),
                         start=(c == 0), stop=(c == C - 1))
                for nb in range(2) for c in range(C)][-1],
                r=[w_t[wi], fm_t[gb]], w=[pbt[0], pbt[1]],
                tag=f"s3p_{which}mm{i}")
            em("scalar", lambda e: [
                e.copy(TL[6][:, nb * 512:(nb + 1) * 512], psB[nb][:, :])
                for nb in range(2)][-1],
                r=[pbt[0], pbt[1]], w=[tlt[6], pbt[0], pbt[1]],
                tag=f"s3p_{which}ev{i}")
            em("scalar", lambda e: e.activation(
                TL[7][:], TL[6][:], AF.Square, bias=0.0, scale=1.0,
                accum_out=ccol[0][:, 0:1]),
                r=[tlt[6]], w=[tlt[7], cct[0]], tag=f"s3p_{which}sq{i}")
            em("vector", lambda e: e.tensor_scalar_max(
                ccol[0][:, 0:1], ccol[0][:, 0:1], 1e-24),
                r=[cct[0]], w=[cct[0]], tag=f"s3p_{which}mx{i}")
            em("scalar", lambda e: e.activation(
                ccol[0][:, 1:2], ccol[0][:, 0:1], AF.Ln, bias=0.0, scale=1.0),
                r=[cct[0]], w=[cct[0]], tag=f"s3p_{which}ln{i}")
            em("scalar", lambda e: e.activation(
                ccol[0][:, 1:2], ccol[0][:, 1:2], AF.Exp, bias=0.0, scale=-0.5),
                r=[cct[0]], w=[cct[0]], tag=f"s3p_{which}ex{i}")
            em("vector", lambda e: e.tensor_scalar(
                TL[6][:], TL[6][:], ccol[0][:, 1:2], None, op0=ALU.mult),
                r=[tlt[6], cct[0]], w=[tlt[6]], tag=f"s3p_{which}n{i}")
            em("vector", lambda e, b2=b2: e.tensor_mul(
                TL[5][:, 0:512], TL[6][:, 0:D:2],
                RP[:, b2 * 1024:b2 * 1024 + 512]),
               r=[tlt[6], rpt[b2]], w=[tlt[5]], tag=f"s3p_{which}r1{i}")
            em("vector", lambda e, b2=b2: e.tensor_mul(
                TL[5][:, 512:1024], TL[6][:, 1:D:2],
                RP[:, b2 * 1024 + 512:b2 * 1024 + 1024]),
               r=[tlt[6], rpt[b2]], w=[tlt[5]], tag=f"s3p_{which}r2{i}")
            em("vector", lambda e: e.tensor_sub(
                TL[7][:, 0:D:2], TL[5][:, 0:512], TL[5][:, 512:1024]),
                r=[tlt[5]], w=[tlt[7]], tag=f"s3p_{which}r3{i}")
            em("vector", lambda e, b2=b2: e.tensor_mul(
                TL[5][:, 0:512], TL[6][:, 0:D:2],
                RP[:, b2 * 1024 + 512:b2 * 1024 + 1024]),
               r=[tlt[6], rpt[b2]], w=[tlt[5]], tag=f"s3p_{which}r4{i}")
            em("vector", lambda e, b2=b2: e.tensor_mul(
                TL[5][:, 512:1024], TL[6][:, 1:D:2],
                RP[:, b2 * 1024:b2 * 1024 + 512]),
               r=[tlt[6], rpt[b2]], w=[tlt[5]], tag=f"s3p_{which}r5{i}")
            em("vector", lambda e: e.tensor_add(
                TL[7][:, 1:D:2], TL[5][:, 0:512], TL[5][:, 512:1024]),
                r=[tlt[5]], w=[tlt[7]], tag=f"s3p_{which}r6{i}")
            em("gpsimd", lambda e, i=i, dst=dst: e.dma_start(
                S[dst][i * P:(i + 1) * P, :], TL[7][:]),
                r=[tlt[7]], w=[cx.dt[dst][i]], dma=True,
                tag=f"s3p_{which}st{i}")
        em("tensor", lambda e, ii=ii: [
            e.matmul(pA(cx, js), fm_lhs(c, ii * P),
                     arp(cx, A3_W + 2 * 8192 + c * 1024 + (js - 2) * 512,
                         [[1, 512]]),
                     start=(c == 0), stop=(c == C - 1))
            for js in (2, 3) for c in range(C)][-1],
            r=[w_t[2], fm_t[gb]], w=[pat[2], pat[3]], tag=f"s3p_vmm{i}")
        em("scalar", lambda e, b2=b2: e.copy(
            arp(cx, A3_EV + b2 * 1024, [[1, 1024]]), psA[:, 1024:2048]),
            r=[pat[2], pat[3]], w=[ve_t[b2], pat[2], pat[3]],
            tag=f"s3p_vev{i}")
        em("gpsimd", lambda e, i=i, b2=b2: e.dma_start(
            S["bV"][i * P:(i + 1) * P, :], arp(cx, A3_EV + b2 * 1024,
                                               [[1, 1024]])),
            r=[ve_t[b2]], w=[cx.dt["bV"][i]], dma=True, tag=f"s3p_vst{i}")
        em("tensor", lambda e, ii=ii: [
            e.matmul(psB[nb][:, :], fm_lhs(c, ii * P),
                     arp(cx, A3_W + 3 * 8192 + c * 1024 + nb * 512, [[1, 512]]),
                     start=(c == 0), stop=(c == C - 1))
            for nb in range(2) for c in range(C)][-1],
            r=[w_t[3], fm_t[gb]], w=[pbt[0], pbt[1]], tag=f"s3p_gmm{i}")
        em("vector", lambda e: [
            e.tensor_add(TL[6][:, nb * 512:(nb + 1) * 512],
                         psB[nb][:, :], BC[2][:, nb * 512:(nb + 1) * 512])
            for nb in range(2)][-1],
            r=[pbt[0], pbt[1], bct[2]], w=[tlt[6], pbt[0], pbt[1]],
            tag=f"s3p_gb{i}")
        em("vector", lambda e: e.tensor_scalar_max(TL[7][:], TL[6][:], 0.0),
           r=[tlt[6]], w=[tlt[7]], tag=f"s3p_grl{i}")
        em("scalar", lambda e: e.activation(
            TL[5][:], TL[6][:], AF.Abs, bias=0.0, scale=1.0),
            r=[tlt[6]], w=[tlt[5]], tag=f"s3p_gab{i}")
        em("scalar", lambda e: e.activation(TL[5][:], TL[5][:], AF.Exp,
                                            bias=0.0, scale=-1.0),
           r=[tlt[5]], w=[tlt[5]], tag=f"s3p_gex{i}")
        em("scalar", lambda e: e.activation(TL[5][:], TL[5][:], AF.Ln,
                                            bias=cx.onescol[:], scale=1.0),
           r=[tlt[5], ctk], w=[tlt[5]], tag=f"s3p_gln{i}")
        em("vector", lambda e: e.tensor_add(TL[7][:], TL[7][:], TL[5][:]),
           r=[tlt[5], tlt[7]], w=[tlt[7]], tag=f"s3p_gcm0{i}")
        em("vector", lambda e: e.tensor_scalar_mul(TL[7][:], TL[7][:], -1.0),
           r=[tlt[7]], w=[tlt[7]], tag=f"s3p_gcm{i}")
        em("gpsimd", lambda e, i=i: e.dma_start(
            S["gj"][i * P:(i + 1) * P, :], TL[7][:]),
            r=[tlt[7]], w=[cx.dt["gj"][i]], dma=True, tag=f"s3p_gst{i}")
        em("tensor", lambda e, ii=ii: [
            e.matmul(psB[nb][:, :], fm_lhs(c, ii * P),
                     arp(cx, A3_W + 4 * 8192 + c * 1024 + nb * 512, [[1, 512]]),
                     start=(c == 0), stop=(c == C - 1))
            for nb in range(2) for c in range(C)][-1],
            r=[w_t[4], fm_t[gb]], w=[pbt[0], pbt[1]], tag=f"s3p_pmm{i}")
        em("vector", lambda e: [
            e.tensor_add(TL[6][:, nb * 512:(nb + 1) * 512],
                         psB[nb][:, :], BC[3][:, nb * 512:(nb + 1) * 512])
            for nb in range(2)][-1],
            r=[pbt[0], pbt[1], bct[3]], w=[tlt[6], pbt[0], pbt[1]],
            tag=f"s3p_pb{i}")
        if i == 0:
            em("vector", lambda e: e.tensor_copy(OUTS[:, 0:1024], TL[6][:]),
               r=[tlt[6]], w=[otk], tag=f"s3p_pm0")
        else:
            em("vector", lambda e: e.tensor_max(OUTS[:, 0:1024],
                                                OUTS[:, 0:1024], TL[6][:]),
               r=[tlt[6], otk], w=[otk], tag=f"s3p_pm{i}")
        em("gpsimd", lambda e, i=i: e.dma_start(
            S["pp"][i * P:(i + 1) * P, :], TL[6][:]),
            r=[tlt[6]], w=[cx.dt["pp"][i]], dma=True, tag=f"s3p_pst{i}")


def phase_s3_p2(cx):
    B, em, S = cx.B, cx.B.emit, cx.S
    NT = cx.NT
    TL, BC = cx.TL, cx.BC
    tlt, bct = cx.tk["TL"], cx.tk["BC"]
    psA = cx.psA
    pat = cx.tk["psA"]
    ctk = cx.tk["const"]
    OUTS, otk = cx.OUTS, cx.tk["OUTS"]
    ccol, cct = cx.ccol, cx.tk["ccol"]
    clkt = cx.tk["clock"]

    for c in range(C):
        pb = c % 2
        em("tensor", lambda e, c=c, pb=pb: e.transpose(
            psA[:, pb * 512:pb * 512 + P],
            OUTS[:, c * P:(c + 1) * P], cx.identf[:]),
            r=[otk, ctk], w=[pat[pb]], tag=f"s3x_tr{c}")
        em("vector", lambda e, c=c, pb=pb: e.reduce_max(
            ccol[3][:, c:c + 1], psA[:, pb * 512:pb * 512 + P], axis=AX.X),
            r=[pat[pb]], w=[cct[3], pat[pb]], tag=f"s3x_mx{c}")
        em("gpsimd", lambda e, c=c: e.dma_start(
            S["brow"].rearrange("(a o) -> a o", o=1)[c * P:(c + 1) * P, :],
            ccol[3][:, c:c + 1]),
            r=[cct[3]], w=[cx.dt["brow"][0]], dma=True, tag=f"s3x_st{c}")
    pm_t = cx.tk["TL"][7]
    em("sync", lambda e: e.dma_start(
        TL[7][0:1, :], S["brow"].rearrange("(o a) -> o a", o=1)[:, 0:D]),
        r=[cx.dt["brow"][0]], w=[pm_t], dma=True, tag="s3x_ldr")
    bcast_row(cx, BC[0], bct[0], TL[7][0:1, :], [pm_t], D, "s3x_b")

    sl = slice(0, NT)
    stt_ = cx.tk["stats"]
    em("vector", lambda e: e.tensor_scalar(
        cx.clockc[:, sl], cx.clockc[:, sl], ccol[2][:, 0:1], None,
        op0=ALU.add),
        r=[clkt, cct[2]], w=[clkt], tag="s3c_cb")
    em("vector", lambda e: e.tensor_scalar_max(cx.stA[:, sl],
                                               cx.clockc[:, sl], 0.0),
       r=[clkt], w=[stt_], tag="s3c_rl")
    em("scalar", lambda e: e.activation(
        cx.stB[:, sl], cx.clockc[:, sl], AF.Abs, bias=0.0, scale=1.0),
        r=[clkt], w=[stt_], tag="s3c_ab")
    em("scalar", lambda e: e.activation(cx.stB[:, sl], cx.stB[:, sl], AF.Exp,
                                        bias=0.0, scale=-1.0),
       r=[stt_], w=[stt_], tag="s3c_ex")
    em("scalar", lambda e: e.activation(cx.stB[:, sl], cx.stB[:, sl], AF.Ln,
                                        bias=cx.onescol[:], scale=1.0),
       r=[stt_, ctk], w=[stt_], tag="s3c_ln")
    em("vector", lambda e: e.tensor_add(
        cx.clockc[:, sl], cx.stA[:, sl], cx.stB[:, sl]),
        r=[stt_], w=[clkt], tag="s3c_fin")


def phase_s3_p3(cx):
    B, em, S = cx.B, cx.B.emit, cx.S
    NT = cx.NT
    TL, BC = cx.TL, cx.BC
    tlt, bct = cx.tk["TL"], cx.tk["BC"]
    psA, psB = cx.psA, cx.psB
    pat, pbt = cx.tk["psA"], cx.tk["psB"]
    ctk = cx.tk["const"]
    cart = cx.tk["carry"]
    clkt = cx.tk["clock"]
    d = cx.s3
    qe_t = d["qe_t"]

    em("gpsimd", lambda e: e.memset(cx.carry[:], 0.0), w=[cart], tag="s3q_c0")

    for i in range(NT):
        b = i % 2
        gjt_, gjtt = TL[0], tlt[0]
        ppt_, pptt = TL[1], tlt[1]
        qat_, qatt = TL[2], tlt[2]
        kat_, katt = TL[3], tlt[3]
        em("sync", lambda e, i=i: e.dma_start(
            gjt_[:], S["gj"][i * P:(i + 1) * P, :]),
            r=[cx.dt["gj"][i]], w=[gjtt], dma=True, tag=f"s3q_lg{i}")
        em("sync", lambda e, i=i: e.dma_start(
            ppt_[:], S["pp"][i * P:(i + 1) * P, :]),
            r=[cx.dt["pp"][i]], w=[pptt], dma=True, tag=f"s3q_lp{i}")
        em("sync", lambda e, i=i: e.dma_start(
            qat_[:], S["qa"][i * P:(i + 1) * P, :]),
            r=[cx.dt["qa"][i]], w=[qatt], dma=True, tag=f"s3q_lq{i}")
        em("sync", lambda e, i=i: e.dma_start(
            kat_[:], S["ka"][i * P:(i + 1) * P, :]),
            r=[cx.dt["ka"][i]], w=[katt], dma=True, tag=f"s3q_lk{i}")
        em("vector", lambda e: e.tensor_sub(TL[4][:], ppt_[:], BC[0][:]),
           r=[pptt, bct[0]], w=[tlt[4]], tag=f"s3q_ps{i}")
        em("scalar", lambda e: e.activation(TL[4][:], TL[4][:], AF.Exp,
                                            bias=0.0, scale=1.0),
           r=[tlt[4]], w=[tlt[4]], tag=f"s3q_pe{i}")
        em("vector", lambda e, i=i: e.tensor_scalar(
            TL[4][:], TL[4][:], cx.clockc[:, i:i + 1], None, op0=ALU.mult),
            r=[tlt[4], clkt], w=[tlt[4]], tag=f"s3q_pc{i}")
        em("tensor", lambda e: [
            [e.matmul(psB[nb][:, :], cx.tri_f[:],
                      gjt_[:, nb * 512:(nb + 1) * 512], start=True, stop=False),
             e.matmul(psB[nb][:, :], cx.onesf[:],
                      cx.carry[0:1, nb * 512:(nb + 1) * 512], start=False,
                      stop=True)][-1]
            for nb in range(2)][-1],
            r=[gjtt, cart, ctk], w=[pbt[0], pbt[1]], tag=f"s3q_gc{i}")
        em("tensor", lambda e: [
            e.matmul(psA[0:1, nb * 512:(nb + 1) * 512], cx.onescol[:],
                     gjt_[:, nb * 512:(nb + 1) * 512], start=True, stop=True)
            for nb in range(2)][-1],
            r=[gjtt, ctk], w=[pat[0], pat[1]], tag=f"s3q_gs{i}")
        em("vector", lambda e: e.tensor_add(
            cx.carry[0:1, 0:1024], cx.carry[0:1, 0:1024], psA[0:1, 0:1024]),
            r=[pat[0], pat[1], cart], w=[cart, pat[0], pat[1]],
            tag=f"s3q_gu{i}")
        em("vector", lambda e: [
            e.tensor_scalar(TL[5][:, nb * 512:(nb + 1) * 512], psB[nb][:, :],
                            50.0, -60.0, op0=ALU.min, op1=ALU.max)
            for nb in range(2)][-1],
            r=[pbt[0], pbt[1]], w=[tlt[5], pbt[0], pbt[1]],
            tag=f"s3q_gl{i}")
        em("scalar", lambda e: e.activation(TL[5][:], TL[5][:], AF.Exp,
                                            bias=0.0, scale=1.0),
           r=[tlt[5]], w=[tlt[5]], tag=f"s3q_ge{i}")
        em("tensor", lambda e: [
            [e.matmul(psB[nb][:, :], cx.tri_f[:],
                      TL[4][:, nb * 512:(nb + 1) * 512], start=True,
                      stop=False),
             e.matmul(psB[nb][:, :], cx.onesf[:],
                      cx.carry[0:1, 1024 + nb * 512:1024 + (nb + 1) * 512],
                      start=False, stop=True)][-1]
            for nb in range(2)][-1],
            r=[tlt[4], cart, ctk], w=[pbt[0], pbt[1]], tag=f"s3q_pq{i}")
        em("tensor", lambda e: [
            e.matmul(psA[0:1, nb * 512:(nb + 1) * 512], cx.onescol[:],
                     TL[4][:, nb * 512:(nb + 1) * 512], start=True, stop=True)
            for nb in range(2)][-1],
            r=[tlt[4], ctk], w=[pat[0], pat[1]], tag=f"s3q_ks{i}")
        em("vector", lambda e: e.tensor_add(
            cx.carry[0:1, 1024:2048], cx.carry[0:1, 1024:2048],
            psA[0:1, 0:1024]),
            r=[pat[0], pat[1], cart], w=[cart, pat[0], pat[1]],
            tag=f"s3q_ku{i}")
        em("vector", lambda e: [
            e.tensor_scalar_add(TL[6][:, nb * 512:(nb + 1) * 512],
                                psB[nb][:, :], 1e-8)
            for nb in range(2)][-1],
            r=[pbt[0], pbt[1]], w=[tlt[6], pbt[0], pbt[1]],
            tag=f"s3q_pa{i}")
        em("scalar", lambda e: e.activation(TL[6][:], TL[6][:], AF.Ln,
                                            bias=0.0, scale=1.0),
           r=[tlt[6]], w=[tlt[6]], tag=f"s3q_pr0{i}")
        em("scalar", lambda e: e.activation(TL[6][:], TL[6][:], AF.Exp,
                                            bias=0.0, scale=-1.0),
           r=[tlt[6]], w=[tlt[6]], tag=f"s3q_pr{i}")
        em("vector", lambda e: e.tensor_mul(qat_[:], qat_[:], TL[5][:]),
           r=[qatt, tlt[5]], w=[qatt], tag=f"s3q_q1{i}")
        em("vector", lambda e: e.tensor_mul(qat_[:], qat_[:], TL[6][:]),
           r=[qatt, tlt[6]], w=[qatt], tag=f"s3q_q2{i}")
        em("vector", lambda e: e.tensor_scalar_add(TL[7][:], TL[5][:], 1e-8),
           r=[tlt[5]], w=[tlt[7]], tag=f"s3q_k1{i}")
        em("scalar", lambda e: e.activation(TL[7][:], TL[7][:], AF.Ln,
                                            bias=0.0, scale=1.0),
           r=[tlt[7]], w=[tlt[7]], tag=f"s3q_k20{i}")
        em("scalar", lambda e: e.activation(TL[7][:], TL[7][:], AF.Exp,
                                            bias=0.0, scale=-1.0),
           r=[tlt[7]], w=[tlt[7]], tag=f"s3q_k2{i}")
        em("vector", lambda e: e.tensor_mul(kat_[:], kat_[:], TL[4][:]),
           r=[katt, tlt[4]], w=[katt], tag=f"s3q_k3{i}")
        em("vector", lambda e: e.tensor_mul(kat_[:], kat_[:], TL[7][:]),
           r=[katt, tlt[7]], w=[katt], tag=f"s3q_k4{i}")
        for which, tile, ttok, dst in (("q", qat_, qatt, "bQT"),
                                       ("k", kat_, katt, "bKT")):
            em("tensor", lambda e, tile=tile, ttok=ttok: [
                e.transpose(psA[:, 1024 + c * P:1024 + (c + 1) * P],
                            tile[:, c * P:(c + 1) * P], cx.identf[:])
                for c in range(C)][-1],
                r=[ttok, ctk], w=[pat[2], pat[3]], tag=f"s3q_{which}t{i}")
            em("scalar", lambda e, b=b: e.copy(
                arp(cx, A3_QE + b * 1024, [[1, 1024]]), psA[:, 1024:2048]),
                r=[pat[2], pat[3]], w=[qe_t[b], pat[2], pat[3]],
                tag=f"s3q_{which}e{i}")
            em("gpsimd", lambda e, i=i, b=b, dst=dst: e.dma_start(
                S[dst].rearrange("(c p) t -> p c t", p=P)[:, :,
                                                          i * P:(i + 1) * P],
                arp(cx, A3_QE + b * 1024, [[128, C], [1, P]])),
                r=[qe_t[b]], w=[cx.dt[dst][i]], dma=True,
                tag=f"s3q_{which}s{i}")


def phase_s3_p4(cx):
    B, em, S, I = cx.B, cx.B.emit, cx.S, cx.I
    NG = cx.NG
    TL, BC, RP = cx.TL, cx.BC, cx.RP
    tlt, bct, rpt = cx.tk["TL"], cx.tk["BC"], cx.tk["RP"]
    psA, psB, ptA = cx.psA, cx.psB, cx.ptA
    pat, pbt, ptt = cx.tk["psA"], cx.tk["psB"], cx.tk["ptA"]
    ctk = cx.tk["const"]
    OUTS, otk = cx.OUTS, cx.tk["OUTS"]
    ccol, cct = cx.ccol, cx.tk["ccol"]

    at = arena_phase(cx, ["qt", "kt0", "kt1", "v0", "v1", "pp0", "pp1",
                          "pt0", "pt1", "wo", "ot0", "ot1"])
    qt_t = at["qt"]
    kt_t = [at["kt0"], at["kt1"]]
    v_t = [at["v0"], at["v1"]]
    pp_t = [at["pp0"], at["pp1"]]
    pt_t = [at["pt0"], at["pt1"]]
    wo_t = at["wo"]
    ot_t = [at["ot0"], at["ot1"]]
    otk = B.alias("o4", otk, cx.s3["wcb_t"])
    cx.tk["OUTS"] = otk

    em("gpsimd", lambda e: e.memset(RP[:], 0.0), w=[rpt[0], rpt[1]],
       tag="s4m_0")
    em("gpsimd", lambda e: e.affine_select(
        out=RP.rearrange("p (a j) -> p a j", a=4),
        in_=RP.rearrange("p (a j) -> p a j", a=4),
        compare_op=ALU.is_ge, fill=-1e30,
        base=0, pattern=[[P, 4], [-1, 512]], channel_multiplier=1),
        r=[rpt[0], rpt[1]], w=[rpt[0], rpt[1]], tag="s4m_1")

    em("gpsimd", lambda e: e.dma_start(
        arp(cx, A4_WO, [[1024, C], [1, 1024]]),
        I["bWo"].rearrange("(c p) n -> p c n", p=P)),
        w=[wo_t], dma=True, tag="s4_ldwo")
    ob_t = load_row_tl(cx, 7, I["b_ob"], D, "s4_ldob")
    bcast_row(cx, BC[1], bct[1], TL[7][0:1, :], [ob_t], D, "s4_bob")

    for qt in range(NG):
        em("sync", lambda e, qt=qt: e.dma_start(
            arp(cx, A4_QT, [[512, C], [1, 512]]),
            S["bQT"][:, qt * 512:(qt + 1) * 512]
            .rearrange("(c p) n -> p c n", p=P)),
            r=[cx.dt["bQT"][qt * 4 + k2] for k2 in range(4)],
            w=[qt_t], dma=True, tag=f"s4_q{qt}")
        for jg in range(qt + 1):
            bj = jg % 2
            em("sync", lambda e, jg=jg, bj=bj: e.dma_start(
                arp(cx, A4_KT + bj * 4096, [[512, C], [1, 512]]),
                S["bKT"][:, jg * 512:(jg + 1) * 512]
                .rearrange("(c p) n -> p c n", p=P)),
                r=[cx.dt["bKT"][jg * 4 + k2] for k2 in range(4)],
                w=[kt_t[bj]], dma=True, tag=f"s4_k{qt}_{jg}")
            em("gpsimd", lambda e, jg=jg, bj=bj: e.dma_start(
                arp(cx, A4_V + bj * 4096, [[1024, 4], [1, 1024]]),
                S["bV"].rearrange("(a p) n -> p a n", p=P)
                [:, jg * 4:(jg + 1) * 4, :]),
                r=[cx.dt["bV"][jg * 4 + k2] for k2 in range(4)],
                w=[v_t[bj]], dma=True, tag=f"s4_v{qt}_{jg}")
            for qs in range(4):
                pa = qs % 4
                par = jg % 2
                bb = (qs + jg) % 2
                mslot = qs * 2
                em("tensor", lambda e, qs=qs, bj=bj, pa=pa: [
                    e.matmul(pA(cx, pa),
                             arp(cx, A4_QT + c * 512 + qs * P, [[1, P]]),
                             arp(cx, A4_KT + bj * 4096 + c * 512, [[1, 512]]),
                             start=(c == 0), stop=(c == C - 1))
                    for c in range(C)][-1],
                    r=[qt_t, kt_t[bj]], w=[pat[pa]],
                    tag=f"s4_s{qt}_{jg}_{qs}")
                if jg == qt:
                    em("vector", lambda e, qs=qs, pa=pa: e.tensor_add(
                        pA(cx, pa), pA(cx, pa),
                        RP[:, qs * 512:(qs + 1) * 512]),
                        r=[pat[pa], rpt[0], rpt[1]], w=[pat[pa]],
                        tag=f"s4_m{qt}_{jg}_{qs}")
                em("vector", lambda e, qs=qs, pa=pa: e.reduce_max(
                    ccol[2][:, qs:qs + 1], pA(cx, pa), axis=AX.X),
                    r=[pat[pa]], w=[cct[2]], tag=f"s4_rm{qt}_{jg}_{qs}")
                if jg == 0:
                    em("vector", lambda e, qs=qs, par=par: e.tensor_copy(
                        ccol[0][:, qs * 2 + par:qs * 2 + par + 1],
                        ccol[2][:, qs:qs + 1]),
                        r=[cct[2]], w=[cct[0]], tag=f"s4_m0{qt}_{jg}_{qs}")
                else:
                    em("vector", lambda e, qs=qs, par=par: e.tensor_max(
                        ccol[0][:, qs * 2 + par:qs * 2 + par + 1],
                        ccol[0][:, qs * 2 + (1 - par):qs * 2 + (2 - par)],
                        ccol[2][:, qs:qs + 1]),
                        r=[cct[2], cct[0]], w=[cct[0]],
                        tag=f"s4_mn{qt}_{jg}_{qs}")
                    em("vector", lambda e, qs=qs, par=par: e.tensor_sub(
                        ccol[1][:, 4 + qs:5 + qs],
                        ccol[0][:, qs * 2 + (1 - par):qs * 2 + (2 - par)],
                        ccol[0][:, qs * 2 + par:qs * 2 + par + 1]),
                        r=[cct[0]], w=[cct[1]], tag=f"s4_al{qt}_{jg}_{qs}")
                    em("scalar", lambda e, qs=qs: e.activation(
                        ccol[1][:, 4 + qs:5 + qs], ccol[1][:, 4 + qs:5 + qs],
                        AF.Exp, bias=0.0, scale=SC_B),
                        r=[cct[1]], w=[cct[1]], tag=f"s4_ae{qt}_{jg}_{qs}")
                em("vector", lambda e, qs=qs, par=par: e.tensor_scalar_mul(
                    ccol[3][:, qs:qs + 1],
                    ccol[0][:, qs * 2 + par:qs * 2 + par + 1], -SC_B),
                    r=[cct[0]], w=[cct[3]], tag=f"s4_nb{qt}_{jg}_{qs}")
                em("scalar", lambda e, qs=qs, pa=pa, bb=bb: e.activation(
                    arp(cx, A4_P + bb * 512, [[1, 512]]), pA(cx, pa), AF.Exp,
                    bias=ccol[3][:, qs:qs + 1], scale=SC_B,
                    accum_out=ccol[2][:, 4 + qs:5 + qs]),
                    r=[pat[pa], cct[3]], w=[pp_t[bb], cct[2], pat[pa]],
                    tag=f"s4_e{qt}_{jg}_{qs}")
                if jg == 0:
                    em("vector", lambda e, qs=qs: e.tensor_copy(
                        ccol[1][:, qs:qs + 1], ccol[2][:, 4 + qs:5 + qs]),
                        r=[cct[2]], w=[cct[1]], tag=f"s4_l0{qt}_{jg}_{qs}")
                else:
                    em("vector", lambda e, qs=qs: e.tensor_scalar(
                        ccol[1][:, qs:qs + 1], ccol[1][:, qs:qs + 1],
                        ccol[1][:, 4 + qs:5 + qs], ccol[2][:, 4 + qs:5 + qs],
                        op0=ALU.mult, op1=ALU.add),
                        r=[cct[2], cct[1]], w=[cct[1]],
                        tag=f"s4_lu{qt}_{jg}_{qs}")
                em("tensor", lambda e, bb=bb: [
                    e.transpose(ptA[bb][:, s * P:(s + 1) * P],
                                arp(cx, A4_P + bb * 512 + s * P, [[1, P]]),
                                cx.ident[:])
                    for s in range(4)][-1],
                    r=[pp_t[bb], ctk], w=[ptt[bb]], tag=f"s4_t{qt}_{jg}_{qs}")
                em("vector", lambda e, bb=bb: e.tensor_copy(
                    arp(cx, A4_PT + bb * 512, [[1, 512]]), ptA[bb][:, 0:512]),
                    r=[ptt[bb]], w=[pt_t[bb], ptt[bb]],
                    tag=f"s4_te{qt}_{jg}_{qs}")
                em("tensor", lambda e, bb=bb, bj=bj: [
                    e.matmul(psB[nb][:, :],
                             arp(cx, A4_PT + bb * 512 + s * P, [[1, P]]),
                             arp(cx, A4_V + bj * 4096 + s * 1024 + nb * 512,
                                 [[1, 512]]),
                             start=(s == 0), stop=(s == 3))
                    for nb in range(2) for s in range(4)][-1],
                    r=[pt_t[bb], v_t[bj]], w=[pbt[0], pbt[1]],
                    tag=f"s4_av{qt}_{jg}_{qs}")
                if jg == 0:
                    em("vector", lambda e, qs=qs: [
                        e.tensor_copy(OUTS[:, qs * 1024 + nb * 512:
                                           qs * 1024 + (nb + 1) * 512],
                                      psB[nb][:, :])
                        for nb in range(2)][-1],
                        r=[pbt[0], pbt[1]], w=[otk, pbt[0], pbt[1]],
                        tag=f"s4_o0{qt}_{jg}_{qs}")
                else:
                    em("vector", lambda e, qs=qs: e.tensor_scalar(
                        OUTS[:, qs * 1024:(qs + 1) * 1024],
                        OUTS[:, qs * 1024:(qs + 1) * 1024],
                        ccol[1][:, 4 + qs:5 + qs], None, op0=ALU.mult),
                        r=[cct[1], otk], w=[otk],
                        tag=f"s4_os{qt}_{jg}_{qs}")
                    em("vector", lambda e, qs=qs: [
                        e.tensor_add(
                            OUTS[:, qs * 1024 + nb * 512:
                                 qs * 1024 + (nb + 1) * 512],
                            OUTS[:, qs * 1024 + nb * 512:
                                 qs * 1024 + (nb + 1) * 512],
                            psB[nb][:, :])
                        for nb in range(2)][-1],
                        r=[pbt[0], pbt[1], otk],
                        w=[otk, pbt[0], pbt[1]],
                        tag=f"s4_ou{qt}_{jg}_{qs}")
        for qs in range(4):
            i = qt * 4 + qs
            bo = qs % 2
            em("scalar", lambda e, qs=qs: e.activation(
                ccol[2][:, qs:qs + 1], ccol[1][:, qs:qs + 1],
                AF.Ln, bias=0.0, scale=1.0),
                r=[cct[1]], w=[cct[2]], tag=f"s4e_r{qt}_{qs}")
            em("scalar", lambda e, qs=qs: e.activation(
                ccol[2][:, qs:qs + 1], ccol[2][:, qs:qs + 1],
                AF.Exp, bias=0.0, scale=-1.0),
                r=[cct[2]], w=[cct[2]], tag=f"s4e_r2{qt}_{qs}")
            em("vector", lambda e, qs=qs: e.tensor_scalar(
                TL[4][:], OUTS[:, qs * 1024:(qs + 1) * 1024],
                ccol[2][:, qs:qs + 1], None, op0=ALU.mult),
                r=[otk, cct[2]], w=[tlt[4]], tag=f"s4e_s{qt}_{qs}")
            em("tensor", lambda e: [
                e.transpose(psA[:, 1024 + c * P:1024 + (c + 1) * P],
                            TL[4][:, c * P:(c + 1) * P], cx.identf[:])
                for c in range(C)][-1],
                r=[tlt[4], ctk], w=[pat[2], pat[3]], tag=f"s4e_t{qt}_{qs}")
            em("scalar", lambda e, bo=bo: e.copy(
                arp(cx, A4_OT + bo * 1024, [[1, 1024]]), psA[:, 1024:2048]),
                r=[pat[2], pat[3]], w=[ot_t[bo], pat[2], pat[3]],
                tag=f"s4e_e{qt}_{qs}")
            em("tensor", lambda e, bo=bo: [
                e.matmul(psB[nb][:, :],
                         arp(cx, A4_OT + bo * 1024 + c * P, [[1, P]]),
                         arp(cx, A4_WO + c * 1024 + nb * 512, [[1, 512]]),
                         start=(c == 0), stop=(c == C - 1))
                for nb in range(2) for c in range(C)][-1],
                r=[ot_t[bo], wo_t], w=[pbt[0], pbt[1]],
                tag=f"s4e_mm{qt}_{qs}")
            em("sync", lambda e, i=i: e.dma_start(
                TL[5][:], S["tb"][i * P:(i + 1) * P, :]),
                r=[cx.dt["tb"][i]], w=[tlt[5]], dma=True,
                tag=f"s4e_lt{qt}_{qs}")
            em("vector", lambda e: [
                e.tensor_add(TL[6][:, nb * 512:(nb + 1) * 512],
                             psB[nb][:, :], BC[1][:, nb * 512:(nb + 1) * 512])
                for nb in range(2)][-1],
                r=[pbt[0], pbt[1], bct[1]], w=[tlt[6], pbt[0], pbt[1]],
                tag=f"s4e_b{qt}_{qs}")
            em("vector", lambda e: e.tensor_add(TL[6][:], TL[6][:], TL[5][:]),
               r=[tlt[5], tlt[6]], w=[tlt[6]], tag=f"s4e_a{qt}_{qs}")
            em("gpsimd", lambda e, i=i: e.dma_start(
                S["x2i"][i * P:(i + 1) * P, :], TL[6][:]),
                r=[tlt[6]], w=[cx.dt["x2i"][i]], dma=True,
                tag=f"s4e_st{qt}_{qs}")


def phase_s3_p5(cx):
    B, em, S, I = cx.B, cx.B.emit, cx.S, cx.I
    NT, NG = cx.NT, cx.NG
    TL, BC = cx.TL, cx.BC
    tlt, bct = cx.tk["TL"], cx.tk["BC"]
    psA, psB = cx.psA, cx.psB
    pat, pbt = cx.tk["psA"], cx.tk["psB"]
    ctk = cx.tk["const"]
    ccol, cct = cx.ccol, cx.tk["ccol"]

    at = arena_phase(cx, ["fm0", "fm1", "hg", "w10", "w11", "w20", "w21"])
    fm_t = [at["fm0"], at["fm1"]]
    hg_t = at["hg"]
    w1_t = [at["w10"], at["w11"]]
    w2_t = [at["w20"], at["w21"]]

    g2_t = load_row_tl(cx, 7, I["b_ln2g"], D, "s5_ldg2")
    bcast_row(cx, BC[2], bct[2], TL[7][0:1, :], [g2_t], D, "s5_bg2")
    bb2_t = load_row_tl(cx, 7, I["b_ln2b"], D, "s5_ldb2l")
    bcast_row(cx, BC[3], bct[3], TL[7][0:1, :], [bb2_t], D, "s5_bb2l")
    b2r_t = load_row_tl(cx, 7, I["bB2"], D, "s5_ldb2")
    bcast_row(cx, BC[0], bct[0], TL[7][0:1, :], [b2r_t], D, "s5_bb2")
    b1_t = Tok("b1pp")
    em("sync", lambda e: e.dma_start(
        TL[7][0:32, 0:P], I["bB1"].rearrange("(a p) -> a p", p=P)),
        w=[cx.tk["TL"][7]], dma=True, tag="s5_ldb1")
    em("tensor", lambda e: e.transpose(
        psA[:, 0:32], TL[7][0:32, 0:P], cx.identf[0:32, 0:32]),
        r=[cx.tk["TL"][7], ctk], w=[pat[0]], tag="s5_trb1")
    em("vector", lambda e: e.tensor_copy(cx.clockc[:, 32:64], psA[:, 0:32]),
       r=[pat[0]], w=[b1_t, pat[0]], tag="s5_evb1")

    stok = ln_stats(cx, S["x2i"], 0, NT, "s5n", src_toks=cx.dt["x2i"])

    for i in range(NT):
        b = i % 2
        g, gb = i // 4, (i // 4) % 2
        xt, xtt = TL[b], tlt[b]
        hf, hft = TL[2 + b], tlt[2 + b]
        em("sync", lambda e, i=i, xt=xt: e.dma_start(
            xt[:], S["x2i"][i * P:(i + 1) * P, :]),
            r=[cx.dt["x2i"][i]], w=[xtt], dma=True, tag=f"s5a_ld{i}")
        em("vector", lambda e, i=i, xt=xt, hf=hf: e.tensor_scalar(
            hf[:], xt[:], cx.stC[:, i:i + 1], cx.stD[:, i:i + 1],
            op0=ALU.mult, op1=ALU.add),
            r=[xtt, stok], w=[hft], tag=f"s5a_sc{i}")
        em("vector", lambda e, hf=hf: e.tensor_mul(hf[:], hf[:], BC[2][:]),
           r=[hft, bct[2]], w=[hft], tag=f"s5a_g{i}")
        em("vector", lambda e, hf=hf: e.tensor_add(hf[:], hf[:], BC[3][:]),
           r=[hft, bct[3]], w=[hft], tag=f"s5a_b{i}")
        em("tensor", lambda e, hf=hf, b=b: [
            e.transpose(psA[:, (2 * b) * 512 + c * P:(2 * b) * 512 + (c + 1) * P],
                        hf[:, c * P:(c + 1) * P], cx.identf[:])
            for c in range(C)][-1],
            r=[hft, ctk], w=[pat[2 * b], pat[2 * b + 1]], tag=f"s5a_tr{i}")
        for hh in range(2):
            em("scalar", lambda e, i=i, gb=gb, b=b, hh=hh: e.copy(
                bass.AP(cx.AR, A5_FM + gb * 8192 + (4 * hh) * 1024 + (i % 4) * P,
                        [[PITCH, P], [1024, 4], [1, P]]),
                psA[:, (2 * b + hh) * 512:(2 * b + hh + 1) * 512]
                .rearrange("p (a b2) -> p a b2", a=4)),
                r=[pat[2 * b + hh]], w=[fm_t[gb], pat[2 * b + hh]],
                tag=f"s5a_ev{i}_{hh}")
        if i % 4 == 3:
            _s5_mlp_group(cx, g, gb, fm_t, hg_t, w1_t, w2_t, b1_t, stok)


def _s5_mlp_group(cx, g, gb, fm_t, hg_t, w1_t, w2_t, b1_t, stok):
    B, em, S, I = cx.B, cx.B.emit, cx.S, cx.I
    TL, BC = cx.TL, cx.BC
    tlt, bct = cx.tk["TL"], cx.tk["BC"]
    psA, psB = cx.psA, cx.psB
    pat, pbt = cx.tk["psA"], cx.tk["psB"]

    def fm_rhs(c):
        return arp(cx, A5_FM + gb * 8192 + c * 1024, [[1, 512]])

    for fg in range(8):
        wb = fg % 2
        em("gpsimd", lambda e, fg=fg, wb=wb: e.dma_start(
            arp(cx, A5_W1 + wb * 4096, [[512, C], [1, 512]]),
            I["bW1"].rearrange("(c p) n -> p c n", p=P)
            [:, :, fg * 512:(fg + 1) * 512]),
            w=[w1_t[wb]], dma=True, tag=f"s5w1_{g}_{fg}")
        for fc4 in range(4):
            fc = fg * 4 + fc4
            pa = fc4 % 4
            em("tensor", lambda e, fc4=fc4, wb=wb, pa=pa: [
                e.matmul(pA(cx, pa),
                         arp(cx, A5_W1 + wb * 4096 + c * 512 + fc4 * P,
                             [[1, P]]),
                         fm_rhs(c), start=(c == 0), stop=(c == C - 1))
                for c in range(C)][-1],
                r=[w1_t[wb], fm_t[gb]], w=[pat[pa]], tag=f"s5m1_{g}_{fc}")
            em("scalar", lambda e, fc=fc, pa=pa: e.activation(
                arp(cx, A5_HG + fc * 512, [[1, 512]]), pA(cx, pa), AF.Gelu,
                bias=cx.clockc[:, 32 + fc:33 + fc], scale=1.0),
                r=[pat[pa], b1_t], w=[hg_t, pat[pa]], tag=f"s5ge_{g}_{fc}")
    for tsp in range(2):
        for fg in range(8):
            wb = fg % 2
            em("gpsimd", lambda e, fg=fg, wb=wb: e.dma_start(
                arp(cx, A5_W2 + wb * 4096, [[1024, 4], [1, 1024]]),
                I["bW2"].rearrange("(a p) n -> p a n", p=P)
                [:, fg * 4:(fg + 1) * 4, :]),
                w=[w2_t[wb]], dma=True, tag=f"s5w2_{g}_{tsp}_{fg}")
            for fc4 in range(4):
                fc = fg * 4 + fc4
                em("tensor", lambda e, tsp=tsp, fc=fc, fc4=fc4, wb=wb: [
                    e.matmul(pA(cx, 2 * tsl + nb),
                             arp(cx, A5_HG + fc * 512 + (tsp * 2 + tsl) * P,
                                 [[1, P]]),
                             arp(cx, A5_W2 + wb * 4096 + fc4 * 1024 + nb * 512,
                                 [[1, 512]]),
                             start=(fc == 0), stop=(fc == 31))
                    for tsl in range(2) for nb in range(2)][-1],
                    r=[w2_t[wb], hg_t],
                    w=[pat[0], pat[1], pat[2], pat[3]],
                    tag=f"s5m2_{g}_{tsp}_{fc}")
        for tsl in range(2):
            ts = tsp * 2 + tsl
            i = g * 4 + ts
            bx = tsl % 2
            em("sync", lambda e, i=i, bx=bx: e.dma_start(
                TL[4 + bx][:], S["x2i"][i * P:(i + 1) * P, :]),
                r=[cx.dt["x2i"][i]], w=[tlt[4 + bx]], dma=True,
                tag=f"s5f_li{g}_{ts}")
            em("sync", lambda e, i=i, bx=bx: e.dma_start(
                TL[6 + bx][:], S["x2"][i * P:(i + 1) * P, :]),
                r=[cx.dt["x2"][i]], w=[tlt[6 + bx]], dma=True,
                tag=f"s5f_lx{g}_{ts}")
            em("vector", lambda e, tsl=tsl, bx=bx: [
                e.tensor_add(
                    TL[4 + bx][:, nb * 512:(nb + 1) * 512],
                    TL[4 + bx][:, nb * 512:(nb + 1) * 512],
                    pA(cx, 2 * tsl + nb))
                for nb in range(2)][-1],
                r=[pat[2 * tsl], pat[2 * tsl + 1], tlt[4 + bx]],
                w=[tlt[4 + bx], pat[2 * tsl], pat[2 * tsl + 1]],
                tag=f"s5f_b{g}_{ts}")
            em("vector", lambda e, bx=bx: e.tensor_add(
                TL[4 + bx][:], TL[4 + bx][:], BC[0][:]),
                r=[tlt[4 + bx], bct[0]], w=[tlt[4 + bx]],
                tag=f"s5f_b2{g}_{ts}")
            em("vector", lambda e, bx=bx: e.tensor_add(
                TL[4 + bx][:], TL[4 + bx][:], TL[6 + bx][:]),
                r=[tlt[4 + bx], tlt[6 + bx]], w=[tlt[4 + bx]],
                tag=f"s5f_a{g}_{ts}")
            em("gpsimd", lambda e, i=i, bx=bx: e.dma_start(
                S["x3"][i * P:(i + 1) * P, :], TL[4 + bx][:]),
                r=[tlt[4 + bx]], w=[cx.dt["x3"][i]], dma=True,
                tag=f"s5f_st{g}_{ts}")



def finalize_output(cx, src_name):
    B, em, S = cx.B, cx.B.emit, cx.S
    TL, tlt = cx.TL, cx.tk["TL"]
    for i in range(cx.NT):
        b = i % 2
        em("sync", lambda e, i=i, b=b: e.dma_start(
            TL[4 + b][:], S[src_name][i * P:(i + 1) * P, :]),
            r=[cx.dt[src_name][i]], w=[tlt[4 + b]], dma=True,
            tag=f"fin_ld{i}")
        em("gpsimd", lambda e, i=i, b=b: e.dma_start(
            cx.y[i * P:(i + 1) * P, :], TL[4 + b][:]),
            r=[tlt[4 + b]], w=[cx.dt["y"][i]], dma=True, tag=f"fin_st{i}")


def build_full(T=4096, upto=4):
    nc, cx = build_program(T)
    phase_s1(cx)
    last = "x1"
    if upto >= 2:
        transpose_ffn_weights(cx)
        phase_ffn(cx, "x1", "x2", 1)
        last = "x2"
    if upto == 25:
        phase_s3_p01(cx)
    if upto >= 3 and upto != 25:
        phase_s3_p01(cx)
        phase_s3_p2(cx)
        phase_s3_p3(cx)
        phase_s3_p4(cx)
        phase_s3_p5(cx)
        last = "x3"
    if upto >= 4 and upto != 25:
        phase_ffn(cx, "x3", "y", 2)
        last = None
    if last is not None:
        finalize_output(cx, last)
    stats = cx.B.finalize(nc)
    for h in reversed(cx._mgr):
        h.__exit__(None, None, None)
    return nc, stats




def _np_inputs_for_core(inputs, b, T=4096):
    rope_half = 512
    freqs = np.exp(-np.arange(rope_half, dtype=np.float32)
                   * (np.log(10000.0) / rope_half)).astype(np.float32)
    th = (np.arange(T, dtype=np.float32)[:, None] * freqs[None, :]).astype(np.float32)
    rc = np.cos(th).astype(np.float32)
    rs = np.sin(th).astype(np.float32)
    d = {
        "x": np.ascontiguousarray(inputs["L0_attn_a_tokens"][b]),
        "aWq": inputs["L0_attn_a_Wq"], "aWkv": inputs["L0_attn_a_Wkv"],
        "aWout": inputs["L0_attn_a_Wout"], "aWg": inputs["L0_attn_a_Wg"],
        "a_normw": inputs["L0_attn_a_norm_w"],
        "fWr": inputs["L0_ffn_W_r"], "fWk": inputs["L0_ffn_W_k"],
        "fWv": inputs["L0_ffn_W_v"],
        "fmur": inputs["L0_ffn_mu_r"], "fmuk": inputs["L0_ffn_mu_k"],
        "bWq": inputs["L0_attn_b_Wq"], "bWk": inputs["L0_attn_b_Wk"],
        "bWv": inputs["L0_attn_b_Wv"], "bWg": inputs["L0_attn_b_gate_W"],
        "bWp": inputs["L0_attn_b_p_W"], "bWc": inputs["L0_attn_b_clock_W"],
        "bWo": inputs["L0_attn_b_out_W"],
        "b_gb": inputs["L0_attn_b_gate_b"], "b_pb": inputs["L0_attn_b_p_b"],
        "b_cb": inputs["L0_attn_b_clock_b"], "b_ob": inputs["L0_attn_b_out_b"],
        "b_ln1g": inputs["L0_attn_b_ln1_g"], "b_ln1b": inputs["L0_attn_b_ln1_b"],
        "b_ln2g": inputs["L0_attn_b_ln2_g"], "b_ln2b": inputs["L0_attn_b_ln2_b"],
        "bW1": inputs["L0_attn_b_mlp_W1"], "bB1": inputs["L0_attn_b_mlp_b1"],
        "bW2": inputs["L0_attn_b_mlp_W2"], "bB2": inputs["L0_attn_b_mlp_b2"],
        "rc": rc, "rs": rs,
    }
    return {k: np.ascontiguousarray(np.asarray(v, dtype=np.float32))
            for k, v in d.items()}


def kernel(**inputs):
    T = 4096
    nc, _ = build_full(T)
    in_maps = [_np_inputs_for_core(inputs, b, T) for b in range(NCORES)]
    res = run_bass_kernel_spmd(nc, in_maps, core_ids=list(range(NCORES)))
    out = np.stack([res.results[b]["y"] for b in range(NCORES)], axis=0)
    return out.astype(np.float32)


if __name__ == "__main__":
    nc, stats = build_full(512)
    print(stats)


# version 14
# speedup vs baseline: 1.0421x; 1.0421x over previous
import contextlib
import numpy as np
import concourse.bass as bass
import concourse.mybir as mybir
from concourse.bass_utils import run_bass_kernel_spmd

F32 = mybir.dt.float32
BF16 = mybir.dt.bfloat16
AF = mybir.ActivationFunctionType
ALU = mybir.AluOpType
AX = mybir.AxisListType

D = 1024
NCORES = 8
C = 8
P = 128
FF = 4096
NH = 16
DH = 64
LN_EPS = 1e-5
RMS_EPS = 1.1920929e-07
SC_A = 0.125
SC_B = 0.03125

ENGS = ("sync", "scalar", "vector", "tensor", "gpsimd")



class Tok:
    __slots__ = ("writer", "readers", "name")

    def __init__(self, name=""):
        self.writer = None
        self.readers = []
        self.name = name


def toks(n, name=""):
    return [Tok(f"{name}[{i}]") for i in range(n)]


class Builder:
    def __init__(self):
        self.ops = []

    def emit(self, eng, fn, r=(), w=(), dma=False, tag=""):
        opid = len(self.ops)
        deps = set()
        for t in r:
            if t.writer is not None:
                deps.add(t.writer)
        for t in w:
            if t.writer is not None:
                deps.add(t.writer)
            deps.update(t.readers)
        for t in r:
            t.readers.append(opid)
        for t in w:
            t.writer = opid
            t.readers = []
        self.ops.append(dict(eng=eng, fn=fn, deps=deps, dma=dma, tag=tag))
        return opid

    def alias(self, name, *olds):
        t = Tok(name)
        rd = []
        for o in olds:
            rd.extend(o.readers)
            if o.writer is not None:
                rd.append(o.writer)
        t.readers = rd
        return t

    def finalize(self, nc):
        ops = self.ops
        n = len(ops)
        if getattr(Builder, "SERIAL", False):
            for i in range(1, n):
                ops[i]["deps"].add(i - 1)
        need_inc = [False] * n
        for op in ops:
            for d in op["deps"]:
                if ops[d]["eng"] != op["eng"] and not ops[d]["dma"]:
                    need_inc[d] = True
        reach = [0] * n
        for i, op in enumerate(ops):
            m = 0
            for d in op["deps"]:
                m |= reach[d] | (1 << d)
            reach[i] = m
        chains = {}
        cnt = {}
        val = [None] * n
        key = [None] * n
        for i, op in enumerate(ops):
            if op["dma"]:
                e = op["eng"]
                cl = chains.setdefault(e, [])
                for c in cl:
                    if (reach[i] >> c["last"]) & 1:
                        c["last"] = i
                        c["count"] += 1
                        val[i], key[i] = 16 * c["count"], c["key"]
                        cnt[c["key"]] = val[i]
                        break
                else:
                    k = ("d", e, len(cl))
                    cl.append(dict(last=i, count=1, key=k))
                    val[i], key[i] = 16, k
                    cnt[k] = 16
            elif need_inc[i]:
                k = ("c", op["eng"])
                cnt[k] = cnt.get(k, 0) + 1
                val[i], key[i] = cnt[k], k
        snap = [None] * n
        vc = {e: {} for e in ENGS}
        waits = [None] * n
        elide = getattr(Builder, "ELIDE", True)
        for i, op in enumerate(ops):
            e = op["eng"]
            myvc = vc[e]
            wl = {}
            for d in op["deps"]:
                dop = ops[d]
                if dop["eng"] == e and not dop["dma"]:
                    continue
                k = key[d]
                if val[d] is None:
                    raise RuntimeError(f"dep op {d} has no sem")
                if not elide or myvc.get(k, 0) < val[d]:
                    wl[k] = max(wl.get(k, 0), val[d])
            for k, v in wl.items():
                myvc[k] = max(myvc.get(k, 0), v)
            for d in op["deps"]:
                if val[d] is not None and snap[d] is not None \
                        and myvc.get(key[d], 0) >= val[d]:
                    for kk, vv in snap[d].items():
                        if myvc.get(kk, 0) < vv:
                            myvc[kk] = vv
            waits[i] = sorted(wl.items())
            if val[i] is not None:
                s = dict(myvc)
                s[key[i]] = val[i]
                snap[i] = s
                myvc[key[i]] = val[i]
        final_waits = [(k, v) for k, v in sorted(cnt.items()) if k[0] == "d"]
        nsems = len(cnt)
        per_eng = {e: [] for e in ENGS}
        for i, op in enumerate(ops):
            per_eng[op["eng"]].append(i)
        with contextlib.ExitStack() as stack:
            sems = {}
            for k in sorted(cnt):
                sems[k] = stack.enter_context(
                    nc.semaphore("_".join(str(p) for p in k)))
            with nc.Block() as block:
                def mk(se):
                    def section(engine):
                        for i in per_eng[se]:
                            op = ops[i]
                            for k, v in waits[i]:
                                engine.wait_ge(sems[k], v)
                            inst = op["fn"](engine)
                            if val[i] is not None:
                                if inst is None:
                                    raise RuntimeError(
                                        f"op {i} ({op['tag']}) needs inc, no inst")
                                if op["dma"] or se == "tensor" \
                                        or getattr(Builder, "DIRECT_INC",
                                                   False):
                                    inst.then_inc(sems[key[i]],
                                                  16 if op["dma"] else 1)
                                else:
                                    engine.maybe_drain_then_inc(
                                        (sems[key[i]], 1), fusable=False)
                        if se == "sync":
                            for k, v in final_waits:
                                engine.wait_ge(sems[k], v)
                    return section
                block.sync(mk("sync"))
                block.scalar(mk("scalar"))
                block.vector(mk("vector"))
                block.tensor(mk("tensor"))
                block.gpsimd(mk("gpsimd"))
        return dict(n_ops=n,
                    n_waits=sum(len(w) for w in waits if w),
                    n_incs=sum(1 for v in val if v is not None),
                    n_sems=nsems,
                    dma_chains={e: len(cl) for e, cl in chains.items()})


Builder.ELIDE = False


class Ctx:
    pass



def build_program(T=4096):
    NT = T // P
    NG = T // 512
    nc = bass.Bass("TRN2", target_bir_lowering=False,
                   detect_race_conditions=False)
    B = Builder()
    cx = Ctx()
    cx.nc, cx.B, cx.T, cx.NT, cx.NG = nc, B, T, NT, NG
    em = B.emit

    def dram(name, shape, dtype=F32, kind="Internal"):
        return nc.dram_tensor(name, list(shape), dtype, kind=kind).ap()

    I = {}
    for name, shape in [
            ("x", (T, D)),
            ("aWq", (D, D)), ("aWkv", (D, 2 * D)), ("aWout", (D, D)),
            ("aWg", (D, NH)), ("a_normw", (D,)),
            ("fWr", (D, D)), ("fWk", (D, D)), ("fWv", (D, D)),
            ("fmur", (D,)), ("fmuk", (D,)),
            ("bWq", (D, D)), ("bWk", (D, D)), ("bWv", (D, D)),
            ("bWg", (D, D)), ("bWp", (D, D)), ("bWc", (D, 1)), ("bWo", (D, D)),
            ("b_gb", (D,)), ("b_pb", (D,)), ("b_cb", (1,)), ("b_ob", (D,)),
            ("b_ln1g", (D,)), ("b_ln1b", (D,)),
            ("b_ln2g", (D,)), ("b_ln2b", (D,)),
            ("bW1", (D, FF)), ("bB1", (FF,)), ("bW2", (FF, D)), ("bB2", (D,)),
            ("rc", (T, 512)), ("rs", (T, 512))]:
        I[name] = dram(name, shape, F32, kind="ExternalInput")
    cx.I = I
    cx.y = dram("y", (T, D), F32, kind="ExternalOutput")

    S = {}
    dbg = getattr(build_program, "debug_outputs", ())
    for name, shape, dt in [
            ("x1", (T, D), F32), ("x2", (T, D), F32), ("x3", (T, D), F32),
            ("aKT", (D, T), BF16), ("aQT", (D, T), BF16),
            ("aV", (T, NH * 65), BF16), ("gT", (NH, T), F32),
            ("t_ffn", (T, D), F32),
            ("wrT", (D, D), BF16), ("wkT", (D, D), BF16), ("wvT", (D, D), BF16),
            ("w1b", (D, FF), BF16), ("w2b", (FF, D), BF16),
            ("tb", (T, D), F32),
            ("qa", (T, D), F32), ("ka", (T, D), F32),
            ("gj", (T, D), F32), ("pp", (T, D), F32),
            ("bQT", (D, T), BF16), ("bKT", (D, T), BF16), ("bV", (T, D), BF16),
            ("x2i", (T, D), F32),
            ("brow", (4096,), F32), ("dbgst", (4 * P, 64), F32),
            ("dbgbc", (4 * P, D), F32)]:
        S[name] = dram(name, shape, dt,
                       kind="ExternalOutput" if name in dbg else "Internal")
    cx.S = S
    cx.dt = {
        "x1": toks(NT, "dx1"), "x2": toks(NT, "dx2"), "x3": toks(NT, "dx3"),
        "t_ffn": toks(NT, "dtf"), "tb": toks(NT, "dtb"),
        "qa": toks(NT, "dqa"), "ka": toks(NT, "dka"),
        "gj": toks(NT, "dgj"), "pp": toks(NT, "dpp"),
        "x2i": toks(NT, "dx2i"), "bV": toks(NT, "dbV"),
        "bQT": toks(NT, "dbQT"), "bKT": toks(NT, "dbKT"),
        "aKT": toks(NG, "daKT"), "aQT": toks(NG, "daQT"),
        "aV": toks(NG, "daV"), "gT": toks(NG, "dgT"),
        "wrT": [Tok("dwrT")], "wkT": [Tok("dwkT")], "wvT": [Tok("dwvT")],
        "w1b": toks(8, "dw1b"), "w2b": toks(8, "dw2b"),
        "brow": [Tok("dbrow")],
        "y": toks(NT, "dy"),
    }
    cx.artoks = []

    mgr = []

    def sbuf(name, shape, dtype):
        h = nc.sbuf_tensor(name, list(shape), dtype)
        t = h.__enter__()
        mgr.append(h)
        return t
    cx._mgr = mgr

    if getattr(build_program, "STATS_FIRST", False):
        stA = sbuf("stA", (P, 64), F32)
        stB = sbuf("stB", (P, 64), F32)
        stC = sbuf("stC", (P, 64), F32)
        stD = sbuf("stD", (P, 64), F32)
        stE = sbuf("stE", (P, 64), F32)
        stF = sbuf("stF", (P, 64), F32)
        clockc = sbuf("clockc", (P, 64), F32)
        ccol = [sbuf(f"ccol{i}", (P, 8), F32) for i in range(4)]
        carry = sbuf("carry", (1, 2 * D), F32)
    AR = sbuf("AR", (P, 60 * 1024), BF16)
    TL = [sbuf(f"TL{i}", (P, D), F32) for i in range(8)]
    BC = [sbuf(f"BC{i}", (P, D), F32) for i in range(4)]
    OUTS = sbuf("OUTS", (P, 4096), F32)
    RP = sbuf("RP", (P, 2048), F32)
    identf = sbuf("identf", (P, P), F32)
    ident = sbuf("ident", (P, P), BF16)
    tri_f = sbuf("tri_f", (P, P), F32)
    onesb = sbuf("onesb", (1, P), BF16)
    onesf = sbuf("onesf", (1, P), F32)
    onescol = sbuf("onescol", (P, 1), F32)
    if not getattr(build_program, "STATS_FIRST", False):
        stA = sbuf("stA", (P, 64), F32)
        stB = sbuf("stB", (P, 64), F32)
        stC = sbuf("stC", (P, 64), F32)
        stD = sbuf("stD", (P, 64), F32)
        stE = sbuf("stE", (P, 64), F32)
        stF = sbuf("stF", (P, 64), F32)
    if not getattr(build_program, "STATS_FIRST", False):
        clockc = sbuf("clockc", (P, 64), F32)
        ccol = [sbuf(f"ccol{i}", (P, 8), F32) for i in range(4)]
        carry = sbuf("carry", (1, 2 * D), F32)

    cx.AR, cx.TL, cx.BC, cx.OUTS, cx.RP = AR, TL, BC, OUTS, RP
    cx.identf, cx.ident, cx.tri_f = identf, ident, tri_f
    cx.onesb, cx.onesf, cx.onescol = onesb, onesf, onescol
    cx.stA, cx.stB, cx.stC, cx.stD, cx.stE = stA, stB, stC, stD, stE
    cx.stF = stF
    cx.clockc, cx.carry = clockc, carry
    cx.ccol = ccol

    h = nc.psum_tensor("psA", [P, 2048], F32)
    psA = h.__enter__(); mgr.append(h)
    psB = []
    for i in range(2):
        h = nc.psum_tensor(f"psB{i}", [P, 512], F32)
        psB.append(h.__enter__()); mgr.append(h)
    ptA = []
    for i in range(2):
        h = nc.psum_tensor(f"ptA{i}", [P, 1024], BF16)
        ptA.append(h.__enter__()); mgr.append(h)
    cx.psA, cx.psB, cx.ptA = psA, psB, ptA

    tk = cx.tk = {}
    tk["psA"] = toks(4, "psA")
    tk["psB"] = toks(2, "psB")
    tk["ptA"] = toks(2, "ptA")
    tk["TL"] = toks(8, "TL")
    tk["BC"] = toks(4, "BC")
    tk["OUTS"] = Tok("OUTS")
    tk["RP"] = toks(2, "RP")
    tk["const"] = Tok("const")
    tk["stats"] = Tok("stats")
    tk["clock"] = Tok("clock")
    tk["carry"] = Tok("carry")
    tk["ccol"] = toks(4, "ccol")

    ctk = tk["const"]

    def gms(ap, v, tag):
        em("gpsimd", lambda e, ap=ap, v=v: e.memset(ap, v),
           r=(), w=[ctk], tag=tag)

    gms(ident[:], 0.0, "id0")
    em("gpsimd", lambda e: e.affine_select(
        out=ident[:], in_=ident[:], compare_op=ALU.not_equal, fill=1.0,
        base=0, pattern=[[-1, P]], channel_multiplier=1), w=[ctk], tag="id1")
    gms(identf[:], 0.0, "idf0")
    em("gpsimd", lambda e: e.affine_select(
        out=identf[:], in_=identf[:], compare_op=ALU.not_equal, fill=1.0,
        base=0, pattern=[[-1, P]], channel_multiplier=1), w=[ctk], tag="idf1")
    gms(tri_f[:], 1.0, "tri0")
    em("gpsimd", lambda e: e.affine_select(
        out=tri_f[:], in_=tri_f[:], compare_op=ALU.is_ge, fill=0.0,
        base=0, pattern=[[1, P]], channel_multiplier=-1), w=[ctk], tag="tri1")
    gms(onesb[:], 1.0, "ob")
    gms(onesf[:], 1.0, "of")
    gms(onescol[:], 1.0, "oc")
    em("scalar", lambda e: e.activation(stA[:, 0:1], onescol[:], AF.Ln,
                                        bias=0.0, scale=1.0),
       r=[ctk], w=[Tok("wu1")], tag="warm_ln")
    em("scalar", lambda e: e.activation(stA[:, 0:1], onescol[:], AF.Exp,
                                        bias=0.0, scale=1.0),
       r=[ctk], w=[Tok("wu2")], tag="warm_exp")

    cx.A_DMASK = 58 * 1024
    dm = lambda off=cx.A_DMASK: bass.AP(AR, off, [[60 * 1024, P], [1, 2048]])
    cx.dmask_ap = dm()
    gms(cx.dmask_ap, 1.0, "dm0")
    if not getattr(build_program, "SKIP_DMASK", False):
        em("gpsimd", lambda e: e.affine_select(
            out=bass.AP(AR, cx.A_DMASK, [[60 * 1024, P], [512, 4], [1, 512]]),
            in_=bass.AP(AR, cx.A_DMASK, [[60 * 1024, P], [512, 4], [1, 512]]),
            compare_op=ALU.is_ge, fill=0.0,
            base=0, pattern=[[-P, 4], [1, 512]], channel_multiplier=-1),
            w=[ctk], tag="dm1")

    return nc, cx


def ar_ap(cx, off, ap_dims):
    return bass.AP(cx.AR, off, [[60 * 1024, P]] + ap_dims)


def arena_phase(cx, names):
    old = cx.artoks
    newt = {nm: cx.B.alias(nm, *old) for nm in names}
    cx.artoks = list(newt.values())
    return newt



def ln_stats(cx, src, n0, nt, tag, fold_rms=False, src_toks=None):
    B = cx.B
    em = B.emit
    tlt = cx.tk["TL"]
    st = toks(nt, f"{tag}st")
    for i in range(nt):
        b = i % 2
        xt, xtt = cx.TL[b], tlt[b]
        em("sync", lambda e, i=i, xt=xt: e.dma_start(
            xt[:], src[(n0 + i) * P:(n0 + i + 1) * P, :]),
            r=([src_toks[n0 + i]] if src_toks else []), w=[xtt],
            dma=True, tag=f"{tag}ld{i}")
        em("vector", lambda e, i=i, xt=xt: e.reduce_sum(
            cx.stA[:, i:i + 1], xt[:], axis=AX.X),
            r=[xtt], w=[st[i]], tag=f"{tag}sm{i}")
        if getattr(ln_stats, "NO_ACCUM", False):
            em("scalar", lambda e, i=i, xt=xt: e.activation(
                cx.TL[2][:], xt[:], AF.Square, bias=0.0, scale=1.0),
                r=[xtt], w=[tlt[2]], tag=f"{tag}sq{i}")
            em("vector", lambda e, i=i: e.reduce_sum(
                cx.stB[:, i:i + 1], cx.TL[2][:], axis=AX.X),
                r=[tlt[2]], w=[st[i]], tag=f"{tag}sqr{i}")
        else:
            em("scalar", lambda e, i=i, xt=xt: e.activation(
                cx.TL[2][:], xt[:], AF.Square, bias=0.0, scale=1.0,
                accum_out=cx.stB[:, i:i + 1]),
                r=[xtt], w=[tlt[2], st[i]], tag=f"{tag}sq{i}")
    stok = Tok(f"{tag}stats")
    m, v, u, r2 = cx.TL[3], cx.TL[4], cx.TL[5], cx.TL[6]
    mt, vt, ut, r2t = tlt[3], tlt[4], tlt[5], tlt[6]
    sl = slice(0, nt)
    nmsl = slice(32, 32 + nt)
    em("vector", lambda e: e.tensor_scalar_mul(m[:, sl], cx.stA[:, sl], 1.0 / D),
       r=st, w=[mt], tag=f"{tag}m")
    em("scalar", lambda e: e.activation(cx.stD[:, nmsl], m[:, sl], AF.Copy,
                                        bias=0.0, scale=-1.0),
       r=[mt], w=[stok], tag=f"{tag}nm")
    em("vector", lambda e: e.tensor_scalar_mul(v[:, sl], cx.stB[:, sl], 1.0 / D),
       r=st, w=[vt], tag=f"{tag}v0")
    em("scalar", lambda e: e.activation(cx.stC[:, nmsl], m[:, sl], AF.Square,
                                        bias=0.0, scale=1.0),
       r=[mt], w=[stok], tag=f"{tag}ms")
    em("gpsimd", lambda e: e.tensor_sub(v[:, sl], v[:, sl], cx.stC[:, nmsl]),
       r=[vt, stok], w=[vt], tag=f"{tag}v")
    if fold_rms in (True, "both"):
        eps2 = RMS_EPS if fold_rms is True else LN_EPS
        dstC = cx.stC if fold_rms is True else cx.stE
        dstD = cx.stD if fold_rms is True else cx.stF
        em("vector", lambda e: e.tensor_scalar(
            r2[:, sl], v[:, sl], 1.0 + eps2, LN_EPS * eps2,
            op0=ALU.mult, op1=ALU.add),
            r=[vt], w=[r2t], tag=f"{tag}r2")
        em("scalar", lambda e: e.activation(dstC[:, sl], r2[:, sl], AF.Ln,
                                            bias=0.0, scale=1.0),
           r=[r2t], w=[stok], tag=f"{tag}l1")
        em("scalar", lambda e: e.activation(dstC[:, sl], dstC[:, sl], AF.Exp,
                                            bias=0.0, scale=-0.5),
           r=[stok], w=[stok], tag=f"{tag}ex")
        em("gpsimd", lambda e: e.tensor_mul(dstD[:, sl], cx.stD[:, nmsl],
                                            dstC[:, sl]),
           r=[stok], w=[stok], tag=f"{tag}fb")
    if fold_rms is not True:
        em("vector", lambda e: e.tensor_scalar_add(u[:, sl], v[:, sl], LN_EPS),
           r=[vt], w=[ut], tag=f"{tag}ue")
        em("scalar", lambda e: e.activation(cx.stC[:, sl], u[:, sl], AF.Ln,
                                            bias=0.0, scale=1.0),
           r=[ut], w=[stok], tag=f"{tag}l1p")
        em("scalar", lambda e: e.activation(cx.stC[:, sl], cx.stC[:, sl], AF.Exp,
                                            bias=0.0, scale=-0.5),
           r=[stok], w=[stok], tag=f"{tag}exp")
        em("gpsimd", lambda e: e.tensor_mul(cx.stD[:, sl], cx.stD[:, nmsl],
                                            cx.stC[:, sl]),
           r=[stok], w=[stok], tag=f"{tag}b1")
    return stok


def bcast_row(cx, dst, dst_tok, row_ap, row_toks, n, tag):
    em = cx.B.emit
    pbt = cx.tk["psB"]
    for j in range(0, n, 512):
        w = min(512, n - j)
        pi = (j // 512) % 2
        em("tensor", lambda e, j=j, w=w, pi=pi: e.matmul(
            cx.psB[pi][:, 0:w], cx.onesf[:], row_ap[0:1, j:j + w],
            start=True, stop=True),
            r=list(row_toks) + [cx.tk["const"]], w=[pbt[pi]], tag=f"{tag}m{j}")
        em("vector", lambda e, j=j, w=w, pi=pi: e.tensor_copy(
            dst[:, j:j + w], cx.psB[pi][:, 0:w]),
            r=[pbt[pi]], w=[dst_tok], tag=f"{tag}e{j}")


def load_row_tl(cx, tl_i, src_1d, n, tag):
    t = cx.tk["TL"][tl_i]
    cx.B.emit("sync", lambda e: e.dma_start(
        cx.TL[tl_i][0:1, 0:n], src_1d.rearrange("(o a) -> o a", o=1)),
        w=[t], dma=True, tag=tag)
    return t



A_FM = 0
A_WKV = 16 * 1024
A_WQ = 32 * 1024
A_WG = 40960
A_EV = 41984
A_VE = 44032
A_GE = 46336

PITCH = 60 * 1024


def pA(cx, js):
    return cx.psA[:, js * 512:(js + 1) * 512]


def arp(cx, off, dims, pbase=0, pcnt=P):
    return bass.AP(cx.AR, off + pbase * PITCH, [[PITCH, pcnt]] + dims)


def phase_s1(cx):
    nc, B, T, NT, NG = cx.nc, cx.B, cx.T, cx.NT, cx.NG
    em = B.emit
    I, S = cx.I, cx.S
    TL, BC = cx.TL, cx.BC
    tlt, bct = cx.tk["TL"], cx.tk["BC"]
    psA, psB = cx.psA, cx.psB
    pat, pbt = cx.tk["psA"], cx.tk["psB"]
    ctk = cx.tk["const"]

    at = arena_phase(cx, ["fm0", "fm1", "wkv", "wq", "wg",
                          "ev0", "ev1", "ev2", "ev3", "ve0", "ve1", "ge"])
    fm_t = [at["fm0"], at["fm1"]]
    wkv_t, wq_t, wg_t = at["wkv"], at["wq"], at["wg"]
    ev_t = [at[f"ev{i}"] for i in range(4)]
    ve_t = [at[f"ve{i}"] for i in range(2)]
    ge_t = at["ge"]
    cx.s1 = dict(fm_t=fm_t, wkv_t=wkv_t, wq_t=wq_t, wg_t=wg_t,
                 ev_t=ev_t, ve_t=ve_t, ge_t=ge_t)

    stok = ln_stats(cx, I["x"], 0, NT, "s1n", fold_rms=True)
    defer = [stok] if getattr(phase_s1, "DEFER_W", False) else []

    em("gpsimd", lambda e: e.dma_start(
        arp(cx, A_WKV, [[2048, C], [1, 2048]]),
        I["aWkv"].rearrange("(c p) n -> p c n", p=P)),
        r=defer, w=[wkv_t], dma=True, tag="ld_wkv")
    em("gpsimd", lambda e: e.dma_start(
        arp(cx, A_WQ, [[1024, C], [1, 1024]]),
        I["aWq"].rearrange("(c p) n -> p c n", p=P)),
        r=defer, w=[wq_t], dma=True, tag="ld_wq")
    em("gpsimd", lambda e: e.dma_start(
        arp(cx, A_WG, [[16, C], [1, 16]]),
        I["aWg"].rearrange("(c p) n -> p c n", p=P)),
        r=defer, w=[wg_t], dma=True, tag="ld_wg")
    nw_t = load_row_tl(cx, 7, I["a_normw"], D, "ld_nw")
    bcast_row(cx, BC[0], bct[0], TL[7][0:1, :], [nw_t], D, "bc_nw")
    for i in range(NT):
        b = i % 2
        g, gb = i // 4, (i // 4) % 2
        xt, xtt = TL[b], tlt[b]
        xf, xft = TL[2 + b], tlt[2 + b]
        em("sync", lambda e, i=i, xt=xt: e.dma_start(
            xt[:], I["x"][i * P:(i + 1) * P, :]),
            w=[xtt], dma=True, tag=f"s1a_ld{i}")
        em("vector", lambda e, i=i, xt=xt, xf=xf: e.tensor_scalar(
            xf[:], xt[:], cx.stC[:, i:i + 1], cx.stD[:, i:i + 1],
            op0=ALU.mult, op1=ALU.add),
            r=[xtt, stok], w=[xft], tag=f"s1a_sc{i}")
        em("vector", lambda e, xf=xf: e.tensor_mul(xf[:], xf[:], BC[0][:]),
           r=[xft, bct[0]], w=[xft], tag=f"s1a_nw{i}")
        em("tensor", lambda e, xf=xf, b=b: [
            e.transpose(psA[:, (2 * b) * 512 + c * P:(2 * b) * 512 + (c + 1) * P],
                        xf[:, c * P:(c + 1) * P], cx.identf[:])
            for c in range(C)][-1],
            r=[xft, ctk], w=[pat[2 * b], pat[2 * b + 1]], tag=f"s1a_tr{i}")
        for hhalf in range(2):
            em("scalar", lambda e, i=i, gb=gb, b=b, hh=hhalf: e.copy(
                bass.AP(cx.AR, A_FM + gb * 8192 + (4 * hh) * 1024 + (i % 4) * P,
                        [[PITCH, P], [1024, 4], [1, P]]),
                psA[:, (2 * b + hh) * 512:(2 * b + hh + 1) * 512]
                .rearrange("p (a b2) -> p a b2", a=4)),
                r=[pat[2 * b + hhalf]], w=[fm_t[gb], pat[2 * b + hhalf]],
                tag=f"s1a_ev{i}_{hhalf}")
        if i % 4 == 3:
            _s1_proj_group(cx, g, gb)


    _s1_attention(cx)


def _s1_proj_group(cx, g, gb):
    B, em, S = cx.B, cx.B.emit, cx.S
    psA, psB = cx.psA, cx.psB
    pat, pbt = cx.tk["psA"], cx.tk["psB"]
    d = cx.s1
    fm_t, wkv_t, wq_t, wg_t = d["fm_t"], d["wkv_t"], d["wq_t"], d["wg_t"]
    ev_t, ve_t, ge_t = d["ev_t"], d["ve_t"], d["ge_t"]

    def fm_rhs(c, j0=0, n=512):
        return arp(cx, A_FM + gb * 8192 + c * 1024 + j0, [[1, n]])

    pg = g % 2
    em("tensor", lambda e, pg=pg: [
        e.matmul(psB[pg][0:16, :], arp(cx, A_WG + c * 16, [[1, 16]]),
                 fm_rhs(c), start=(c == 0), stop=(c == C - 1))
        for c in range(C)][-1],
        r=[wg_t, fm_t[gb]], w=[pbt[pg]], tag=f"s1g_mm{g}")
    em("scalar", lambda e, pg=pg: e.activation(
        arp(cx, A_GE, [[1, 512]], pcnt=16), psB[pg][0:16, :],
        AF.Sigmoid, bias=0.0, scale=1.0),
        r=[pbt[pg]], w=[ge_t, pbt[pg]], tag=f"s1g_sg{g}")
    em("gpsimd", lambda e, g=g: e.dma_start(
        S["gT"][:, g * 512:(g + 1) * 512], arp(cx, A_GE, [[1, 512]], pcnt=16)),
        r=[ge_t], w=[cx.dt["gT"][g]], dma=True, tag=f"s1g_st{g}")

    for which, wtok, wbase, wpitch, dst in (
            ("k", wkv_t, A_WKV, 2048, S["aKT"]),
            ("q", wq_t, A_WQ, 1024, S["aQT"])):
        for dk in range(C):
            js = dk % 4
            evi = dk % 4
            em("tensor", lambda e, dk=dk, js=js, wbase=wbase, wpitch=wpitch,
               wtok=wtok: [
                e.matmul(pA(cx, js), arp(cx, wbase + c * wpitch + dk * P, [[1, P]]),
                         fm_rhs(c), start=(c == 0), stop=(c == C - 1))
                for c in range(C)][-1],
                r=[wtok, fm_t[gb]], w=[pat[js]], tag=f"s1{which}T_mm{dk}_{g}")
            em("scalar", lambda e, js=js, evi=evi: e.copy(
                arp(cx, A_EV + evi * 512, [[1, 512]]), pA(cx, js)),
                r=[pat[js]], w=[ev_t[evi], pat[js]],
                tag=f"s1{which}T_ev{dk}_{g}")
            dtk = cx.dt["aKT" if which == "k" else "aQT"][g]
            em("gpsimd", lambda e, dk=dk, g=g, evi=evi, dst=dst: e.dma_start(
                dst[dk * P:(dk + 1) * P, g * 512:(g + 1) * 512],
                arp(cx, A_EV + evi * 512, [[1, 512]])),
                r=[ev_t[evi]], w=[dtk], dma=True, tag=f"s1{which}T_st{dk}_{g}")

    for ii in range(4):
        jsA, jsB = 2 * (ii % 2), 2 * (ii % 2) + 1
        vv = ii % 2
        em("tensor", lambda e, ii=ii, jsA=jsA: [
            e.matmul(pA(cx, jsA), fm_rhs(c, ii * P, P),
                     arp(cx, A_WKV + c * 2048 + 1024, [[1, 512]]),
                     start=(c == 0), stop=(c == C - 1))
            for c in range(C)][-1],
            r=[wkv_t, fm_t[gb]], w=[pat[jsA]], tag=f"s1v_mmA{ii}_{g}")
        em("tensor", lambda e, ii=ii, jsB=jsB: [
            e.matmul(pA(cx, jsB), fm_rhs(c, ii * P, P),
                     arp(cx, A_WKV + c * 2048 + 1536, [[1, 512]]),
                     start=(c == 0), stop=(c == C - 1))
            for c in range(C)][-1],
            r=[wkv_t, fm_t[gb]], w=[pat[jsB]], tag=f"s1v_mmB{ii}_{g}")
        em("gpsimd", lambda e, vv=vv: e.memset(
            arp(cx, A_VE + vv * 1088, [[1, 1040]]), 1.0),
            r=[], w=[ve_t[vv]], tag=f"s1v_on{ii}_{g}")
        em("vector", lambda e, vv=vv, jsA=jsA: e.tensor_copy(
            bass.AP(cx.AR, A_VE + vv * 1088, [[PITCH, P], [65, 8], [1, 64]]),
            pA(cx, jsA).rearrange("p (a b2) -> p a b2", a=8)),
            r=[pat[jsA]], w=[ve_t[vv], pat[jsA]], tag=f"s1v_evA{ii}_{g}")
        em("vector", lambda e, vv=vv, jsB=jsB: e.tensor_copy(
            bass.AP(cx.AR, A_VE + vv * 1088 + 8 * 65,
                    [[PITCH, P], [65, 8], [1, 64]]),
            pA(cx, jsB).rearrange("p (a b2) -> p a b2", a=8)),
            r=[pat[jsB]], w=[ve_t[vv], pat[jsB]], tag=f"s1v_evB{ii}_{g}")
        em("gpsimd", lambda e, ii=ii, g=g, vv=vv: e.dma_start(
            S["aV"][(g * 4 + ii) * P:(g * 4 + ii + 1) * P, :],
            arp(cx, A_VE + vv * 1088, [[1, 1040]])),
            r=[ve_t[vv]], w=[cx.dt["aV"][g]], dma=True, tag=f"s1v_st{ii}_{g}")


def _s1_attention(cx):
    B, em, S, I = cx.B, cx.B.emit, cx.S, cx.I
    T, NG = cx.T, cx.NG
    psA, psB = cx.psA, cx.psB
    pat, pbt = cx.tk["psA"], cx.tk["psB"]
    TL, tlt, RP = cx.TL, cx.tk["TL"], cx.RP
    rpt = cx.tk["RP"]
    ctk = cx.tk["const"]
    OUTS, otk = cx.OUTS, cx.tk["OUTS"]
    d = cx.s1

    A_QT = 0
    A_KT = 4 * 1024
    A_V = 12 * 1024
    A_PT = 17 * 1024
    A_WO = 21 * 1024 + 512
    A_YT = 29 * 1024 + 1024

    at = arena_phase(cx, ["qt", "kt0", "kt1", "v0", "v1",
                          "pt0a", "pt0b", "pt1a", "pt1b", "wo", "yt"])
    qt_t = at["qt"]
    kt_t = [at["kt0"], at["kt1"]]
    v_t = [at["v0"], at["v1"]]
    pt_t = [[at["pt0a"], at["pt0b"]], [at["pt1a"], at["pt1b"]]]
    wo_t = at["wo"]
    yt_t = at["yt"]
    cx.s1at = at

    em("gpsimd", lambda e: e.dma_start(
        arp(cx, A_WO, [[1024, C], [1, 1024]]),
        I["aWout"].rearrange("(c p) n -> p c n", p=P)),
        r=(), w=[wo_t], dma=True, tag="ld_wout")

    for qt in range(NG):
        em("sync", lambda e, qt=qt: e.dma_start(
            arp(cx, A_QT, [[512, C], [1, 512]]),
            S["aQT"][:, qt * 512:(qt + 1) * 512]
            .rearrange("(c p) n -> p c n", p=P)),
            r=[cx.dt["aQT"][qt]], w=[qt_t], dma=True, tag=f"at_q{qt}")
        for hg in range(2):
            for jg in range(qt + 1):
                bj = jg % 2
                em("sync", lambda e, jg=jg, bj=bj: e.dma_start(
                    arp(cx, A_KT + bj * 4096, [[512, C], [1, 512]]),
                    S["aKT"][:, jg * 512:(jg + 1) * 512]
                    .rearrange("(c p) n -> p c n", p=P)),
                    r=[cx.dt["aKT"][jg]], w=[kt_t[bj]], dma=True,
                    tag=f"at_k{qt}_{hg}_{jg}")
                em("gpsimd", lambda e, jg=jg, bj=bj, hg=hg: e.dma_start(
                    arp(cx, A_V + bj * 2176, [[520, 4], [1, 520]]),
                    S["aV"].rearrange("(a p) n -> p a n", p=P)
                    [:, jg * 4:(jg + 1) * 4, hg * 520:(hg + 1) * 520]),
                    r=[cx.dt["aV"][jg]], w=[v_t[bj]], dma=True,
                    tag=f"at_v{qt}_{hg}_{jg}")
                for h in range(8):
                    habs = hg * 8 + h
                    ch, off = habs // 2, (habs % 2) * 64
                    bp = h % 2
                    for hf in range(2):
                        em("tensor", lambda e, ch=ch, off=off, bj=bj, hf=hf: [
                            e.matmul(pA(cx, js),
                                     arp(cx, A_KT + bj * 4096 + ch * 512
                                         + js * P, [[1, P]], pbase=off,
                                         pcnt=64),
                                     arp(cx, A_QT + ch * 512, [[1, 512]],
                                         pbase=off, pcnt=64),
                                     start=True, stop=True)
                            for js in (2 * hf, 2 * hf + 1)][-1],
                            r=[kt_t[bj], qt_t],
                            w=[pat[2 * hf], pat[2 * hf + 1]],
                            tag=f"at_s{qt}_{hg}_{jg}_{h}_{hf}")
                    for hf in range(2):
                        em("scalar", lambda e, bp=bp, hf=hf: e.activation(
                            arp(cx, A_PT + bp * 2048 + hf * 1024, [[1, 1024]]),
                            psA[:, hf * 1024:(hf + 1) * 1024],
                            AF.Exp, bias=0.0, scale=SC_A),
                            r=[pat[2 * hf], pat[2 * hf + 1]],
                            w=[pt_t[bp][hf], pat[2 * hf], pat[2 * hf + 1]],
                            tag=f"at_e{qt}_{hg}_{jg}_{h}_{hf}")
                        if jg == qt:
                            em("vector", lambda e, bp=bp, hf=hf: e.tensor_mul(
                                arp(cx, A_PT + bp * 2048 + hf * 1024,
                                    [[1, 1024]]),
                                arp(cx, A_PT + bp * 2048 + hf * 1024,
                                    [[1, 1024]]),
                                bass.AP(cx.AR, cx.A_DMASK + hf * 1024,
                                        [[60 * 1024, P], [1, 1024]])),
                                r=[pt_t[bp][hf], ctk], w=[pt_t[bp][hf]],
                                tag=f"at_m{qt}_{hg}_{jg}_{h}_{hf}")
                    for hf in range(2):
                        em("tensor", lambda e, h=h, bj=bj, bp=bp, hf=hf: [
                            e.matmul(psB[bp][0:65, :],
                                     bass.AP(cx.AR, A_V + bj * 2176 + js * 520
                                             + h * 65, [[PITCH, P], [1, 65]]),
                                     arp(cx, A_PT + bp * 2048 + js * 512,
                                         [[1, 512]]),
                                     start=(js == 0), stop=(js == 3))
                            for js in (2 * hf, 2 * hf + 1)][-1],
                            r=[v_t[bj], pt_t[bp][hf]], w=[pbt[bp]],
                            tag=f"at_av{qt}_{hg}_{jg}_{h}_{hf}")
                    if jg == 0:
                        em("vector", lambda e, h=h, bp=bp: e.tensor_copy(
                            OUTS[0:65, h * 512:(h + 1) * 512],
                            psB[bp][0:65, :]),
                            r=[pbt[bp]], w=[otk, pbt[bp]],
                            tag=f"at_o0{qt}_{hg}_{h}")
                    else:
                        em("vector", lambda e, h=h, bp=bp: e.tensor_add(
                            OUTS[0:65, h * 512:(h + 1) * 512],
                            OUTS[0:65, h * 512:(h + 1) * 512],
                            psB[bp][0:65, :]),
                            r=[pbt[bp], otk], w=[otk, pbt[bp]],
                            tag=f"at_oa{qt}_{hg}_{jg}_{h}")
            for hb in range(2):
                em("sync", lambda e, qt=qt, hg=hg, hb=hb: e.dma_start(
                    RP[0:1, :].rearrange("o (h n) -> o h n", h=4),
                    S["gT"][hg * 8 + hb * 4:hg * 8 + hb * 4 + 4,
                            qt * 512:(qt + 1) * 512]
                    .rearrange("(o h) n -> o h n", o=1)),
                    r=[cx.dt["gT"][qt]], w=[rpt[0], rpt[1]], dma=True,
                    tag=f"ae_g{qt}_{hg}_{hb}")
                for hh in range(4):
                    h = hb * 4 + hh
                    habs = hg * 8 + h
                    ch, off = habs // 2, (habs % 2) * 64
                    bs = h % 2
                    em("scalar", lambda e, h=h, bs=bs: e.activation(
                        TL[5][0:1, bs * 512:(bs + 1) * 512],
                        OUTS[64:65, h * 512:(h + 1) * 512],
                        AF.Ln, bias=0.0, scale=1.0),
                        r=[otk], w=[tlt[5]], tag=f"ae_r{qt}_{hg}_{h}")
                    em("scalar", lambda e, h=h, bs=bs: e.activation(
                        TL[5][0:1, bs * 512:(bs + 1) * 512],
                        TL[5][0:1, bs * 512:(bs + 1) * 512],
                        AF.Exp, bias=0.0, scale=-1.0),
                        r=[tlt[5]], w=[tlt[5]], tag=f"ae_r2{qt}_{hg}_{h}")
                    em("vector", lambda e, hh=hh, bs=bs: e.tensor_mul(
                        TL[5][0:1, bs * 512:(bs + 1) * 512],
                        TL[5][0:1, bs * 512:(bs + 1) * 512],
                        RP[0:1, hh * 512:(hh + 1) * 512]),
                        r=[tlt[5], rpt[0], rpt[1]], w=[tlt[5]],
                        tag=f"ae_s{qt}_{hg}_{h}")
                    em("tensor", lambda e, bs=bs: e.matmul(
                        psB[bs][0:64, :], cx.onesf[0:1, 0:64],
                        TL[5][0:1, bs * 512:(bs + 1) * 512],
                        start=True, stop=True),
                        r=[tlt[5], ctk], w=[pbt[bs]], tag=f"ae_b{qt}_{hg}_{h}")
                    em("vector", lambda e, h=h, bs=bs, ch=ch, off=off:
                       e.tensor_mul(
                        arp(cx, A_YT + ch * 512, [[1, 512]], pbase=off, pcnt=64),
                        OUTS[0:64, h * 512:(h + 1) * 512],
                        psB[bs][0:64, :]),
                        r=[otk, pbt[bs]], w=[yt_t, pbt[bs]],
                        tag=f"ae_y{qt}_{hg}_{h}")
        for ts in range(4):
            j0, j1 = (ts % 2) * 2, (ts % 2) * 2 + 1
            em("tensor", lambda e, ts=ts, j0=j0, j1=j1: [
                e.matmul(pA(cx, (j0 if nb == 0 else j1)),
                         arp(cx, A_YT + ch2 * 512 + ts * P, [[1, P]]),
                         arp(cx, A_WO + ch2 * 1024 + nb * 512, [[1, 512]]),
                         start=(ch2 == 0), stop=(ch2 == C - 1))
                for nb in range(2) for ch2 in range(C)][-1],
                r=[yt_t, wo_t], w=[pat[j0], pat[j1]],
                tag=f"ao_mm{qt}_{ts}")
            bx = ts % 2
            em("sync", lambda e, qt=qt, ts=ts, bx=bx: e.dma_start(
                TL[bx][:], I["x"][(qt * 4 + ts) * P:(qt * 4 + ts + 1) * P, :]),
                w=[tlt[bx]], dma=True, tag=f"ao_ldx{qt}_{ts}")
            em("vector", lambda e, bx=bx, j0=j0, j1=j1: [
                e.tensor_add(TL[2 + bx][:, nb * 512:(nb + 1) * 512],
                             TL[bx][:, nb * 512:(nb + 1) * 512],
                             pA(cx, (j0 if nb == 0 else j1)))
                for nb in range(2)][-1],
                r=[tlt[bx], pat[j0], pat[j1]],
                w=[tlt[2 + bx], pat[j0], pat[j1]],
                tag=f"ao_add{qt}_{ts}")
            em("gpsimd", lambda e, qt=qt, ts=ts, bx=bx: e.dma_start(
                S["x1"][(qt * 4 + ts) * P:(qt * 4 + ts + 1) * P, :],
                TL[2 + bx][:]),
                r=[tlt[2 + bx]], w=[cx.dt["x1"][qt * 4 + ts]], dma=True,
                tag=f"ao_st{qt}_{ts}")



A2_W = 0
A2_XT = 24576
A2_KT = 28672
A2_WL = 30720


def transpose_ffn_weights(cx):
    B, em, S, I = cx.B, cx.B.emit, cx.S, cx.I
    ptA = cx.ptA
    ptt = cx.tk["ptA"]
    ctk = cx.tk["const"]
    at = arena_phase(cx, ["wl", "st0", "st1"])
    wl_t = at["wl"]
    st_t = [at["st0"], at["st1"]]
    for wi, (wname, dname) in enumerate(
            (("fWr", "wrT"), ("fWk", "wkT"), ("fWv", "wvT"))):
        em("gpsimd", lambda e, wname=wname: e.dma_start(
            arp(cx, A2_WL, [[1024, C], [1, 1024]]),
            I[wname].rearrange("(c p) n -> p c n", p=P)),
            w=[wl_t], dma=True, tag=f"wt_ld{wi}")
        for cc in range(C):
            bp = cc % 2
            em("tensor", lambda e, cc=cc, bp=bp: [
                e.transpose(ptA[bp][:, dc * P:(dc + 1) * P],
                            arp(cx, A2_WL + dc * 1024 + cc * P, [[1, P]]),
                            cx.ident[:])
                for dc in range(C)][-1],
                r=[wl_t, ctk], w=[ptt[bp]], tag=f"wt_tr{wi}_{cc}")
            em("scalar", lambda e, cc=cc, bp=bp: e.copy(
                arp(cx, A2_KT + bp * 1024, [[1, 1024]]), ptA[bp][:, :]),
                r=[ptt[bp]], w=[st_t[bp], ptt[bp]], tag=f"wt_ev{wi}_{cc}")
            em("gpsimd", lambda e, cc=cc, bp=bp, dname=dname: e.dma_start(
                S[dname][cc * P:(cc + 1) * P, :],
                arp(cx, A2_KT + bp * 1024, [[1, 1024]])),
                r=[st_t[bp]], w=[cx.dt[dname][0]], dma=True,
                tag=f"wt_st{wi}_{cc}")


def phase_ffn(cx, src_name, dst_name, which):
    B, em, S, I = cx.B, cx.B.emit, cx.S, cx.I
    T, NT = cx.T, cx.NT
    TL, BC = cx.TL, cx.BC
    tlt, bct = cx.tk["TL"], cx.tk["BC"]
    psA, psB = cx.psA, cx.psB
    pat, pbt = cx.tk["psA"], cx.tk["psB"]
    ctk = cx.tk["const"]
    src = S[src_name]
    dst_ap = cx.y if dst_name == "y" else S[dst_name]
    dst_toks = cx.dt[dst_name]

    pfx = f"f{which}"
    at = arena_phase(cx, ["w0", "w1", "w2", "xt0", "xt1", "kt0", "kt1"])
    w_t = [at[f"w{i}"] for i in range(3)]
    xt_t = [at["xt0"], at["xt1"]]
    kt_t = [at["kt0"], at["kt1"]]
    for wi, dname in enumerate(("wrT", "wkT", "wvT")):
        em("sync", lambda e, wi=wi, dname=dname: e.dma_start(
            arp(cx, A2_W + wi * 8192, [[1024, C], [1, 1024]]),
            S[dname].rearrange("(c p) n -> p c n", p=P)),
            r=[cx.dt[dname][0]], w=[w_t[wi]], dma=True, tag=f"{pfx}_ldw{wi}")
    mur_t = load_row_tl(cx, 7, I["fmur"], D, f"{pfx}_ldmur")
    bcast_row(cx, BC[1], bct[1], TL[7][0:1, :], [mur_t], D, f"{pfx}_bmur")
    muk_t = load_row_tl(cx, 7, I["fmuk"], D, f"{pfx}_ldmuk")
    bcast_row(cx, BC[2], bct[2], TL[7][0:1, :], [muk_t], D, f"{pfx}_bmuk")

    stok = ln_stats(cx, src, 0, NT, f"{pfx}n", src_toks=cx.dt[src_name])
    for i in range(NT):
        b = i % 2
        xt, xtt = TL[b], tlt[b]
        tt_, ttt = TL[2 + b], tlt[2 + b]
        em("sync", lambda e, i=i, xt=xt: e.dma_start(
            xt[:], src[i * P:(i + 1) * P, :]),
            r=[cx.dt[src_name][i]], w=[xtt], dma=True, tag=f"{pfx}t_ld{i}")
        em("vector", lambda e, i=i, xt=xt, tt_=tt_: e.tensor_scalar(
            tt_[:], xt[:], cx.stC[:, i:i + 1], cx.stD[:, i:i + 1],
            op0=ALU.mult, op1=ALU.add),
            r=[xtt, stok], w=[ttt], tag=f"{pfx}t_sc{i}")
        em("gpsimd", lambda e, i=i, tt_=tt_: e.dma_start(
            S["t_ffn"][i * P:(i + 1) * P, :], tt_[:]),
            r=[ttt], w=[cx.dt["t_ffn"][i]], dma=True, tag=f"{pfx}t_st{i}")

    if which == 1:
        for k in range(8):
            em("gpsimd", lambda e, k=k: e.dma_start(
                S["w1b"][k * P:(k + 1) * P, :], I["bW1"][k * P:(k + 1) * P, :]),
                w=[cx.dt["w1b"][k]], dma=True, tag=f"pc_w1_{k}")
            em("gpsimd", lambda e, k=k: e.dma_start(
                S["w2b"][k * 512:(k + 1) * 512, :],
                I["bW2"][k * 512:(k + 1) * 512, :]),
                w=[cx.dt["w2b"][k]], dma=True, tag=f"pc_w2_{k}")

    pend = None
    for i in range(NT):
        b = i % 2
        ta, tat = TL[b], tlt[b]
        ts_, tst = TL[2 + b], tlt[2 + b]
        dd, ddt = TL[4], tlt[4]
        xr, xrt = TL[5], tlt[5]
        ka, kat = TL[6], tlt[6]
        rr, rrt = TL[7], tlt[7]
        rx, rxt = (BC[0], bct[0]) if i % 2 == 0 else (BC[3], bct[3])
        em("sync", lambda e, i=i, ta=ta: e.dma_start(
            ta[:], S["t_ffn"][i * P:(i + 1) * P, :]),
            r=[cx.dt["t_ffn"][i]], w=[tat], dma=True, tag=f"{pfx}m_lda{i}")
        em("sync", lambda e, i=i, rx=rx: e.dma_start(
            rx[:], src[i * P:(i + 1) * P, :]),
            r=[cx.dt[src_name][i]], w=[rxt], dma=True, tag=f"{pfx}m_ldx{i}")
        if i == 0:
            em("gpsimd", lambda e, ts_=ts_: e.memset(ts_[0:1, :], 0.0),
               w=[tst], tag=f"{pfx}m_z0")
            em("sync", lambda e, ts_=ts_: e.dma_start(
                ts_[1:P, :], S["t_ffn"][0:P - 1, :]),
                r=[cx.dt["t_ffn"][0]], w=[tst], dma=True, tag=f"{pfx}m_lds0")
        else:
            em("sync", lambda e, i=i, ts_=ts_: e.dma_start(
                ts_[:], S["t_ffn"][i * P - 1:(i + 1) * P - 1, :]),
                r=[cx.dt["t_ffn"][i - 1], cx.dt["t_ffn"][i]], w=[tst],
                dma=True, tag=f"{pfx}m_lds{i}")
        em("vector", lambda e, ta=ta, ts_=ts_, dd=dd: e.tensor_sub(
            dd[:], ta[:], ts_[:]),
            r=[tat, tst], w=[ddt], tag=f"{pfx}m_d{i}")
        for half, (mubc, mubct, xoff) in enumerate(
                ((BC[1], bct[1], 0), (BC[2], bct[2], 1024))):
            em("vector", lambda e, dd=dd, xr=xr, mubc=mubc: e.tensor_mul(
                xr[:], dd[:], mubc[:]),
                r=[ddt, mubct], w=[xrt], tag=f"{pfx}m_x{half}a{i}")
            em("vector", lambda e, xr=xr, ts_=ts_: e.tensor_add(
                xr[:], xr[:], ts_[:]),
                r=[xrt, tst], w=[xrt], tag=f"{pfx}m_x{half}b{i}")
            pa0 = 2 * (i % 2)
            em("tensor", lambda e, xr=xr, pa0=pa0: [
                e.transpose(psA[:, pa0 * 512 + c * P:pa0 * 512 + (c + 1) * P],
                            xr[:, c * P:(c + 1) * P], cx.identf[:])
                for c in range(C)][-1],
                r=[xrt, ctk], w=[pat[pa0], pat[pa0 + 1]],
                tag=f"{pfx}m_xt{half}_{i}")
            em("scalar", lambda e, pa0=pa0, xoff=xoff, b=b: e.copy(
                arp(cx, A2_XT + b * 2048 + xoff, [[1, 1024]]),
                psA[:, pa0 * 512:pa0 * 512 + 1024]),
                r=[pat[pa0], pat[pa0 + 1]],
                w=[xt_t[b], pat[pa0], pat[pa0 + 1]],
                tag=f"{pfx}m_xe{half}_{i}")
        if pend is not None:
            pend()
            pend = None
        em("tensor", lambda e, b=b: [
            e.matmul(psB[nb][:, :],
                     arp(cx, A2_XT + b * 2048 + c * P, [[1, P]]),
                     arp(cx, A2_W + 0 + c * 1024 + nb * 512, [[1, 512]]),
                     start=(c == 0), stop=(c == C - 1))
            for nb in range(2) for c in range(C)][-1],
            r=[xt_t[b], w_t[0]], w=[pbt[0], pbt[1]], tag=f"{pfx}m_rmm{i}")
        em("scalar", lambda e, rr=rr: [
            e.activation(rr[:, nb * 512:(nb + 1) * 512], psB[nb][:, :],
                         AF.Sigmoid, bias=0.0, scale=1.0)
            for nb in range(2)][-1],
            r=[pbt[0], pbt[1]], w=[rrt, pbt[0], pbt[1]],
            tag=f"{pfx}m_rsg{i}")
        pa0 = 2 * (i % 2)
        em("tensor", lambda e, b=b, pa0=pa0: [
            e.matmul(psA[:, (pa0 + nb) * 512:(pa0 + nb + 1) * 512],
                     arp(cx, A2_XT + b * 2048 + 1024 + c * P, [[1, P]]),
                     arp(cx, A2_W + 8192 + c * 1024 + nb * 512, [[1, 512]]),
                     start=(c == 0), stop=(c == C - 1))
            for nb in range(2) for c in range(C)][-1],
            r=[xt_t[b], w_t[1]], w=[pat[pa0], pat[pa0 + 1]],
            tag=f"{pfx}m_kmm{i}")
        em("scalar", lambda e, ka=ka, pa0=pa0: e.activation(
            ka[:], psA[:, pa0 * 512:pa0 * 512 + 1024], AF.Relu,
            bias=0.0, scale=1.0),
            r=[pat[pa0], pat[pa0 + 1]], w=[kat, pat[pa0], pat[pa0 + 1]],
            tag=f"{pfx}m_krl{i}")
        em("scalar", lambda e, ka=ka: e.activation(
            ka[:], ka[:], AF.Square, bias=0.0, scale=1.0),
            r=[kat], w=[kat], tag=f"{pfx}m_ksq{i}")
        em("tensor", lambda e, ka=ka, pa0=pa0: [
            e.transpose(psA[:, pa0 * 512 + c * P:pa0 * 512 + (c + 1) * P],
                        ka[:, c * P:(c + 1) * P], cx.identf[:])
            for c in range(C)][-1],
            r=[kat, ctk], w=[pat[pa0], pat[pa0 + 1]], tag=f"{pfx}m_kt{i}")
        em("scalar", lambda e, pa0=pa0, b=b: e.copy(
            arp(cx, A2_KT + b * 1024, [[1, 1024]]),
            psA[:, pa0 * 512:pa0 * 512 + 1024]),
            r=[pat[pa0], pat[pa0 + 1]], w=[kt_t[b], pat[pa0], pat[pa0 + 1]],
            tag=f"{pfx}m_ke{i}")
        em("tensor", lambda e, b=b: [
            e.matmul(psB[nb][:, :],
                     arp(cx, A2_KT + b * 1024 + c * P, [[1, P]]),
                     arp(cx, A2_W + 16384 + c * 1024 + nb * 512, [[1, 512]]),
                     start=(c == 0), stop=(c == C - 1))
            for nb in range(2) for c in range(C)][-1],
            r=[kt_t[b], w_t[2]], w=[pbt[0], pbt[1]], tag=f"{pfx}m_vmm{i}")

        def mk_back(i=i, ka=ka, kat=kat, rr=rr, rrt=rrt, rx=rx, rxt=rxt):
            def go():
                em("vector", lambda e: [
                    e.tensor_mul(ka[:, nb * 512:(nb + 1) * 512],
                                 rr[:, nb * 512:(nb + 1) * 512], psB[nb][:, :])
                    for nb in range(2)][-1],
                    r=[rrt, pbt[0], pbt[1]], w=[kat, pbt[0], pbt[1]],
                    tag=f"{pfx}m_mul{i}")
                em("vector", lambda e: e.tensor_add(
                    ka[:], ka[:], rx[:]),
                    r=[kat, rxt], w=[kat], tag=f"{pfx}m_res{i}")
                em("gpsimd", lambda e: e.dma_start(
                    dst_ap[i * P:(i + 1) * P, :], ka[:]),
                    r=[kat], w=[dst_toks[i]], dma=True, tag=f"{pfx}m_st{i}")
            return go
        pend = mk_back()
    pend()



A3_FM = 0
A3_W = 16384
A3_EV = 57344
A3_QE = 59392

A4_QT = 0
A4_KT = 4096
A4_V = 12288
A4_P = 20480
A4_PT = 21504
A4_WO = 22528
A4_OT = 30720

A5_FM = 0
A5_HG = 16384
A5_W1 = 32768
A5_W2 = 40960


def phase_s3_p01(cx):
    B, em, S, I = cx.B, cx.B.emit, cx.S, cx.I
    T, NT, NG = cx.T, cx.NT, cx.NG
    TL, BC, RP = cx.TL, cx.BC, cx.RP
    tlt, bct, rpt = cx.tk["TL"], cx.tk["BC"], cx.tk["RP"]
    psA, psB = cx.psA, cx.psB
    pat, pbt = cx.tk["psA"], cx.tk["psB"]
    ctk = cx.tk["const"]
    OUTS, otk = cx.OUTS, cx.tk["OUTS"]
    ccol, cct = cx.ccol, cx.tk["ccol"]
    clkt = cx.tk["clock"]

    at = arena_phase(cx, ["fm0", "fm1", "wq", "wk", "wv", "wg", "wp",
                          "ve0", "ve1", "qe0", "qe1"])
    fm_t = [at["fm0"], at["fm1"]]
    w_t = [at["wq"], at["wk"], at["wv"], at["wg"], at["wp"]]
    ve_t = [at["ve0"], at["ve1"]]
    cx.s3 = dict(fm_t=fm_t, w_t=w_t, ve_t=ve_t,
                 qe_t=[at["qe0"], at["qe1"]])

    for wi, wname in enumerate(("bWq", "bWk", "bWv", "bWg", "bWp")):
        em("gpsimd", lambda e, wi=wi, wname=wname: e.dma_start(
            arp(cx, A3_W + wi * 8192, [[1024, C], [1, 1024]]),
            I[wname].rearrange("(c p) n -> p c n", p=P)),
            w=[w_t[wi]], dma=True, tag=f"s3_ldw{wi}")
    g1_t = load_row_tl(cx, 7, I["b_ln1g"], D, "s3_ldg1")
    bcast_row(cx, BC[0], bct[0], TL[7][0:1, :], [g1_t], D, "s3_bg1")
    b1_t = load_row_tl(cx, 7, I["b_ln1b"], D, "s3_ldb1")
    bcast_row(cx, BC[1], bct[1], TL[7][0:1, :], [b1_t], D, "s3_bb1")
    gb_t = load_row_tl(cx, 7, I["b_gb"], D, "s3_ldgb")
    bcast_row(cx, BC[2], bct[2], TL[7][0:1, :], [gb_t], D, "s3_bgb")
    pb_t = load_row_tl(cx, 7, I["b_pb"], D, "s3_ldpb")
    bcast_row(cx, BC[3], bct[3], TL[7][0:1, :], [pb_t], D, "s3_bpb")
    wc_t = cx.tk["TL"][7]
    em("sync", lambda e: e.dma_start(
        TL[7][0:1, :], I["bWc"].rearrange("a o -> o a")),
        w=[wc_t], dma=True, tag="s3_ldwc")
    wcb_t = B.alias("wcb", cx.tk["OUTS"])
    for j in range(0, D, 512):
        pi = (j // 512) % 2
        em("tensor", lambda e, j=j, pi=pi: e.matmul(
            psB[pi][:, :], cx.onesf[:], TL[7][0:1, j:j + 512],
            start=True, stop=True),
            r=[wc_t, ctk], w=[pbt[pi]], tag=f"s3_bwc{j}")
        em("vector", lambda e, j=j, pi=pi: e.tensor_copy(
            OUTS[:, 1024 + j:1024 + j + 512], psB[pi][:, :]),
            r=[pbt[pi]], w=[wcb_t, pbt[pi]], tag=f"s3_bwce{j}")
    cb_t = cx.tk["TL"][7]
    em("sync", lambda e: e.dma_start(
        TL[7][0:1, 0:1], I["b_cb"].rearrange("(o a) -> o a", o=1)),
        w=[cb_t], dma=True, tag="s3_ldcb")
    em("tensor", lambda e: e.matmul(
        psB[0][:, 0:1], cx.onesf[:], TL[7][0:1, 0:1], start=True, stop=True),
        r=[cb_t, ctk], w=[pbt[0]], tag="s3_bcb")
    em("vector", lambda e: e.tensor_copy(ccol[2][:, 0:1], psB[0][:, 0:1]),
       r=[pbt[0]], w=[cct[2]], tag="s3_bcbe")
    cx.s3["wcb_t"] = wcb_t

    if getattr(phase_s3_p01, "DBG_BC", False):
        for bi in range(4):
            em("gpsimd", lambda e, bi=bi: e.dma_start(
                S["dbgbc"][bi * P:(bi + 1) * P, :], BC[bi][:]),
                r=[bct[bi]], w=(), dma=True, tag=f"dbgbc{bi}")
    stok = ln_stats(cx, S["x2"], 0, NT, "s3n", src_toks=cx.dt["x2"],
                    fold_rms="both")
    if "dbgst" in getattr(build_program, "debug_outputs", ()):
        for k, stn in enumerate(("stC", "stD", "stE", "stF")):
            st_ = getattr(cx, stn)
            em("gpsimd", lambda e, k=k, st_=st_: e.dma_start(
                S["dbgst"][k * P:(k + 1) * P, :], st_[:]),
                r=[stok], w=(), dma=True, tag=f"dbgst{k}")
    v_, u_ = TL[4], TL[5]
    vt_, ut_ = tlt[4], tlt[5]
    sl = slice(0, NT)
    for i in range(NT):
        b = i % 2
        g, gb = i // 4, (i // 4) % 2
        xt, xtt = TL[b], tlt[b]
        tt_, ttt = TL[2 + b], tlt[2 + b]
        xnf, xnft = TL[4 + b], tlt[4 + b]
        em("sync", lambda e, i=i, xt=xt: e.dma_start(
            xt[:], S["x2"][i * P:(i + 1) * P, :]),
            r=[cx.dt["x2"][i]], w=[xtt], dma=True, tag=f"s3a_ld{i}")
        em("vector", lambda e, i=i, xt=xt, tt_=tt_: e.tensor_scalar(
            tt_[:], xt[:], cx.stC[:, i:i + 1], cx.stD[:, i:i + 1],
            op0=ALU.mult, op1=ALU.add),
            r=[xtt, stok], w=[ttt], tag=f"s3a_t{i}")
        em("gpsimd", lambda e, i=i, tt_=tt_: e.dma_start(
            S["tb"][i * P:(i + 1) * P, :], tt_[:]),
            r=[ttt], w=[cx.dt["tb"][i]], dma=True, tag=f"s3a_tst{i}")
        em("vector", lambda e, i=i, xt=xt, xnf=xnf: e.tensor_scalar(
            xnf[:], xt[:], cx.stE[:, i:i + 1], cx.stF[:, i:i + 1],
            op0=ALU.mult, op1=ALU.add),
            r=[xtt, stok], w=[xnft], tag=f"s3a_x0{i}")
        em("vector", lambda e, xnf=xnf: e.tensor_mul(xnf[:], xnf[:], BC[0][:]),
           r=[xnft, bct[0]], w=[xnft], tag=f"s3a_xg{i}")
        em("vector", lambda e, xnf=xnf: e.tensor_add(xnf[:], xnf[:], BC[1][:]),
           r=[xnft, bct[1]], w=[xnft], tag=f"s3a_xb{i}")
        dbg_xn = getattr(phase_s3_p01, "DBG_XN", ())
        if i in dbg_xn:
            k = list(dbg_xn).index(i)
            em("gpsimd", lambda e, k=k, xnf=xnf: e.dma_start(
                S["dbgbc"][k * P:(k + 1) * P, :], xnf[:]),
                r=[xnft], w=(), dma=True, tag=f"dbgxn{i}")
        if not getattr(phase_s3_p01, "SKIP_CLOCK", False):
            em("vector", lambda e, xnf=xnf: e.tensor_mul(
                TL[6][:], xnf[:], OUTS[:, 1024:2048]),
                r=[xnft, wcb_t], w=[tlt[6]], tag=f"s3a_ck{i}")
            em("vector", lambda e, i=i: e.reduce_sum(
                cx.clockc[:, i:i + 1], TL[6][:], axis=AX.X),
                r=[tlt[6]], w=[clkt], tag=f"s3a_ck2{i}")
        else:
            em("gpsimd", lambda e, i=i: e.memset(cx.clockc[:, i:i + 1], 0.5),
               r=(), w=[clkt], tag=f"s3a_ckz{i}")
        em("tensor", lambda e, xnf=xnf: [
            e.transpose(psA[:, 1024 + c * P:1024 + (c + 1) * P],
                        xnf[:, c * P:(c + 1) * P], cx.identf[:])
            for c in range(C)][-1],
            r=[xnft, ctk], w=[pat[2], pat[3]], tag=f"s3a_tr{i}")
        for hh in range(2):
            em("scalar", lambda e, i=i, gb=gb, hh=hh: e.copy(
                bass.AP(cx.AR, A3_FM + gb * 8192 + (4 * hh) * 1024 + (i % 4) * P,
                        [[PITCH, P], [1024, 4], [1, P]]),
                psA[:, (2 + hh) * 512:(3 + hh) * 512]
                .rearrange("p (a b2) -> p a b2", a=4)),
                r=[pat[2 + hh]], w=[fm_t[gb], pat[2 + hh]],
                tag=f"s3a_ev{i}_{hh}")
        if i % 4 == 3:
            _s3_proj_group(cx, g, gb)


def _s3_proj_group(cx, g, gb):
    B, em, S, I = cx.B, cx.B.emit, cx.S, cx.I
    TL, BC, RP = cx.TL, cx.BC, cx.RP
    tlt, bct, rpt = cx.tk["TL"], cx.tk["BC"], cx.tk["RP"]
    psA, psB = cx.psA, cx.psB
    pat, pbt = cx.tk["psA"], cx.tk["psB"]
    ctk = cx.tk["const"]
    OUTS, otk = cx.OUTS, cx.tk["OUTS"]
    ccol, cct = cx.ccol, cx.tk["ccol"]
    d = cx.s3
    fm_t, w_t, ve_t = d["fm_t"], d["w_t"], d["ve_t"]

    def fm_lhs(c, j0, n=P):
        return arp(cx, A3_FM + gb * 8192 + c * 1024 + j0, [[1, n]])

    for ii in range(4):
        i = g * 4 + ii
        b2 = ii % 2
        em("sync", lambda e, i=i, b2=b2: e.dma_start(
            RP[:, b2 * 1024:b2 * 1024 + 512], I["rc"][i * P:(i + 1) * P, :]),
            w=[rpt[b2]], dma=True, tag=f"s3p_rc{i}")
        em("sync", lambda e, i=i, b2=b2: e.dma_start(
            RP[:, b2 * 1024 + 512:b2 * 1024 + 1024],
            I["rs"][i * P:(i + 1) * P, :]),
            w=[rpt[b2]], dma=True, tag=f"s3p_rs{i}")
        for which, wi, dst in (("q", 0, "qa"), ("k", 1, "ka")):
            em("tensor", lambda e, ii=ii, wi=wi: [
                e.matmul(psB[nb][:, :], fm_lhs(c, ii * P),
                         arp(cx, A3_W + wi * 8192 + c * 1024 + nb * 512,
                             [[1, 512]]),
                         start=(c == 0), stop=(c == C - 1))
                for nb in range(2) for c in range(C)][-1],
                r=[w_t[wi], fm_t[gb]], w=[pbt[0], pbt[1]],
                tag=f"s3p_{which}mm{i}")
            em("scalar", lambda e: [
                e.copy(TL[6][:, nb * 512:(nb + 1) * 512], psB[nb][:, :])
                for nb in range(2)][-1],
                r=[pbt[0], pbt[1]], w=[tlt[6], pbt[0], pbt[1]],
                tag=f"s3p_{which}ev{i}")
            em("scalar", lambda e: e.activation(
                TL[7][:], TL[6][:], AF.Square, bias=0.0, scale=1.0,
                accum_out=ccol[0][:, 0:1]),
                r=[tlt[6]], w=[tlt[7], cct[0]], tag=f"s3p_{which}sq{i}")
            em("vector", lambda e: e.tensor_scalar_max(
                ccol[0][:, 0:1], ccol[0][:, 0:1], 1e-24),
                r=[cct[0]], w=[cct[0]], tag=f"s3p_{which}mx{i}")
            em("scalar", lambda e: e.activation(
                ccol[0][:, 1:2], ccol[0][:, 0:1], AF.Ln, bias=0.0, scale=1.0),
                r=[cct[0]], w=[cct[0]], tag=f"s3p_{which}ln{i}")
            em("scalar", lambda e: e.activation(
                ccol[0][:, 1:2], ccol[0][:, 1:2], AF.Exp, bias=0.0, scale=-0.5),
                r=[cct[0]], w=[cct[0]], tag=f"s3p_{which}ex{i}")
            em("vector", lambda e: e.tensor_scalar(
                TL[6][:], TL[6][:], ccol[0][:, 1:2], None, op0=ALU.mult),
                r=[tlt[6], cct[0]], w=[tlt[6]], tag=f"s3p_{which}n{i}")
            em("vector", lambda e, b2=b2: e.tensor_mul(
                TL[5][:, 0:512], TL[6][:, 0:D:2],
                RP[:, b2 * 1024:b2 * 1024 + 512]),
               r=[tlt[6], rpt[b2]], w=[tlt[5]], tag=f"s3p_{which}r1{i}")
            em("vector", lambda e, b2=b2: e.tensor_mul(
                TL[5][:, 512:1024], TL[6][:, 1:D:2],
                RP[:, b2 * 1024 + 512:b2 * 1024 + 1024]),
               r=[tlt[6], rpt[b2]], w=[tlt[5]], tag=f"s3p_{which}r2{i}")
            em("vector", lambda e: e.tensor_sub(
                TL[7][:, 0:D:2], TL[5][:, 0:512], TL[5][:, 512:1024]),
                r=[tlt[5]], w=[tlt[7]], tag=f"s3p_{which}r3{i}")
            em("vector", lambda e, b2=b2: e.tensor_mul(
                TL[5][:, 0:512], TL[6][:, 0:D:2],
                RP[:, b2 * 1024 + 512:b2 * 1024 + 1024]),
               r=[tlt[6], rpt[b2]], w=[tlt[5]], tag=f"s3p_{which}r4{i}")
            em("vector", lambda e, b2=b2: e.tensor_mul(
                TL[5][:, 512:1024], TL[6][:, 1:D:2],
                RP[:, b2 * 1024:b2 * 1024 + 512]),
               r=[tlt[6], rpt[b2]], w=[tlt[5]], tag=f"s3p_{which}r5{i}")
            em("vector", lambda e: e.tensor_add(
                TL[7][:, 1:D:2], TL[5][:, 0:512], TL[5][:, 512:1024]),
                r=[tlt[5]], w=[tlt[7]], tag=f"s3p_{which}r6{i}")
            em("gpsimd", lambda e, i=i, dst=dst: e.dma_start(
                S[dst][i * P:(i + 1) * P, :], TL[7][:]),
                r=[tlt[7]], w=[cx.dt[dst][i]], dma=True,
                tag=f"s3p_{which}st{i}")
        em("tensor", lambda e, ii=ii: [
            e.matmul(pA(cx, js), fm_lhs(c, ii * P),
                     arp(cx, A3_W + 2 * 8192 + c * 1024 + (js - 2) * 512,
                         [[1, 512]]),
                     start=(c == 0), stop=(c == C - 1))
            for js in (2, 3) for c in range(C)][-1],
            r=[w_t[2], fm_t[gb]], w=[pat[2], pat[3]], tag=f"s3p_vmm{i}")
        em("scalar", lambda e, b2=b2: e.copy(
            arp(cx, A3_EV + b2 * 1024, [[1, 1024]]), psA[:, 1024:2048]),
            r=[pat[2], pat[3]], w=[ve_t[b2], pat[2], pat[3]],
            tag=f"s3p_vev{i}")
        em("gpsimd", lambda e, i=i, b2=b2: e.dma_start(
            S["bV"][i * P:(i + 1) * P, :], arp(cx, A3_EV + b2 * 1024,
                                               [[1, 1024]])),
            r=[ve_t[b2]], w=[cx.dt["bV"][i]], dma=True, tag=f"s3p_vst{i}")
        em("tensor", lambda e, ii=ii: [
            e.matmul(psB[nb][:, :], fm_lhs(c, ii * P),
                     arp(cx, A3_W + 3 * 8192 + c * 1024 + nb * 512, [[1, 512]]),
                     start=(c == 0), stop=(c == C - 1))
            for nb in range(2) for c in range(C)][-1],
            r=[w_t[3], fm_t[gb]], w=[pbt[0], pbt[1]], tag=f"s3p_gmm{i}")
        em("vector", lambda e: [
            e.tensor_add(TL[6][:, nb * 512:(nb + 1) * 512],
                         psB[nb][:, :], BC[2][:, nb * 512:(nb + 1) * 512])
            for nb in range(2)][-1],
            r=[pbt[0], pbt[1], bct[2]], w=[tlt[6], pbt[0], pbt[1]],
            tag=f"s3p_gb{i}")
        em("vector", lambda e: e.tensor_scalar_max(TL[7][:], TL[6][:], 0.0),
           r=[tlt[6]], w=[tlt[7]], tag=f"s3p_grl{i}")
        em("scalar", lambda e: e.activation(
            TL[5][:], TL[6][:], AF.Abs, bias=0.0, scale=1.0),
            r=[tlt[6]], w=[tlt[5]], tag=f"s3p_gab{i}")
        em("scalar", lambda e: e.activation(TL[5][:], TL[5][:], AF.Exp,
                                            bias=0.0, scale=-1.0),
           r=[tlt[5]], w=[tlt[5]], tag=f"s3p_gex{i}")
        em("scalar", lambda e: e.activation(TL[5][:], TL[5][:], AF.Ln,
                                            bias=cx.onescol[:], scale=1.0),
           r=[tlt[5], ctk], w=[tlt[5]], tag=f"s3p_gln{i}")
        em("vector", lambda e: e.tensor_add(TL[7][:], TL[7][:], TL[5][:]),
           r=[tlt[5], tlt[7]], w=[tlt[7]], tag=f"s3p_gcm0{i}")
        em("vector", lambda e: e.tensor_scalar_mul(TL[7][:], TL[7][:], -1.0),
           r=[tlt[7]], w=[tlt[7]], tag=f"s3p_gcm{i}")
        em("gpsimd", lambda e, i=i: e.dma_start(
            S["gj"][i * P:(i + 1) * P, :], TL[7][:]),
            r=[tlt[7]], w=[cx.dt["gj"][i]], dma=True, tag=f"s3p_gst{i}")
        em("tensor", lambda e, ii=ii: [
            e.matmul(psB[nb][:, :], fm_lhs(c, ii * P),
                     arp(cx, A3_W + 4 * 8192 + c * 1024 + nb * 512, [[1, 512]]),
                     start=(c == 0), stop=(c == C - 1))
            for nb in range(2) for c in range(C)][-1],
            r=[w_t[4], fm_t[gb]], w=[pbt[0], pbt[1]], tag=f"s3p_pmm{i}")
        em("vector", lambda e: [
            e.tensor_add(TL[6][:, nb * 512:(nb + 1) * 512],
                         psB[nb][:, :], BC[3][:, nb * 512:(nb + 1) * 512])
            for nb in range(2)][-1],
            r=[pbt[0], pbt[1], bct[3]], w=[tlt[6], pbt[0], pbt[1]],
            tag=f"s3p_pb{i}")
        if i == 0:
            em("vector", lambda e: e.tensor_copy(OUTS[:, 0:1024], TL[6][:]),
               r=[tlt[6]], w=[otk], tag=f"s3p_pm0")
        else:
            em("vector", lambda e: e.tensor_max(OUTS[:, 0:1024],
                                                OUTS[:, 0:1024], TL[6][:]),
               r=[tlt[6], otk], w=[otk], tag=f"s3p_pm{i}")
        em("gpsimd", lambda e, i=i: e.dma_start(
            S["pp"][i * P:(i + 1) * P, :], TL[6][:]),
            r=[tlt[6]], w=[cx.dt["pp"][i]], dma=True, tag=f"s3p_pst{i}")


def phase_s3_p2(cx):
    B, em, S = cx.B, cx.B.emit, cx.S
    NT = cx.NT
    TL, BC = cx.TL, cx.BC
    tlt, bct = cx.tk["TL"], cx.tk["BC"]
    psA = cx.psA
    pat = cx.tk["psA"]
    ctk = cx.tk["const"]
    OUTS, otk = cx.OUTS, cx.tk["OUTS"]
    ccol, cct = cx.ccol, cx.tk["ccol"]
    clkt = cx.tk["clock"]

    for c in range(C):
        pb = c % 2
        em("tensor", lambda e, c=c, pb=pb: e.transpose(
            psA[:, pb * 512:pb * 512 + P],
            OUTS[:, c * P:(c + 1) * P], cx.identf[:]),
            r=[otk, ctk], w=[pat[pb]], tag=f"s3x_tr{c}")
        em("vector", lambda e, c=c, pb=pb: e.reduce_max(
            ccol[3][:, c:c + 1], psA[:, pb * 512:pb * 512 + P], axis=AX.X),
            r=[pat[pb]], w=[cct[3], pat[pb]], tag=f"s3x_mx{c}")
        em("gpsimd", lambda e, c=c: e.dma_start(
            S["brow"].rearrange("(a o) -> a o", o=1)[c * P:(c + 1) * P, :],
            ccol[3][:, c:c + 1]),
            r=[cct[3]], w=[cx.dt["brow"][0]], dma=True, tag=f"s3x_st{c}")
    pm_t = cx.tk["TL"][7]
    em("sync", lambda e: e.dma_start(
        TL[7][0:1, :], S["brow"].rearrange("(o a) -> o a", o=1)[:, 0:D]),
        r=[cx.dt["brow"][0]], w=[pm_t], dma=True, tag="s3x_ldr")
    bcast_row(cx, BC[0], bct[0], TL[7][0:1, :], [pm_t], D, "s3x_b")

    sl = slice(0, NT)
    stt_ = cx.tk["stats"]
    em("vector", lambda e: e.tensor_scalar(
        cx.clockc[:, sl], cx.clockc[:, sl], ccol[2][:, 0:1], None,
        op0=ALU.add),
        r=[clkt, cct[2]], w=[clkt], tag="s3c_cb")
    em("vector", lambda e: e.tensor_scalar_max(cx.stA[:, sl],
                                               cx.clockc[:, sl], 0.0),
       r=[clkt], w=[stt_], tag="s3c_rl")
    em("scalar", lambda e: e.activation(
        cx.stB[:, sl], cx.clockc[:, sl], AF.Abs, bias=0.0, scale=1.0),
        r=[clkt], w=[stt_], tag="s3c_ab")
    em("scalar", lambda e: e.activation(cx.stB[:, sl], cx.stB[:, sl], AF.Exp,
                                        bias=0.0, scale=-1.0),
       r=[stt_], w=[stt_], tag="s3c_ex")
    em("scalar", lambda e: e.activation(cx.stB[:, sl], cx.stB[:, sl], AF.Ln,
                                        bias=cx.onescol[:], scale=1.0),
       r=[stt_, ctk], w=[stt_], tag="s3c_ln")
    em("vector", lambda e: e.tensor_add(
        cx.clockc[:, sl], cx.stA[:, sl], cx.stB[:, sl]),
        r=[stt_], w=[clkt], tag="s3c_fin")


def phase_s3_p3(cx):
    B, em, S = cx.B, cx.B.emit, cx.S
    NT = cx.NT
    TL, BC = cx.TL, cx.BC
    tlt, bct = cx.tk["TL"], cx.tk["BC"]
    psA, psB = cx.psA, cx.psB
    pat, pbt = cx.tk["psA"], cx.tk["psB"]
    ctk = cx.tk["const"]
    cart = cx.tk["carry"]
    clkt = cx.tk["clock"]
    d = cx.s3
    qe_t = d["qe_t"]

    em("gpsimd", lambda e: e.memset(cx.carry[:], 0.0), w=[cart], tag="s3q_c0")

    for i in range(NT):
        b = i % 2
        gjt_, gjtt = TL[0], tlt[0]
        ppt_, pptt = TL[1], tlt[1]
        qat_, qatt = TL[2], tlt[2]
        kat_, katt = TL[3], tlt[3]
        em("sync", lambda e, i=i: e.dma_start(
            gjt_[:], S["gj"][i * P:(i + 1) * P, :]),
            r=[cx.dt["gj"][i]], w=[gjtt], dma=True, tag=f"s3q_lg{i}")
        em("sync", lambda e, i=i: e.dma_start(
            ppt_[:], S["pp"][i * P:(i + 1) * P, :]),
            r=[cx.dt["pp"][i]], w=[pptt], dma=True, tag=f"s3q_lp{i}")
        em("sync", lambda e, i=i: e.dma_start(
            qat_[:], S["qa"][i * P:(i + 1) * P, :]),
            r=[cx.dt["qa"][i]], w=[qatt], dma=True, tag=f"s3q_lq{i}")
        em("sync", lambda e, i=i: e.dma_start(
            kat_[:], S["ka"][i * P:(i + 1) * P, :]),
            r=[cx.dt["ka"][i]], w=[katt], dma=True, tag=f"s3q_lk{i}")
        em("vector", lambda e: e.tensor_sub(TL[4][:], ppt_[:], BC[0][:]),
           r=[pptt, bct[0]], w=[tlt[4]], tag=f"s3q_ps{i}")
        em("scalar", lambda e: e.activation(TL[4][:], TL[4][:], AF.Exp,
                                            bias=0.0, scale=1.0),
           r=[tlt[4]], w=[tlt[4]], tag=f"s3q_pe{i}")
        em("vector", lambda e, i=i: e.tensor_scalar(
            TL[4][:], TL[4][:], cx.clockc[:, i:i + 1], None, op0=ALU.mult),
            r=[tlt[4], clkt], w=[tlt[4]], tag=f"s3q_pc{i}")
        em("tensor", lambda e: [
            [e.matmul(psB[nb][:, :], cx.tri_f[:],
                      gjt_[:, nb * 512:(nb + 1) * 512], start=True, stop=False),
             e.matmul(psB[nb][:, :], cx.onesf[:],
                      cx.carry[0:1, nb * 512:(nb + 1) * 512], start=False,
                      stop=True)][-1]
            for nb in range(2)][-1],
            r=[gjtt, cart, ctk], w=[pbt[0], pbt[1]], tag=f"s3q_gc{i}")
        em("tensor", lambda e: [
            e.matmul(psA[0:1, nb * 512:(nb + 1) * 512], cx.onescol[:],
                     gjt_[:, nb * 512:(nb + 1) * 512], start=True, stop=True)
            for nb in range(2)][-1],
            r=[gjtt, ctk], w=[pat[0], pat[1]], tag=f"s3q_gs{i}")
        em("vector", lambda e: e.tensor_add(
            cx.carry[0:1, 0:1024], cx.carry[0:1, 0:1024], psA[0:1, 0:1024]),
            r=[pat[0], pat[1], cart], w=[cart, pat[0], pat[1]],
            tag=f"s3q_gu{i}")
        em("vector", lambda e: [
            e.tensor_scalar(TL[5][:, nb * 512:(nb + 1) * 512], psB[nb][:, :],
                            50.0, -60.0, op0=ALU.min, op1=ALU.max)
            for nb in range(2)][-1],
            r=[pbt[0], pbt[1]], w=[tlt[5], pbt[0], pbt[1]],
            tag=f"s3q_gl{i}")
        em("scalar", lambda e: e.activation(TL[5][:], TL[5][:], AF.Exp,
                                            bias=0.0, scale=1.0),
           r=[tlt[5]], w=[tlt[5]], tag=f"s3q_ge{i}")
        em("tensor", lambda e: [
            [e.matmul(psB[nb][:, :], cx.tri_f[:],
                      TL[4][:, nb * 512:(nb + 1) * 512], start=True,
                      stop=False),
             e.matmul(psB[nb][:, :], cx.onesf[:],
                      cx.carry[0:1, 1024 + nb * 512:1024 + (nb + 1) * 512],
                      start=False, stop=True)][-1]
            for nb in range(2)][-1],
            r=[tlt[4], cart, ctk], w=[pbt[0], pbt[1]], tag=f"s3q_pq{i}")
        em("tensor", lambda e: [
            e.matmul(psA[0:1, nb * 512:(nb + 1) * 512], cx.onescol[:],
                     TL[4][:, nb * 512:(nb + 1) * 512], start=True, stop=True)
            for nb in range(2)][-1],
            r=[tlt[4], ctk], w=[pat[0], pat[1]], tag=f"s3q_ks{i}")
        em("vector", lambda e: e.tensor_add(
            cx.carry[0:1, 1024:2048], cx.carry[0:1, 1024:2048],
            psA[0:1, 0:1024]),
            r=[pat[0], pat[1], cart], w=[cart, pat[0], pat[1]],
            tag=f"s3q_ku{i}")
        em("vector", lambda e: [
            e.tensor_scalar_add(TL[6][:, nb * 512:(nb + 1) * 512],
                                psB[nb][:, :], 1e-8)
            for nb in range(2)][-1],
            r=[pbt[0], pbt[1]], w=[tlt[6], pbt[0], pbt[1]],
            tag=f"s3q_pa{i}")
        em("scalar", lambda e: e.activation(TL[6][:], TL[6][:], AF.Ln,
                                            bias=0.0, scale=1.0),
           r=[tlt[6]], w=[tlt[6]], tag=f"s3q_pr0{i}")
        em("scalar", lambda e: e.activation(TL[6][:], TL[6][:], AF.Exp,
                                            bias=0.0, scale=-1.0),
           r=[tlt[6]], w=[tlt[6]], tag=f"s3q_pr{i}")
        em("vector", lambda e: e.tensor_mul(qat_[:], qat_[:], TL[5][:]),
           r=[qatt, tlt[5]], w=[qatt], tag=f"s3q_q1{i}")
        em("vector", lambda e: e.tensor_mul(qat_[:], qat_[:], TL[6][:]),
           r=[qatt, tlt[6]], w=[qatt], tag=f"s3q_q2{i}")
        em("vector", lambda e: e.tensor_scalar_add(TL[7][:], TL[5][:], 1e-8),
           r=[tlt[5]], w=[tlt[7]], tag=f"s3q_k1{i}")
        em("scalar", lambda e: e.activation(TL[7][:], TL[7][:], AF.Ln,
                                            bias=0.0, scale=1.0),
           r=[tlt[7]], w=[tlt[7]], tag=f"s3q_k20{i}")
        em("scalar", lambda e: e.activation(TL[7][:], TL[7][:], AF.Exp,
                                            bias=0.0, scale=-1.0),
           r=[tlt[7]], w=[tlt[7]], tag=f"s3q_k2{i}")
        em("vector", lambda e: e.tensor_mul(kat_[:], kat_[:], TL[4][:]),
           r=[katt, tlt[4]], w=[katt], tag=f"s3q_k3{i}")
        em("vector", lambda e: e.tensor_mul(kat_[:], kat_[:], TL[7][:]),
           r=[katt, tlt[7]], w=[katt], tag=f"s3q_k4{i}")
        for which, tile, ttok, dst in (("q", qat_, qatt, "bQT"),
                                       ("k", kat_, katt, "bKT")):
            em("tensor", lambda e, tile=tile, ttok=ttok: [
                e.transpose(psA[:, 1024 + c * P:1024 + (c + 1) * P],
                            tile[:, c * P:(c + 1) * P], cx.identf[:])
                for c in range(C)][-1],
                r=[ttok, ctk], w=[pat[2], pat[3]], tag=f"s3q_{which}t{i}")
            em("scalar", lambda e, b=b: e.copy(
                arp(cx, A3_QE + b * 1024, [[1, 1024]]), psA[:, 1024:2048]),
                r=[pat[2], pat[3]], w=[qe_t[b], pat[2], pat[3]],
                tag=f"s3q_{which}e{i}")
            em("gpsimd", lambda e, i=i, b=b, dst=dst: e.dma_start(
                S[dst].rearrange("(c p) t -> p c t", p=P)[:, :,
                                                          i * P:(i + 1) * P],
                arp(cx, A3_QE + b * 1024, [[128, C], [1, P]])),
                r=[qe_t[b]], w=[cx.dt[dst][i]], dma=True,
                tag=f"s3q_{which}s{i}")


def phase_s3_p4(cx):
    B, em, S, I = cx.B, cx.B.emit, cx.S, cx.I
    NG = cx.NG
    TL, BC, RP = cx.TL, cx.BC, cx.RP
    tlt, bct, rpt = cx.tk["TL"], cx.tk["BC"], cx.tk["RP"]
    psA, psB, ptA = cx.psA, cx.psB, cx.ptA
    pat, pbt, ptt = cx.tk["psA"], cx.tk["psB"], cx.tk["ptA"]
    ctk = cx.tk["const"]
    OUTS, otk = cx.OUTS, cx.tk["OUTS"]
    ccol, cct = cx.ccol, cx.tk["ccol"]

    at = arena_phase(cx, ["qt", "kt0", "kt1", "v0", "v1", "pp0", "pp1",
                          "pt0", "pt1", "wo", "ot0", "ot1"])
    qt_t = at["qt"]
    kt_t = [at["kt0"], at["kt1"]]
    v_t = [at["v0"], at["v1"]]
    pp_t = [at["pp0"], at["pp1"]]
    pt_t = [at["pt0"], at["pt1"]]
    wo_t = at["wo"]
    ot_t = [at["ot0"], at["ot1"]]
    otk = B.alias("o4", otk, cx.s3["wcb_t"])
    cx.tk["OUTS"] = otk

    em("gpsimd", lambda e: e.memset(RP[:], 0.0), w=[rpt[0], rpt[1]],
       tag="s4m_0")
    em("gpsimd", lambda e: e.affine_select(
        out=RP.rearrange("p (a j) -> p a j", a=4),
        in_=RP.rearrange("p (a j) -> p a j", a=4),
        compare_op=ALU.is_ge, fill=-1e30,
        base=0, pattern=[[P, 4], [-1, 512]], channel_multiplier=1),
        r=[rpt[0], rpt[1]], w=[rpt[0], rpt[1]], tag="s4m_1")

    em("gpsimd", lambda e: e.dma_start(
        arp(cx, A4_WO, [[1024, C], [1, 1024]]),
        I["bWo"].rearrange("(c p) n -> p c n", p=P)),
        w=[wo_t], dma=True, tag="s4_ldwo")
    ob_t = load_row_tl(cx, 7, I["b_ob"], D, "s4_ldob")
    bcast_row(cx, BC[1], bct[1], TL[7][0:1, :], [ob_t], D, "s4_bob")

    for qt in range(NG):
        em("sync", lambda e, qt=qt: e.dma_start(
            arp(cx, A4_QT, [[512, C], [1, 512]]),
            S["bQT"][:, qt * 512:(qt + 1) * 512]
            .rearrange("(c p) n -> p c n", p=P)),
            r=[cx.dt["bQT"][qt * 4 + k2] for k2 in range(4)],
            w=[qt_t], dma=True, tag=f"s4_q{qt}")
        for jg in range(qt + 1):
            bj = jg % 2
            em("sync", lambda e, jg=jg, bj=bj: e.dma_start(
                arp(cx, A4_KT + bj * 4096, [[512, C], [1, 512]]),
                S["bKT"][:, jg * 512:(jg + 1) * 512]
                .rearrange("(c p) n -> p c n", p=P)),
                r=[cx.dt["bKT"][jg * 4 + k2] for k2 in range(4)],
                w=[kt_t[bj]], dma=True, tag=f"s4_k{qt}_{jg}")
            em("gpsimd", lambda e, jg=jg, bj=bj: e.dma_start(
                arp(cx, A4_V + bj * 4096, [[1024, 4], [1, 1024]]),
                S["bV"].rearrange("(a p) n -> p a n", p=P)
                [:, jg * 4:(jg + 1) * 4, :]),
                r=[cx.dt["bV"][jg * 4 + k2] for k2 in range(4)],
                w=[v_t[bj]], dma=True, tag=f"s4_v{qt}_{jg}")
            for qs in range(4):
                pa = qs % 4
                par = jg % 2
                bb = (qs + jg) % 2
                mslot = qs * 2
                em("tensor", lambda e, qs=qs, bj=bj, pa=pa: [
                    e.matmul(pA(cx, pa),
                             arp(cx, A4_QT + c * 512 + qs * P, [[1, P]]),
                             arp(cx, A4_KT + bj * 4096 + c * 512, [[1, 512]]),
                             start=(c == 0), stop=(c == C - 1))
                    for c in range(C)][-1],
                    r=[qt_t, kt_t[bj]], w=[pat[pa]],
                    tag=f"s4_s{qt}_{jg}_{qs}")
                if jg == qt:
                    em("vector", lambda e, qs=qs, pa=pa: e.tensor_add(
                        pA(cx, pa), pA(cx, pa),
                        RP[:, qs * 512:(qs + 1) * 512]),
                        r=[pat[pa], rpt[0], rpt[1]], w=[pat[pa]],
                        tag=f"s4_m{qt}_{jg}_{qs}")
                em("vector", lambda e, qs=qs, pa=pa: e.reduce_max(
                    ccol[2][:, qs:qs + 1], pA(cx, pa), axis=AX.X),
                    r=[pat[pa]], w=[cct[2]], tag=f"s4_rm{qt}_{jg}_{qs}")
                if jg == 0:
                    em("vector", lambda e, qs=qs, par=par: e.tensor_copy(
                        ccol[0][:, qs * 2 + par:qs * 2 + par + 1],
                        ccol[2][:, qs:qs + 1]),
                        r=[cct[2]], w=[cct[0]], tag=f"s4_m0{qt}_{jg}_{qs}")
                else:
                    em("vector", lambda e, qs=qs, par=par: e.tensor_max(
                        ccol[0][:, qs * 2 + par:qs * 2 + par + 1],
                        ccol[0][:, qs * 2 + (1 - par):qs * 2 + (2 - par)],
                        ccol[2][:, qs:qs + 1]),
                        r=[cct[2], cct[0]], w=[cct[0]],
                        tag=f"s4_mn{qt}_{jg}_{qs}")
                    em("vector", lambda e, qs=qs, par=par: e.tensor_sub(
                        ccol[1][:, 4 + qs:5 + qs],
                        ccol[0][:, qs * 2 + (1 - par):qs * 2 + (2 - par)],
                        ccol[0][:, qs * 2 + par:qs * 2 + par + 1]),
                        r=[cct[0]], w=[cct[1]], tag=f"s4_al{qt}_{jg}_{qs}")
                    em("scalar", lambda e, qs=qs: e.activation(
                        ccol[1][:, 4 + qs:5 + qs], ccol[1][:, 4 + qs:5 + qs],
                        AF.Exp, bias=0.0, scale=SC_B),
                        r=[cct[1]], w=[cct[1]], tag=f"s4_ae{qt}_{jg}_{qs}")
                em("vector", lambda e, qs=qs, par=par: e.tensor_scalar_mul(
                    ccol[3][:, qs:qs + 1],
                    ccol[0][:, qs * 2 + par:qs * 2 + par + 1], -SC_B),
                    r=[cct[0]], w=[cct[3]], tag=f"s4_nb{qt}_{jg}_{qs}")
                em("scalar", lambda e, qs=qs, pa=pa, bb=bb: e.activation(
                    arp(cx, A4_P + bb * 512, [[1, 512]]), pA(cx, pa), AF.Exp,
                    bias=ccol[3][:, qs:qs + 1], scale=SC_B,
                    accum_out=ccol[2][:, 4 + qs:5 + qs]),
                    r=[pat[pa], cct[3]], w=[pp_t[bb], cct[2], pat[pa]],
                    tag=f"s4_e{qt}_{jg}_{qs}")
                if jg == 0:
                    em("vector", lambda e, qs=qs: e.tensor_copy(
                        ccol[1][:, qs:qs + 1], ccol[2][:, 4 + qs:5 + qs]),
                        r=[cct[2]], w=[cct[1]], tag=f"s4_l0{qt}_{jg}_{qs}")
                else:
                    em("vector", lambda e, qs=qs: e.tensor_scalar(
                        ccol[1][:, qs:qs + 1], ccol[1][:, qs:qs + 1],
                        ccol[1][:, 4 + qs:5 + qs], ccol[2][:, 4 + qs:5 + qs],
                        op0=ALU.mult, op1=ALU.add),
                        r=[cct[2], cct[1]], w=[cct[1]],
                        tag=f"s4_lu{qt}_{jg}_{qs}")
                em("tensor", lambda e, bb=bb: [
                    e.transpose(ptA[bb][:, s * P:(s + 1) * P],
                                arp(cx, A4_P + bb * 512 + s * P, [[1, P]]),
                                cx.ident[:])
                    for s in range(4)][-1],
                    r=[pp_t[bb], ctk], w=[ptt[bb]], tag=f"s4_t{qt}_{jg}_{qs}")
                em("vector", lambda e, bb=bb: e.tensor_copy(
                    arp(cx, A4_PT + bb * 512, [[1, 512]]), ptA[bb][:, 0:512]),
                    r=[ptt[bb]], w=[pt_t[bb], ptt[bb]],
                    tag=f"s4_te{qt}_{jg}_{qs}")
                em("tensor", lambda e, bb=bb, bj=bj: [
                    e.matmul(psB[nb][:, :],
                             arp(cx, A4_PT + bb * 512 + s * P, [[1, P]]),
                             arp(cx, A4_V + bj * 4096 + s * 1024 + nb * 512,
                                 [[1, 512]]),
                             start=(s == 0), stop=(s == 3))
                    for nb in range(2) for s in range(4)][-1],
                    r=[pt_t[bb], v_t[bj]], w=[pbt[0], pbt[1]],
                    tag=f"s4_av{qt}_{jg}_{qs}")
                if jg == 0:
                    em("vector", lambda e, qs=qs: [
                        e.tensor_copy(OUTS[:, qs * 1024 + nb * 512:
                                           qs * 1024 + (nb + 1) * 512],
                                      psB[nb][:, :])
                        for nb in range(2)][-1],
                        r=[pbt[0], pbt[1]], w=[otk, pbt[0], pbt[1]],
                        tag=f"s4_o0{qt}_{jg}_{qs}")
                else:
                    em("vector", lambda e, qs=qs: e.tensor_scalar(
                        OUTS[:, qs * 1024:(qs + 1) * 1024],
                        OUTS[:, qs * 1024:(qs + 1) * 1024],
                        ccol[1][:, 4 + qs:5 + qs], None, op0=ALU.mult),
                        r=[cct[1], otk], w=[otk],
                        tag=f"s4_os{qt}_{jg}_{qs}")
                    em("vector", lambda e, qs=qs: [
                        e.tensor_add(
                            OUTS[:, qs * 1024 + nb * 512:
                                 qs * 1024 + (nb + 1) * 512],
                            OUTS[:, qs * 1024 + nb * 512:
                                 qs * 1024 + (nb + 1) * 512],
                            psB[nb][:, :])
                        for nb in range(2)][-1],
                        r=[pbt[0], pbt[1], otk],
                        w=[otk, pbt[0], pbt[1]],
                        tag=f"s4_ou{qt}_{jg}_{qs}")
        for qs in range(4):
            i = qt * 4 + qs
            bo = qs % 2
            em("scalar", lambda e, qs=qs: e.activation(
                ccol[2][:, qs:qs + 1], ccol[1][:, qs:qs + 1],
                AF.Ln, bias=0.0, scale=1.0),
                r=[cct[1]], w=[cct[2]], tag=f"s4e_r{qt}_{qs}")
            em("scalar", lambda e, qs=qs: e.activation(
                ccol[2][:, qs:qs + 1], ccol[2][:, qs:qs + 1],
                AF.Exp, bias=0.0, scale=-1.0),
                r=[cct[2]], w=[cct[2]], tag=f"s4e_r2{qt}_{qs}")
            em("vector", lambda e, qs=qs: e.tensor_scalar(
                TL[4][:], OUTS[:, qs * 1024:(qs + 1) * 1024],
                ccol[2][:, qs:qs + 1], None, op0=ALU.mult),
                r=[otk, cct[2]], w=[tlt[4]], tag=f"s4e_s{qt}_{qs}")
            em("tensor", lambda e: [
                e.transpose(psA[:, 1024 + c * P:1024 + (c + 1) * P],
                            TL[4][:, c * P:(c + 1) * P], cx.identf[:])
                for c in range(C)][-1],
                r=[tlt[4], ctk], w=[pat[2], pat[3]], tag=f"s4e_t{qt}_{qs}")
            em("scalar", lambda e, bo=bo: e.copy(
                arp(cx, A4_OT + bo * 1024, [[1, 1024]]), psA[:, 1024:2048]),
                r=[pat[2], pat[3]], w=[ot_t[bo], pat[2], pat[3]],
                tag=f"s4e_e{qt}_{qs}")
            em("tensor", lambda e, bo=bo: [
                e.matmul(psB[nb][:, :],
                         arp(cx, A4_OT + bo * 1024 + c * P, [[1, P]]),
                         arp(cx, A4_WO + c * 1024 + nb * 512, [[1, 512]]),
                         start=(c == 0), stop=(c == C - 1))
                for nb in range(2) for c in range(C)][-1],
                r=[ot_t[bo], wo_t], w=[pbt[0], pbt[1]],
                tag=f"s4e_mm{qt}_{qs}")
            em("sync", lambda e, i=i: e.dma_start(
                TL[5][:], S["tb"][i * P:(i + 1) * P, :]),
                r=[cx.dt["tb"][i]], w=[tlt[5]], dma=True,
                tag=f"s4e_lt{qt}_{qs}")
            em("vector", lambda e: [
                e.tensor_add(TL[6][:, nb * 512:(nb + 1) * 512],
                             psB[nb][:, :], BC[1][:, nb * 512:(nb + 1) * 512])
                for nb in range(2)][-1],
                r=[pbt[0], pbt[1], bct[1]], w=[tlt[6], pbt[0], pbt[1]],
                tag=f"s4e_b{qt}_{qs}")
            em("vector", lambda e: e.tensor_add(TL[6][:], TL[6][:], TL[5][:]),
               r=[tlt[5], tlt[6]], w=[tlt[6]], tag=f"s4e_a{qt}_{qs}")
            em("gpsimd", lambda e, i=i: e.dma_start(
                S["x2i"][i * P:(i + 1) * P, :], TL[6][:]),
                r=[tlt[6]], w=[cx.dt["x2i"][i]], dma=True,
                tag=f"s4e_st{qt}_{qs}")


def phase_s3_p5(cx):
    B, em, S, I = cx.B, cx.B.emit, cx.S, cx.I
    NT, NG = cx.NT, cx.NG
    TL, BC = cx.TL, cx.BC
    tlt, bct = cx.tk["TL"], cx.tk["BC"]
    psA, psB = cx.psA, cx.psB
    pat, pbt = cx.tk["psA"], cx.tk["psB"]
    ctk = cx.tk["const"]
    ccol, cct = cx.ccol, cx.tk["ccol"]

    at = arena_phase(cx, ["fm0", "fm1", "hg", "w10", "w11", "w20", "w21"])
    fm_t = [at["fm0"], at["fm1"]]
    hg_t = at["hg"]
    w1_t = [at["w10"], at["w11"]]
    w2_t = [at["w20"], at["w21"]]

    g2_t = load_row_tl(cx, 7, I["b_ln2g"], D, "s5_ldg2")
    bcast_row(cx, BC[2], bct[2], TL[7][0:1, :], [g2_t], D, "s5_bg2")
    bb2_t = load_row_tl(cx, 7, I["b_ln2b"], D, "s5_ldb2l")
    bcast_row(cx, BC[3], bct[3], TL[7][0:1, :], [bb2_t], D, "s5_bb2l")
    b2r_t = load_row_tl(cx, 7, I["bB2"], D, "s5_ldb2")
    bcast_row(cx, BC[0], bct[0], TL[7][0:1, :], [b2r_t], D, "s5_bb2")
    b1_t = Tok("b1pp")
    em("sync", lambda e: e.dma_start(
        TL[7][0:32, 0:P], I["bB1"].rearrange("(a p) -> a p", p=P)),
        w=[cx.tk["TL"][7]], dma=True, tag="s5_ldb1")
    em("tensor", lambda e: e.transpose(
        psA[:, 0:32], TL[7][0:32, 0:P], cx.identf[0:32, 0:32]),
        r=[cx.tk["TL"][7], ctk], w=[pat[0]], tag="s5_trb1")
    em("vector", lambda e: e.tensor_copy(cx.clockc[:, 32:64], psA[:, 0:32]),
       r=[pat[0]], w=[b1_t, pat[0]], tag="s5_evb1")

    stok = ln_stats(cx, S["x2i"], 0, NT, "s5n", src_toks=cx.dt["x2i"])

    for i in range(NT):
        b = i % 2
        g, gb = i // 4, (i // 4) % 2
        xt, xtt = TL[b], tlt[b]
        hf, hft = TL[2 + b], tlt[2 + b]
        em("sync", lambda e, i=i, xt=xt: e.dma_start(
            xt[:], S["x2i"][i * P:(i + 1) * P, :]),
            r=[cx.dt["x2i"][i]], w=[xtt], dma=True, tag=f"s5a_ld{i}")
        em("vector", lambda e, i=i, xt=xt, hf=hf: e.tensor_scalar(
            hf[:], xt[:], cx.stC[:, i:i + 1], cx.stD[:, i:i + 1],
            op0=ALU.mult, op1=ALU.add),
            r=[xtt, stok], w=[hft], tag=f"s5a_sc{i}")
        em("vector", lambda e, hf=hf: e.tensor_mul(hf[:], hf[:], BC[2][:]),
           r=[hft, bct[2]], w=[hft], tag=f"s5a_g{i}")
        em("vector", lambda e, hf=hf: e.tensor_add(hf[:], hf[:], BC[3][:]),
           r=[hft, bct[3]], w=[hft], tag=f"s5a_b{i}")
        em("tensor", lambda e, hf=hf, b=b: [
            e.transpose(psA[:, (2 * b) * 512 + c * P:(2 * b) * 512 + (c + 1) * P],
                        hf[:, c * P:(c + 1) * P], cx.identf[:])
            for c in range(C)][-1],
            r=[hft, ctk], w=[pat[2 * b], pat[2 * b + 1]], tag=f"s5a_tr{i}")
        for hh in range(2):
            em("scalar", lambda e, i=i, gb=gb, b=b, hh=hh: e.copy(
                bass.AP(cx.AR, A5_FM + gb * 8192 + (4 * hh) * 1024 + (i % 4) * P,
                        [[PITCH, P], [1024, 4], [1, P]]),
                psA[:, (2 * b + hh) * 512:(2 * b + hh + 1) * 512]
                .rearrange("p (a b2) -> p a b2", a=4)),
                r=[pat[2 * b + hh]], w=[fm_t[gb], pat[2 * b + hh]],
                tag=f"s5a_ev{i}_{hh}")
        if i % 4 == 3:
            _s5_mlp_group(cx, g, gb, fm_t, hg_t, w1_t, w2_t, b1_t, stok)


def _s5_mlp_group(cx, g, gb, fm_t, hg_t, w1_t, w2_t, b1_t, stok):
    B, em, S, I = cx.B, cx.B.emit, cx.S, cx.I
    TL, BC = cx.TL, cx.BC
    tlt, bct = cx.tk["TL"], cx.tk["BC"]
    psA, psB = cx.psA, cx.psB
    pat, pbt = cx.tk["psA"], cx.tk["psB"]

    def fm_rhs(c):
        return arp(cx, A5_FM + gb * 8192 + c * 1024, [[1, 512]])

    for fg in range(8):
        wb = fg % 2
        em("sync", lambda e, fg=fg, wb=wb: e.dma_start(
            arp(cx, A5_W1 + wb * 4096, [[512, C], [1, 512]]),
            S["w1b"].rearrange("(c p) n -> p c n", p=P)
            [:, :, fg * 512:(fg + 1) * 512]),
            r=list(cx.dt["w1b"]), w=[w1_t[wb]], dma=True,
            tag=f"s5w1_{g}_{fg}")
        for fc4 in range(4):
            fc = fg * 4 + fc4
            pa = fc4 % 4
            em("tensor", lambda e, fc4=fc4, wb=wb, pa=pa: [
                e.matmul(pA(cx, pa),
                         arp(cx, A5_W1 + wb * 4096 + c * 512 + fc4 * P,
                             [[1, P]]),
                         fm_rhs(c), start=(c == 0), stop=(c == C - 1))
                for c in range(C)][-1],
                r=[w1_t[wb], fm_t[gb]], w=[pat[pa]], tag=f"s5m1_{g}_{fc}")
            em("scalar", lambda e, fc=fc, pa=pa: e.activation(
                arp(cx, A5_HG + fc * 512, [[1, 512]]), pA(cx, pa), AF.Gelu,
                bias=cx.clockc[:, 32 + fc:33 + fc], scale=1.0),
                r=[pat[pa], b1_t], w=[hg_t, pat[pa]], tag=f"s5ge_{g}_{fc}")
    for tsp in range(2):
        for fg in range(8):
            wb = fg % 2
            em("sync", lambda e, fg=fg, wb=wb: e.dma_start(
                arp(cx, A5_W2 + wb * 4096, [[1024, 4], [1, 1024]]),
                S["w2b"].rearrange("(a p) n -> p a n", p=P)
                [:, fg * 4:(fg + 1) * 4, :]),
                r=[cx.dt["w2b"][fg]], w=[w2_t[wb]], dma=True,
                tag=f"s5w2_{g}_{tsp}_{fg}")
            for fc4 in range(4):
                fc = fg * 4 + fc4
                em("tensor", lambda e, tsp=tsp, fc=fc, fc4=fc4, wb=wb: [
                    e.matmul(pA(cx, 2 * tsl + nb),
                             arp(cx, A5_HG + fc * 512 + (tsp * 2 + tsl) * P,
                                 [[1, P]]),
                             arp(cx, A5_W2 + wb * 4096 + fc4 * 1024 + nb * 512,
                                 [[1, 512]]),
                             start=(fc == 0), stop=(fc == 31))
                    for tsl in range(2) for nb in range(2)][-1],
                    r=[w2_t[wb], hg_t],
                    w=[pat[0], pat[1], pat[2], pat[3]],
                    tag=f"s5m2_{g}_{tsp}_{fc}")
        for tsl in range(2):
            ts = tsp * 2 + tsl
            i = g * 4 + ts
            bx = tsl % 2
            em("sync", lambda e, i=i, bx=bx: e.dma_start(
                TL[4 + bx][:], S["x2i"][i * P:(i + 1) * P, :]),
                r=[cx.dt["x2i"][i]], w=[tlt[4 + bx]], dma=True,
                tag=f"s5f_li{g}_{ts}")
            em("sync", lambda e, i=i, bx=bx: e.dma_start(
                TL[6 + bx][:], S["x2"][i * P:(i + 1) * P, :]),
                r=[cx.dt["x2"][i]], w=[tlt[6 + bx]], dma=True,
                tag=f"s5f_lx{g}_{ts}")
            em("vector", lambda e, tsl=tsl, bx=bx: [
                e.tensor_add(
                    TL[4 + bx][:, nb * 512:(nb + 1) * 512],
                    TL[4 + bx][:, nb * 512:(nb + 1) * 512],
                    pA(cx, 2 * tsl + nb))
                for nb in range(2)][-1],
                r=[pat[2 * tsl], pat[2 * tsl + 1], tlt[4 + bx]],
                w=[tlt[4 + bx], pat[2 * tsl], pat[2 * tsl + 1]],
                tag=f"s5f_b{g}_{ts}")
            em("vector", lambda e, bx=bx: e.tensor_add(
                TL[4 + bx][:], TL[4 + bx][:], BC[0][:]),
                r=[tlt[4 + bx], bct[0]], w=[tlt[4 + bx]],
                tag=f"s5f_b2{g}_{ts}")
            em("vector", lambda e, bx=bx: e.tensor_add(
                TL[4 + bx][:], TL[4 + bx][:], TL[6 + bx][:]),
                r=[tlt[4 + bx], tlt[6 + bx]], w=[tlt[4 + bx]],
                tag=f"s5f_a{g}_{ts}")
            em("gpsimd", lambda e, i=i, bx=bx: e.dma_start(
                S["x3"][i * P:(i + 1) * P, :], TL[4 + bx][:]),
                r=[tlt[4 + bx]], w=[cx.dt["x3"][i]], dma=True,
                tag=f"s5f_st{g}_{ts}")



def finalize_output(cx, src_name):
    B, em, S = cx.B, cx.B.emit, cx.S
    TL, tlt = cx.TL, cx.tk["TL"]
    for i in range(cx.NT):
        b = i % 2
        em("sync", lambda e, i=i, b=b: e.dma_start(
            TL[4 + b][:], S[src_name][i * P:(i + 1) * P, :]),
            r=[cx.dt[src_name][i]], w=[tlt[4 + b]], dma=True,
            tag=f"fin_ld{i}")
        em("gpsimd", lambda e, i=i, b=b: e.dma_start(
            cx.y[i * P:(i + 1) * P, :], TL[4 + b][:]),
            r=[tlt[4 + b]], w=[cx.dt["y"][i]], dma=True, tag=f"fin_st{i}")


def build_full(T=4096, upto=4):
    nc, cx = build_program(T)
    phase_s1(cx)
    last = "x1"
    if upto >= 2:
        transpose_ffn_weights(cx)
        phase_ffn(cx, "x1", "x2", 1)
        last = "x2"
    if upto == 25:
        phase_s3_p01(cx)
    if upto >= 3 and upto != 25:
        phase_s3_p01(cx)
        phase_s3_p2(cx)
        phase_s3_p3(cx)
        phase_s3_p4(cx)
        phase_s3_p5(cx)
        last = "x3"
    if upto >= 4 and upto != 25:
        phase_ffn(cx, "x3", "y", 2)
        last = None
    if last is not None:
        finalize_output(cx, last)
    stats = cx.B.finalize(nc)
    for h in reversed(cx._mgr):
        h.__exit__(None, None, None)
    return nc, stats




def _np_inputs_for_core(inputs, b, T=4096):
    rope_half = 512
    freqs = np.exp(-np.arange(rope_half, dtype=np.float32)
                   * (np.log(10000.0) / rope_half)).astype(np.float32)
    th = (np.arange(T, dtype=np.float32)[:, None] * freqs[None, :]).astype(np.float32)
    rc = np.cos(th).astype(np.float32)
    rs = np.sin(th).astype(np.float32)
    d = {
        "x": np.ascontiguousarray(inputs["L0_attn_a_tokens"][b]),
        "aWq": inputs["L0_attn_a_Wq"], "aWkv": inputs["L0_attn_a_Wkv"],
        "aWout": inputs["L0_attn_a_Wout"], "aWg": inputs["L0_attn_a_Wg"],
        "a_normw": inputs["L0_attn_a_norm_w"],
        "fWr": inputs["L0_ffn_W_r"], "fWk": inputs["L0_ffn_W_k"],
        "fWv": inputs["L0_ffn_W_v"],
        "fmur": inputs["L0_ffn_mu_r"], "fmuk": inputs["L0_ffn_mu_k"],
        "bWq": inputs["L0_attn_b_Wq"], "bWk": inputs["L0_attn_b_Wk"],
        "bWv": inputs["L0_attn_b_Wv"], "bWg": inputs["L0_attn_b_gate_W"],
        "bWp": inputs["L0_attn_b_p_W"], "bWc": inputs["L0_attn_b_clock_W"],
        "bWo": inputs["L0_attn_b_out_W"],
        "b_gb": inputs["L0_attn_b_gate_b"], "b_pb": inputs["L0_attn_b_p_b"],
        "b_cb": inputs["L0_attn_b_clock_b"], "b_ob": inputs["L0_attn_b_out_b"],
        "b_ln1g": inputs["L0_attn_b_ln1_g"], "b_ln1b": inputs["L0_attn_b_ln1_b"],
        "b_ln2g": inputs["L0_attn_b_ln2_g"], "b_ln2b": inputs["L0_attn_b_ln2_b"],
        "bW1": inputs["L0_attn_b_mlp_W1"], "bB1": inputs["L0_attn_b_mlp_b1"],
        "bW2": inputs["L0_attn_b_mlp_W2"], "bB2": inputs["L0_attn_b_mlp_b2"],
        "rc": rc, "rs": rs,
    }
    return {k: np.ascontiguousarray(np.asarray(v, dtype=np.float32))
            for k, v in d.items()}


def kernel(**inputs):
    T = 4096
    nc, _ = build_full(T)
    in_maps = [_np_inputs_for_core(inputs, b, T) for b in range(NCORES)]
    res = run_bass_kernel_spmd(nc, in_maps, core_ids=list(range(NCORES)))
    out = np.stack([res.results[b]["y"] for b in range(NCORES)], axis=0)
    return out.astype(np.float32)


if __name__ == "__main__":
    nc, stats = build_full(512)
    print(stats)
